# version 18
# baseline (speedup 1.0000x reference)
import math

import numpy as np

import concourse.bass as bass
import concourse.mybir as mybir
from concourse.bass_utils import run_bass_kernel_spmd

F32 = mybir.dt.float32
BF16 = mybir.dt.bfloat16
AF = mybir.ActivationFunctionType
ALU = mybir.AluOpType
AX = mybir.AxisListType

B, S, E = 4, 2048, 2048
H, LAT, D, R = 16, 512, 128, 64
NEXP, II = 16, 2048
TOK = 1024
SF = 2048
EPS = 1e-5
ALPHA = 1.702
LIMIT = 7.0
SCALE = 1.0 / math.sqrt(D)

SBUF_BASE = 16512
SBUF_TOP = 229344
KB = 1024

SEG_A = SBUF_BASE
SEG_B1 = SEG_A + 4 * KB
SEG_B2 = SEG_B1 + 64 * KB
SEG_C = SEG_B2 + 64 * KB
SEG_D = SEG_C + 16 * KB
SEG_E = SEG_D + 32 * KB


class Builder:

    ENGS = ("pe", "dve", "act", "pool", "sp")

    def __init__(self, nc):
        self.nc = nc
        self.stream = {e: [] for e in self.ENGS}
        self.sems = {}
        self.counts = {}
        self._sem_guards = []
        for name in self.ENGS:
            self._new_sem(name)
        self.final_cells = {}
        self.res = {}
        self.waited = {e: {} for e in self.ENGS}
        self.pe_pending = []

    def _new_sem(self, name):
        cm = self.nc.semaphore(name + "_sem")
        self.sems[name] = cm.__enter__()
        self._sem_guards.append(cm)
        self.counts[name] = 0

    def _r(self, name):
        return self.res.setdefault(name, {"w": [], "r": []})

    def _emit_wait(self, eng, ev):
        sem, val = ev
        wm = self.waited[eng]
        s = self.sems[sem]
        if isinstance(val, list):
            if wm.get(sem) == "final":
                return
            wm[sem] = "final"
            self.stream[eng].append(lambda e, s=s, val=val: e.wait_ge(s, val[0]))
            return
        cur = wm.get(sem, 0)
        if cur == "final" or (cur != "final" and cur >= val):
            return
        wm[sem] = val
        self.stream[eng].append(lambda e, s=s, val=val: e.wait_ge(s, val))

    def _deps(self, eng, reads, writes):
        best = {}
        cells = []
        for rn in reads:
            r = self._r(rn)
            for ev in r["w"]:
                (cells.append(ev) if isinstance(ev[1], list) else
                 best.__setitem__(ev[0], max(best.get(ev[0], 0), ev[1])))
        for wn in writes:
            r = self._r(wn)
            for ev in r["w"] + r["r"]:
                (cells.append(ev) if isinstance(ev[1], list) else
                 best.__setitem__(ev[0], max(best.get(ev[0], 0), ev[1])))
        for ev in cells:
            self._emit_wait(eng, ev)
        for sem, val in best.items():
            self._emit_wait(eng, (sem, val))

    def _record(self, reads, writes, ev):
        for rn in reads:
            self._r(rn)["r"].append(ev)
        for wn in writes:
            r = self._r(wn)
            r["w"] = [ev]
            r["r"] = []

    def op(self, eng, emit, reads=(), writes=(), inc=True):
        self._deps(eng, reads, writes)
        if eng == "pe" and not inc:
            self.pe_pending.append((tuple(reads), tuple(writes)))
            self.stream[eng].append(emit)
            return
        self.counts[eng] += 1
        val = self.counts[eng]
        s = self.sems[eng]

        def wrapped(e, emit=emit, s=s):
            emit(e).then_inc(s, 1)
        self.stream[eng].append(wrapped)
        ev = (eng, val)
        self._record(reads, writes, ev)
        if eng == "pe":
            for rds, wrs in self.pe_pending:
                self._record(rds, wrs, ev)
            self.pe_pending = []

    def dma(self, eng, emit, reads=(), writes=(), sem=None, final=False):
        assert eng in ("sp", "pool") and sem is not None
        if sem not in self.sems:
            self._new_sem(sem)
            if final:
                self.final_cells[sem] = [None]
        self._deps(eng, reads, writes)
        self.counts[sem] += 16
        val = self.counts[sem]
        s = self.sems[sem]

        def wrapped(e, emit=emit, s=s):
            emit(e).then_inc(s, 16)
        self.stream[eng].append(wrapped)
        ev = (sem, self.final_cells[sem]) if final else (sem, val)
        if final:
            assert sem in self.final_cells
        self._record(reads, writes, ev)

    def record_extra_write(self, name, ev):
        self._r(name)["w"].append(ev)

    def barrier(self, eng, names):
        for rn in names:
            r = self._r(rn)
            for ev in r["w"]:
                self._emit_wait(eng, ev)
            for ev in r["r"]:
                self._emit_wait(eng, ev)

    def retire(self, old_names, new_name):
        rn = self._r(new_name)
        for old in old_names:
            ro = self._r(old)
            rn["r"].extend(ro["w"])
            rn["r"].extend(ro["r"])

    def finalize(self, blk):
        for sem, cell in self.final_cells.items():
            cell[0] = self.counts[sem]
        fns = {"pe": blk.tensor, "dve": blk.vector, "act": blk.scalar,
               "pool": blk.gpsimd, "sp": blk.sync}
        self.id_base = {}
        for eng in self.ENGS:
            ops = self.stream[eng]
            if not ops:
                continue
            self.id_base[eng] = self.nc.next_id()

            def body(e, ops=ops):
                for f in ops:
                    f(e)
            fns[eng](body)
        for sem, c in self.counts.items():
            assert c < 60000, (sem, c)



def _slab_pack(wT, nslab, slabw):
    Kin, Mout = wT.shape
    kc = Kin // 128
    assert Mout == nslab * slabw
    return np.ascontiguousarray(wT.reshape(kc, 128, nslab, slabw).transpose(2, 1, 0, 3))


def host_prep(inp):
    f32 = np.float32
    T = lambda a: np.ascontiguousarray(np.asarray(a, dtype=f32).T)
    out = {}
    out["wkv"] = _slab_pack(T(inp["L0_attn_Wkv_d"]), 1, 512)
    out["wqd"] = _slab_pack(T(inp["L0_attn_Wq_d"]), 1, 512)
    out["wku"] = _slab_pack(T(inp["L0_attn_Wk_u"]), 16, 64)
    out["wqu"] = _slab_pack(T(inp["L0_attn_Wq_u"]), 16, 64)
    out["wrk"] = _slab_pack(T(inp["L0_attn_Wrk"]), 16, 64)
    out["wrq"] = _slab_pack(T(inp["L0_attn_Wrq"]), 16, 64)
    out["wvu"] = _slab_pack(T(inp["L0_attn_Wv_u"]), 1, 2048)
    wo = T(inp["L0_attn_Wo"])
    out["wo"] = np.ascontiguousarray(wo.reshape(16, 128, 4, 512).transpose(1, 0, 2, 3))

    A = lambda k: np.asarray(inp[k], dtype=f32)
    out["bkv"] = np.ascontiguousarray(A("L0_attn_bkv_d").reshape(4, 128).T)
    out["bqd"] = np.ascontiguousarray(A("L0_attn_bq_d").reshape(4, 128).T)
    out["bku"] = np.ascontiguousarray(np.tile(A("L0_attn_bk_u").reshape(16, 64).T, (2, 1)))
    out["bqu"] = np.ascontiguousarray(A("L0_attn_bq_u").reshape(16, 64).T)
    out["brk"] = np.ascontiguousarray(np.tile(A("L0_attn_brk").reshape(16, 64).T, (2, 1)))
    brq = np.zeros((128, 16), f32)
    brq[64:] = A("L0_attn_brq").reshape(16, 64).T
    out["brq"] = brq
    out["bvu"] = A("L0_attn_bv_u").reshape(1, 2048)
    out["bo"] = A("L0_attn_bo").reshape(1, 2048)
    out["normw"] = A("L0_ffn_norm_w").reshape(1, 2048)
    out["gateb"] = A("L0_ffn_gate_b").reshape(1, 16)
    out["gatew"] = _slab_pack(T(inp["L0_ffn_gate_w"]), 1, 16)

    w1 = np.asarray(inp["L0_ffn_w1"], dtype=f32)
    perm = np.concatenate([np.arange(0, 2 * II, 2), np.arange(1, 2 * II, 2)])
    w1p = w1[:, perm, :]
    w1s = np.empty((16, 16, 128, 16, 256), f32)
    for e in range(16):
        w1s[e] = _slab_pack(np.ascontiguousarray(w1p[e].T), 16, 256)
    out["w1s"] = w1s
    out["b1t"] = np.ascontiguousarray(
        np.asarray(inp["L0_ffn_b1"], dtype=f32)[:, perm].reshape(16, 32, 128).transpose(0, 2, 1))
    w2 = np.asarray(inp["L0_ffn_w2"], dtype=f32)
    w2s = np.empty((16, 8, 128, 16, 256), f32)
    for e in range(16):
        w2s[e] = _slab_pack(np.ascontiguousarray(w2[e].T), 8, 256)
    out["w2s"] = w2s
    out["b2"] = np.asarray(inp["L0_ffn_b2"], dtype=f32)

    out["ident"] = np.eye(128, dtype=f32)
    out["ones"] = np.ones((1, 128), f32)

    inv_freq = 1.0 / (10000.0 ** (np.arange(0, R, 2, dtype=np.float64) / R))
    t64 = np.arange(S, dtype=np.float64)
    emb = np.concatenate([np.outer(t64, inv_freq)] * 2, axis=1)
    cos = np.cos(emb).astype(f32)
    sin_eff = np.sin(emb).astype(f32)
    sin_eff[:, :32] *= -1.0

    x = np.asarray(inp["L0_attn_x"], dtype=f32)
    per_core = []
    for c in range(8):
        b, hf = c // 2, c % 2
        own = slice(hf * 1024, (hf + 1) * 1024)
        oth = slice((1 - hf) * 1024, (2 - hf) * 1024)
        xk = np.concatenate([x[b, own], x[b, oth]], axis=0)
        order = np.concatenate([np.arange(hf * 1024, (hf + 1) * 1024),
                                np.arange((1 - hf) * 1024, (2 - hf) * 1024)])
        d = dict(out)
        d["xk"] = np.ascontiguousarray(xk)
        import ml_dtypes as _mld
        d["cosk"] = np.ascontiguousarray(np.tile(cos[order].T, (2, 1))).astype(_mld.bfloat16)
        d["sink"] = np.ascontiguousarray(np.tile(sin_eff[order].T, (2, 1))).astype(_mld.bfloat16)
        per_core.append(d)
    return per_core



def build_program():
    nc = bass.Bass("TRN2", target_bir_lowering=False, debug=False)
    gi = lambda name, shape: nc.dram_tensor(name, list(shape), F32, kind="ExternalInput").ap()
    d = {}
    d["xk"] = gi("xk", (SF, E))
    d["cosk"] = nc.dram_tensor("cosk", [128, SF], BF16, kind="ExternalInput").ap()
    d["sink"] = nc.dram_tensor("sink", [128, SF], BF16, kind="ExternalInput").ap()
    for n, shp in [
                   ("wkv", (1, 128, 16, 512)), ("wqd", (1, 128, 16, 512)),
                   ("wku", (16, 128, 4, 64)), ("wqu", (16, 128, 4, 64)),
                   ("wrk", (16, 128, 16, 64)), ("wrq", (16, 128, 4, 64)),
                   ("wvu", (1, 128, 4, 2048)), ("wo", (128, 16, 4, 512)),
                   ("bkv", (128, 4)), ("bqd", (128, 4)), ("bku", (128, 16)),
                   ("bqu", (64, 16)), ("brk", (128, 16)), ("brq", (128, 16)),
                   ("bvu", (1, 2048)), ("bo", (1, 2048)), ("normw", (1, 2048)),
                   ("gateb", (1, 16)), ("gatew", (1, 128, 16, 16)),
                   ("b1t", (16, 128, 32)), ("b2", (16, 2048)),
                   ("ident", (128, 128)), ("ones", (1, 128)),
                   ("w1s", (16, 16, 128, 16, 256)), ("w2s", (16, 8, 128, 16, 256))]:
        d[n] = gi(n, shp)

    out = nc.dram_tensor("out", [TOK, E], F32, kind="ExternalOutput").ap()
    x1D = nc.dram_tensor("x1_scr", [TOK, E], F32, kind="Internal").ap()
    xn1D = nc.dram_tensor("xn1_scr", [TOK, E], F32, kind="Internal").ap()
    attnD = nc.dram_tensor("attn_scr", [2, 128, 16, 512], BF16, kind="Internal").ap()
    qTD = nc.dram_tensor("qt_scr", [16, 128, TOK], BF16, kind="Internal").ap()
    qTDs = nc.dram_tensor("qts_scr", [16, 128, TOK], BF16, kind="Internal").ap()

    bld = Builder(nc)
    build_program.bld = bld
    build_program.phases = phases = []
    def mark(tag):
        phases.append((tag, {e: len(bld.stream[e]) for e in bld.ENGS}))
    psum = nc.alloc_psum_tensor([128, 4096], F32)
    bank = lambda i: psum[:, i * 512:(i + 1) * 512]
    bank_bf = lambda i: psum[:, i * 512:(i + 1) * 512].bitcast(BF16)

    nalloc = [0]

    def sb(name, shape, dtype, offset):
        nbytes = int(np.prod(shape[1:])) * mybir.dt.size(dtype)
        assert SBUF_BASE <= offset and offset + nbytes <= SBUF_TOP, (name, offset, nbytes)
        nalloc[0] += 1
        return nc.alloc_sbuf_tensor_at(f"{name}", shape=shape, dtype=dtype, offset=offset)

    ident = sb("identb", [128, 128], BF16, SEG_A)
    ident32 = sb("ident32", [128, 128], F32, SEG_A + 256)
    ones32 = sb("ones32", [1, 128], F32, SEG_A + 768)
    gatew_sb = sb("gatew", [128, 16 * 16], F32, SEG_A + 1280)
    gateb_b = sb("gatebb", [128, 16], F32, SEG_A + 2304)
    stat = sb("stat", [128, 384], F32, SEG_A + 2432)
    epsb = sb("epsb", [128, 1], F32, SEG_A + 4032)


    xnT = sb("xnT", [128, 16 * SF], BF16, SEG_B1)
    xw = [sb(f"xw{i}", [128, E], F32, SEG_B2 + i * 8 * KB) for i in range(2)]
    xnb = [sb(f"xnb{i}", [128, E], BF16, SEG_B2 + 16 * KB + i * 4 * KB) for i in range(2)]

    cosk = sb("coskb", [128, SF], BF16, SEG_E)
    sink = sb("sinkb", [128, SF], BF16, SEG_E + 4 * KB)
    krs = [sb(f"krs{i}", [128, 512], BF16, SEG_D + 24 * KB + i * KB) for i in range(6)]
    rotb = [sb(f"rotb{i}", [128, 512], BF16, SEG_E + 8 * KB + i * KB) for i in range(6)]
    bkv_sb = sb("bkvs", [128, 4], F32, SEG_E + 20 * KB)
    bqd_sb = sb("bqds", [128, 4], F32, SEG_E + 20 * KB + 32)
    bku_sb = sb("bkus", [128, 16], F32, SEG_E + 20 * KB + 64)
    bqu_sb = sb("bqus", [64, 16], F32, SEG_E + 20 * KB + 128)
    brk_sb = sb("brks", [128, 16], F32, SEG_E + 20 * KB + 192)
    brq_sb = sb("brqs", [128, 16], F32, SEG_E + 20 * KB + 256)
    wkuh = [sb(f"wkuh{i}", [128, 4 * 64], BF16, SEG_E + 20 * KB + 320 + i * 512) for i in range(2)]
    wquh = [sb(f"wquh{i}", [128, 4 * 64], BF16, SEG_E + 20 * KB + 1344 + i * 512) for i in range(2)]
    wrqh = [sb(f"wrqh{i}", [128, 4 * 64], BF16, SEG_E + 20 * KB + 2368 + i * 512) for i in range(2)]
    wrkh = [sb(f"wrkh{i}", [128, 16 * 64], BF16, SEG_E + 20 * KB + 3392 + i * 2 * KB) for i in range(2)]

    kv_dT = sb("kv_dT", [128, 4 * SF], BF16, SEG_C)
    wkvc = [sb(f"wkvc{i}", [128, 16 * 128], BF16, SEG_D + i * 4 * KB) for i in range(2)]
    wqdc = [sb(f"wqdc{i}", [128, 16 * 128], BF16, SEG_D + 8 * KB + i * 4 * KB) for i in range(2)]
    q_dT = sb("q_dT", [128, 4 * TOK], BF16, SEG_D + 16 * KB)

    def ldsp(dst, src, res, sem=None, final=False):
        bld.dma("sp", lambda e, dst=dst, src=src: e.dma_start(out=dst, in_=src),
                writes=(res,), sem=sem or res, final=final)

    def ldpool(dst, src, res, sem=None, final=False):
        bld.dma("pool", lambda e, dst=dst, src=src: e.dma_start(out=dst, in_=src),
                writes=(res,), sem=sem or res, final=final)

    ldpool(ident[:, :], d["ident"][:, :], "identb")
    ldsp(ident32[:, :], d["ident"][:, :], "ident32", sem="csp", final=True)
    ldsp(ones32[:, :], d["ones"][:, :], "ones32", sem="csp", final=True)
    ldsp(gatew_sb[:, :], d["gatew"][0].rearrange("p a b -> p (a b)"), "gatew", sem="csp", final=True)
    ldsp(gateb_b[:, :], d["gateb"][0:1, :].to_broadcast([128, 16]), "gatebb", sem="csp", final=True)
    ldsp(cosk[:, :], d["cosk"][:, :], "cosk", sem="csp", final=True)
    ldsp(sink[:, :], d["sink"][:, :], "sink", sem="csp", final=True)
    for nm, dst in [("bkv", bkv_sb), ("bqd", bqd_sb), ("bku", bku_sb),
                    ("bqu", bqu_sb), ("brk", brk_sb), ("brq", brq_sb)]:
        ldsp(dst[:, :], d[nm][:, :], nm + "s", sem="csp", final=True)
    ldpool(wkvc[0][:, :].rearrange("p (a b) -> p a b", a=16), d["wkv"][0, :, :, 0:128], "wkvc0")
    ldpool(wqdc[0][:, :].rearrange("p (a b) -> p a b", a=16), d["wqd"][0, :, :, 0:128], "wqdc0")

    bld.op("dve", lambda e: e.memset(epsb[:, :], EPS), writes=("epsb",))

    mark("P1")
    def p1_chain(kt):
        p = kt % 2
        xb = xw[p]
        xr = f"xw{p}"
        bld.dma("sp", lambda e, xb=xb, kt=kt: e.dma_start(out=xb[:, :], in_=d["xk"][kt * 128:(kt + 1) * 128, :]),
                writes=(xr,), sem=xr)
        sN = 26 + 2 * p
        for bc in range(4):
            bld.op("dve", lambda e, xb=xb, bc=bc: e.bn_stats(
                out=stat[:, 2 + 6 * bc:8 + 6 * bc], in_=xb[:, bc * 512:(bc + 1) * 512]),
                reads=(xr,), writes=("bn6",))
        bld.op("dve", lambda e: e.bn_aggr(out=stat[:, 0:2], in_=stat[:, 2:26]),
               reads=("bn6",), writes=("bn2",))
        bld.op("dve", lambda e, sN=sN: e.tensor_scalar_mul(stat[:, sN:sN + 1], stat[:, 0:1], -1.0),
               reads=("bn2",), writes=(f"nm{p}",))
        bld.op("act", lambda e, sN=sN: e.activation(out=stat[:, sN + 1:sN + 2], in_=stat[:, 1:2],
                                                    func=AF.Ln, bias=epsb[:, 0:1]),
               reads=("bn2", "epsb"), writes=(f"rs{p}",))
        bld.op("act", lambda e, sN=sN: e.activation(out=stat[:, sN + 1:sN + 2], in_=stat[:, sN + 1:sN + 2],
                                                    func=AF.Exp, scale=-0.5),
               reads=(f"rs{p}",), writes=(f"rs{p}",))
        bld.op("dve", lambda e, xb=xb, p=p, sN=sN: e.tensor_scalar(
            out=xnb[p][:, :], in0=xb[:, :], scalar1=stat[:, sN:sN + 1],
            scalar2=stat[:, sN + 1:sN + 2], op0=ALU.add, op1=ALU.mult),
            reads=(xr, f"nm{p}", f"rs{p}"), writes=(f"xnb{p}",))

    def p1_trans(kt):
        p = kt % 2
        for g in range(4):
            bk = 2 * p + g % 2
            for j in range(4):
                ec = g * 4 + j
                bld.op("pe", lambda e, p=p, ec=ec, bk=bk, j=j: e.transpose(
                    bank_bf(bk)[:, j * 128:(j + 1) * 128],
                    xnb[p][:, ec * 128:(ec + 1) * 128], ident[:, :]),
                    reads=(f"xnb{p}", "identb"), writes=(f"ps{bk}",), inc=(j == 3))
            bld.op("act", lambda e, g=g, bk=bk, kt=kt: e.copy(
                out=xnT[:, :].rearrange("p (c t) -> p c t", c=16)[:, g * 4:g * 4 + 4, kt * 128:(kt + 1) * 128],
                in_=bank_bf(bk)[:, 0:512]),
                reads=(f"ps{bk}",), writes=("xnT",))

    for kt in range(16):
        p1_chain(kt)
        if kt > 0:
            p1_trans(kt - 1)
    p1_trans(15)

    mark("P2")
    ldpool(wkvc[1][:, :].rearrange("p (a b) -> p a b", a=16), d["wkv"][0, :, :, 128:256], "wkvc1")
    ldpool(wqdc[1][:, :].rearrange("p (a b) -> p a b", a=16), d["wqd"][0, :, :, 128:256], "wqdc1")
    for mc in range(4):
        sl = mc % 2
        for tc in range(4):
            bk = tc % 2
            for kc in range(16):
                bld.op("pe", lambda e, sl=sl, kc=kc, tc=tc, bk=bk: e.matmul(
                    bank(bk), wkvc[sl][:, kc * 128:(kc + 1) * 128],
                    xnT[:, kc * SF + tc * 512: kc * SF + (tc + 1) * 512],
                    start=(kc == 0), stop=(kc == 15)),
                    reads=(f"wkvc{sl}", "xnT"), writes=(f"ps{bk}",), inc=(kc == 15))
            bld.op("dve", lambda e, mc=mc, tc=tc, bk=bk: e.tensor_scalar(
                out=kv_dT[:, mc * SF + tc * 512: mc * SF + (tc + 1) * 512],
                in0=bank(bk), scalar1=bkv_sb[:, mc:mc + 1], scalar2=None, op0=ALU.add),
                reads=(f"ps{bk}", "bkvs"), writes=("kv_dT",))
        for tc in range(2):
            bk = 2 + tc % 2
            for kc in range(16):
                bld.op("pe", lambda e, sl=sl, kc=kc, tc=tc, bk=bk: e.matmul(
                    bank(bk), wqdc[sl][:, kc * 128:(kc + 1) * 128],
                    xnT[:, kc * SF + tc * 512: kc * SF + (tc + 1) * 512],
                    start=(kc == 0), stop=(kc == 15)),
                    reads=(f"wqdc{sl}", "xnT"), writes=(f"ps{bk}",), inc=(kc == 15))
            bld.op("dve", lambda e, mc=mc, tc=tc, bk=bk: e.tensor_scalar(
                out=q_dT[:, mc * TOK + tc * 512: mc * TOK + (tc + 1) * 512],
                in0=bank(bk), scalar1=bqd_sb[:, mc:mc + 1], scalar2=None, op0=ALU.add),
                reads=(f"ps{bk}", "bqds"), writes=("q_dT",))
        if mc + 2 < 4:
            ldpool(wkvc[sl][:, :].rearrange("p (a b) -> p a b", a=16),
                   d["wkv"][0, :, :, (mc + 2) * 128:(mc + 3) * 128], f"wkvc{sl}")
            ldpool(wqdc[sl][:, :].rearrange("p (a b) -> p a b", a=16),
                   d["wqd"][0, :, :, (mc + 2) * 128:(mc + 3) * 128], f"wqdc{sl}")

    mark("P3a")
    kT = sb("kT", [128, 16 * SF], BF16, SEG_B2)
    bld.retire(["xw0", "xw1", "xnb0", "xnb1"], "kT")
    ldpool(wrkh[0][:, :], d["wrk"][0].rearrange("p a b -> p (a b)"), "wrkh0")
    ldpool(wkuh[0][:, :], d["wku"][0].rearrange("p a b -> p (a b)"), "wkuh0")
    rope_prev = None

    def rope_a(h, tc, bk, costab, brs, brsname, rp, ro=64):
        kb_, rb_ = krs[rp], rotb[rp]
        kr_, rr_ = f"krs{rp}", f"rotb{rp}"
        bld.op("dve", lambda e, bk=bk, h=h, ro=ro: e.tensor_scalar(
            out=kb_[ro:ro + 64, :], in0=bank(bk)[ro:ro + 64, :],
            scalar1=brs[ro:ro + 64, h:h + 1], scalar2=None, op0=ALU.add),
            reads=(f"ps{bk}", brsname), writes=(kr_,))
        bld.dma("sp", lambda e, kb_=kb_, rb_=rb_, ro=ro: e.dma_start(
            out=rb_[ro + 32:ro + 64, :], in_=kb_[ro:ro + 32, :]),
            reads=(kr_,), writes=(rr_,), sem="rot" + str(rp))
        bld.dma("sp", lambda e, kb_=kb_, rb_=rb_, ro=ro: e.dma_start(
            out=rb_[ro:ro + 32, :], in_=kb_[ro + 32:ro + 64, :]),
            reads=(kr_,), writes=(rr_,), sem="rot" + str(rp))
        bld.op("dve", lambda e, tc=tc, kb_=kb_, ro=ro: e.tensor_mul(
            kb_[ro:ro + 64, :], kb_[ro:ro + 64, :], costab[ro:ro + 64, tc * 512:(tc + 1) * 512]),
            reads=(kr_, "cosk"), writes=(kr_,))
        bld.op("pool", lambda e, tc=tc, rb_=rb_, ro=ro: e.tensor_mul(
            rb_[ro:ro + 64, :], rb_[ro:ro + 64, :], sink[ro:ro + 64, tc * 512:(tc + 1) * 512]),
            reads=(rr_, "sink"), writes=(rr_,))

    def rope_b(h, tc, tgt, tgtres, T, rp, ro=64):
        kb_, rb_ = krs[rp], rotb[rp]
        kr_, rr_ = f"krs{rp}", f"rotb{rp}"
        if tgt is None:
            bld.op("dve", lambda e, kb_=kb_, rb_=rb_: e.tensor_add(
                kb_[64:128, :], kb_[64:128, :], rb_[64:128, :]),
                reads=(kr_, rr_), writes=(kr_,))
            bld.dma("pool", lambda e, h=h, tc=tc, kb_=kb_: e.dma_start(
                out=qTD[h, :, tc * 512:(tc + 1) * 512], in_=kb_[:, :]),
                reads=(kr_,), writes=("qTDr",), sem="qtd" + str(rp))
            bld.dma("pool", lambda e, h=h, tc=tc, kb_=kb_: e.dma_start(
                out=qTDs[h, 0:64, tc * 512:(tc + 1) * 512], in_=kb_[64:128, :]),
                reads=(kr_,), writes=("qTDsr",), sem="qtds" + str(rp))
            bld.dma("pool", lambda e, h=h, tc=tc, kb_=kb_: e.dma_start(
                out=qTDs[h, 64:128, tc * 512:(tc + 1) * 512], in_=kb_[0:64, :]),
                reads=(kr_,), writes=("qTDsr",), sem="qtds" + str(rp))
            return
        bld.op("dve", lambda e, h=h, tc=tc, tgt=tgt, T=T, kb_=kb_, rb_=rb_, ro=ro: e.tensor_add(
            tgt[ro:ro + 64, h * T + tc * 512: h * T + (tc + 1) * 512],
            kb_[ro:ro + 64, :], rb_[ro:ro + 64, :]),
            reads=(kr_, rr_), writes=(tgtres,))

    ldpool(wquh[0][:, :], d["wqu"][0].rearrange("p a b -> p (a b)"), "wquh0")
    ldpool(wrqh[0][:, :], d["wrq"][0].rearrange("p a b -> p (a b)"), "wrqh0")
    gtile = [0]
    for h in range(16):
        sl = h % 2
        for pair in range(2):
            tcA, tcB = 2 * pair, 2 * pair + 1
            bkA = gtile[0] % 4
            rpA = gtile[0] % 6
            gtile[0] += 1
            bkB = gtile[0] % 4
            rpB = gtile[0] % 6
            gtile[0] += 1
            for kc in range(4):
                bld.op("pe", lambda e, sl=sl, kc=kc, tcA=tcA, bkA=bkA: e.matmul(
                    bank(bkA)[0:64, :], wkuh[sl][:, kc * 64:(kc + 1) * 64],
                    kv_dT[:, kc * SF + tcA * 512: kc * SF + (tcA + 1) * 512],
                    start=(kc == 0), stop=(kc == 3), tile_position=(0, 0)),
                    reads=(f"wkuh{sl}", "kv_dT"), writes=(f"ps{bkA}",), inc=False)
                bld.op("pe", lambda e, sl=sl, kc=kc, tcB=tcB, bkB=bkB: e.matmul(
                    bank(bkB)[64:128, :], wkuh[sl][:, kc * 64:(kc + 1) * 64],
                    kv_dT[:, kc * SF + tcB * 512: kc * SF + (tcB + 1) * 512],
                    start=(kc == 0), stop=(kc == 3), tile_position=(0, 64)),
                    reads=(f"wkuh{sl}", "kv_dT"), writes=(f"ps{bkB}",), inc=False)
            for kc in range(16):
                bld.op("pe", lambda e, sl=sl, kc=kc, tcA=tcA, bkA=bkA: e.matmul(
                    bank(bkA)[64:128, :], wrkh[sl][:, kc * 64:(kc + 1) * 64],
                    xnT[:, kc * SF + tcA * 512: kc * SF + (tcA + 1) * 512],
                    start=(kc == 0), stop=(kc == 15), tile_position=(0, 64)),
                    reads=(f"wrkh{sl}", "xnT"), writes=(f"ps{bkA}",), inc=(kc == 15))
                bld.op("pe", lambda e, sl=sl, kc=kc, tcB=tcB, bkB=bkB: e.matmul(
                    bank(bkB)[0:64, :], wrkh[sl][:, kc * 64:(kc + 1) * 64],
                    xnT[:, kc * SF + tcB * 512: kc * SF + (tcB + 1) * 512],
                    start=(kc == 0), stop=(kc == 15), tile_position=(0, 0)),
                    reads=(f"wrkh{sl}", "xnT"), writes=(f"ps{bkB}",), inc=(kc == 15))
            bld.op("act", lambda e, h=h, tcA=tcA, bkA=bkA: e.activation(
                out=kT[0:64, h * SF + tcA * 512: h * SF + (tcA + 1) * 512],
                in_=bank(bkA)[0:64, :], func=AF.Identity, bias=bku_sb[0:64, h:h + 1]),
                reads=(f"ps{bkA}", "bkus"), writes=("kT",))
            rope_a(h, tcA, bkA, cosk, brk_sb, "brks", rpA, ro=64)
            if rope_prev is not None:
                rope_b(*rope_prev)
            rope_prev = (h, tcA, kT, "kT", SF, rpA, 64)
            bld.op("act", lambda e, h=h, tcB=tcB, bkB=bkB: e.activation(
                out=kT[64:128, h * SF + tcB * 512: h * SF + (tcB + 1) * 512],
                in_=bank(bkB)[64:128, :], func=AF.Identity, bias=bku_sb[64:128, h:h + 1]),
                reads=(f"ps{bkB}", "bkus"), writes=("kT",))
            rope_a(h, tcB, bkB, cosk, brk_sb, "brks", rpB, ro=0)
            if rope_prev is not None:
                rope_b(*rope_prev)
            rope_prev = (h, tcB, kT, "kT", SF, rpB, 0)
        for tc in range(2):
            bk = gtile[0] % 4
            rp = gtile[0] % 6
            gtile[0] += 1
            for kc in range(4):
                bld.op("pe", lambda e, sl=sl, kc=kc, tc=tc, bk=bk: e.matmul(
                    bank(bk)[0:64, :], wquh[sl][:, kc * 64:(kc + 1) * 64],
                    q_dT[:, kc * TOK + tc * 512: kc * TOK + (tc + 1) * 512],
                    start=(kc == 0), stop=(kc == 3), tile_position=(0, 0)),
                    reads=(f"wquh{sl}", "q_dT"), writes=(f"ps{bk}",), inc=False)
                bld.op("pe", lambda e, sl=sl, kc=kc, tc=tc, bk=bk: e.matmul(
                    bank(bk)[64:128, :], wrqh[sl][:, kc * 64:(kc + 1) * 64],
                    q_dT[:, kc * TOK + tc * 512: kc * TOK + (tc + 1) * 512],
                    start=(kc == 0), stop=(kc == 3), tile_position=(0, 64)),
                    reads=(f"wrqh{sl}", "q_dT"), writes=(f"ps{bk}",), inc=(kc == 3))
            bld.op("act", lambda e, h=h, tc=tc, bk=bk, kb_=krs[rp]: e.activation(
                out=kb_[0:64, :],
                in_=bank(bk)[0:64, :], func=AF.Identity, bias=bqu_sb[0:64, h:h + 1]),
                reads=(f"ps{bk}", "bqus"), writes=(f"krs{rp}",))
            rope_a(h, tc, bk, cosk, brq_sb, "brqs", rp, ro=64)
            if rope_prev is not None:
                rope_b(*rope_prev)
            rope_prev = (h, tc, None, None, TOK, rp, 64)
        if h + 1 < 16:
            ldpool(wrkh[1 - sl][:, :], d["wrk"][h + 1].rearrange("p a b -> p (a b)"), f"wrkh{1 - sl}")
            ldpool(wkuh[1 - sl][:, :], d["wku"][h + 1].rearrange("p a b -> p (a b)"), f"wkuh{1 - sl}")
            ldpool(wquh[1 - sl][:, :], d["wqu"][h + 1].rearrange("p a b -> p (a b)"), f"wquh{1 - sl}")
            ldpool(wrqh[1 - sl][:, :], d["wrq"][h + 1].rearrange("p a b -> p (a b)"), f"wrqh{1 - sl}")
    mark("P3b")
    if rope_prev is not None:
        rope_b(*rope_prev)
        rope_prev = None

    mark("P4")
    qTh = [sb(f"qTh{i}", [128, TOK], BF16, SEG_B1 + i * 2 * KB) for i in range(2)]
    qThs = [sb(f"qThs{i}", [128, TOK], BF16, SEG_B1 + 4 * KB + i * 2 * KB) for i in range(2)]
    for nm in ("qTh0", "qTh1", "qThs0", "qThs1"):
        bld.retire(["xnT"], nm)
    wvu_sb = sb("wvu", [128, 4 * 2048], BF16, SEG_B1 + 32 * KB)
    v4 = sb("v4", [128, 16 * 512], BF16, SEG_B1 + 48 * KB)
    bld.retire(["xnT"], "wvu")
    bld.retire(["xnT"], "v4")
    probsT = sb("probsT", [128, 16 * 512], BF16, SEG_D)
    probs = [sb(f"probs{i}", [128, SF], BF16, SEG_D + 16 * KB + i * 4 * KB) for i in range(2)]
    bvu_b = sb("bvub", [128, 2048], F32, SEG_D + 24 * KB)
    for rn in ("wkvc0", "wkvc1", "wqdc0", "wqdc1", "q_dT"):
        bld.retire([rn], "probsT")
        bld.retire([rn], "probs0")
        bld.retire([rn], "probs1")
        bld.retire([rn], "bvub")
    strip = [sb(f"strip{i}", [128, 512], BF16, SEG_E + 16 * KB + i * KB) for i in range(2)]
    diag = [sb(f"diag{i}", [128, 128], BF16, SEG_E + 18 * KB + i * 256) for i in range(2)]
    bld.retire(["rotb0"], "diag0")
    bld.retire(["rotb0"], "diag1")
    for rn in ("krs0", "rotb0"):
        bld.retire([rn], "strip0")
        bld.retire([rn], "strip1")
    for rn in [f"krs{i}" for i in range(6)] + [f"rotb{i}" for i in range(6)]:
        for nn in ("probsT", "probs0", "probs1", "bvub"):
            bld.retire([rn], nn)

    ldpool(wvu_sb[:, :], d["wvu"][0].rearrange("p a b -> p (a b)"), "wvu")
    ldsp(bvu_b[:, :], d["bvu"][0:1, :].to_broadcast([128, 2048]), "bvub")

    def emit_v4(hg):
        for kt in range(16):
            for kc in range(4):
                bld.op("pe", lambda e, kc=kc, kt=kt, hg=hg: e.matmul(
                    bank(7), kv_dT[:, kc * SF + kt * 128: kc * SF + (kt + 1) * 128],
                    wvu_sb[:, kc * 2048 + hg * 512: kc * 2048 + (hg + 1) * 512],
                    start=(kc == 0), stop=(kc == 3)),
                    reads=("kv_dT", "wvu"), writes=("ps7",), inc=(kc == 3))
            bld.op("dve", lambda e, kt=kt, hg=hg: e.scalar_tensor_tensor(
                out=v4[:, kt * 512:(kt + 1) * 512], in0=bank(7), scalar=1.0,
                in1=bvu_b[:, hg * 512:(hg + 1) * 512], op0=ALU.mult, op1=ALU.add),
                reads=("ps7", "bvub"), writes=("v4",))

    def emit_scores_pe(h, qi, par):
        base = 4 * par
        for kb in range(4):
            qsrc = qTh if kb % 2 == 0 else qThs
            qres = f"qTh{h % 2}" if kb % 2 == 0 else f"qThs{h % 2}"
            bld.op("pe", lambda e, h=h, qi=qi, kb=kb, base=base, qsrc=qsrc, qres=qres: e.matmul(
                bank(base + kb), qsrc[h % 2][:, qi * 128:(qi + 1) * 128],
                kT[:, h * SF + kb * 512: h * SF + (kb + 1) * 512],
                start=True, stop=True),
                reads=(qres, "kT"), writes=(f"ps{base + kb}",), inc=True)

    def emit_chain(h, qi, par):
        sc = 32 + 16 * par
        base = 4 * par
        pbanks = tuple(f"ps{base + kb}" for kb in range(4))
        bld.op("dve", lambda e, sc=sc, base=base: e.reduce_max(
            out=stat[:, sc + 5:sc + 6],
            in_=psum[:, base * 512:base * 512 + 2048].rearrange("p (a b) -> p a b", b=8)[:, :, 0],
            axis=AX.X, negate=True),
            reads=pbanks, writes=(f"nmc_{par}",))
        bld.op("dve", lambda e, sc=sc: e.tensor_scalar_mul(
            stat[:, sc + 6:sc + 7], stat[:, sc + 5:sc + 6], SCALE),
            reads=(f"nmc_{par}",), writes=(f"nms_{par}",))
        bld.op("act", lambda e, sc=sc, par=par, base=base: e.activation(
            out=probs[par][:, :], in_=psum[:, base * 512:base * 512 + 2048],
            func=AF.Exp, bias=stat[:, sc + 6:sc + 7], scale=SCALE,
            accum_out=stat[:, sc + 12:sc + 13]),
            reads=pbanks + (f"nms_{par}",),
            writes=(f"probs{par}", f"z_{par}"))
    def emit_chain_b(par):
        sc = 32 + 16 * par
        bld.op("dve", lambda e, sc=sc: e.reciprocal(
            out=stat[:, sc + 13:sc + 14], in_=stat[:, sc + 12:sc + 13]),
            reads=(f"z_{par}",), writes=(f"zi_{par}",))
        bld.op("dve", lambda e, par=par, sc=sc: e.tensor_scalar_mul(
            diag[par][:, :], ident[:, :], stat[:, sc + 13:sc + 14]),
            reads=("identb", f"zi_{par}"), writes=(f"diag{par}",))

    def emit_transposes(h, qg, qt, par):
        base = 4 * par
        for g4 in range(4):
            tb = base + g4
            for j in range(4):
                bld.op("pe", lambda e, par=par, g4=g4, j=j, tb=tb: e.matmul(
                    bank(tb)[:, j * 128:(j + 1) * 128],
                    probs[par][:, (g4 * 4 + j) * 128:(g4 * 4 + j + 1) * 128],
                    diag[par][:, :], start=True, stop=True),
                    reads=(f"probs{par}", f"diag{par}"), writes=(f"ps{tb}",), inc=(j == 3))
            if g4 != 3:
                bld.op("dve", lambda e, g4=g4, qt=qt, tb=tb: e.tensor_copy(
                    out=probsT[:, :].rearrange("p (c q) -> p c q", c=16)
                        [:, g4 * 4:g4 * 4 + 4, qt * 128:(qt + 1) * 128],
                    in_=bank(tb)),
                    reads=(f"ps{tb}",), writes=("probsT",))
            else:
                bld.op("act", lambda e, g4=g4, qt=qt, tb=tb: e.copy(
                    out=probsT[:, :].rearrange("p (c q) -> p c q", c=16)
                        [:, g4 * 4:g4 * 4 + 4, qt * 128:(qt + 1) * 128],
                    in_=bank(tb)),
                    reads=(f"ps{tb}",), writes=("probsT",))

    def emit_pv(h, qg):
        hh = h % 4
        for kc in range(16):
            bld.op("pe", lambda e, kc=kc, hh=hh: e.matmul(
                bank(6), v4[:, kc * 512 + hh * 128: kc * 512 + (hh + 1) * 128],
                probsT[:, kc * 512:(kc + 1) * 512],
                start=(kc == 0), stop=(kc == 15)),
                reads=("v4", "probsT"), writes=("ps6",), inc=(kc == 15))
        ssl = qg % 2
        bld.op("act", lambda e, ssl=ssl: e.copy(out=strip[ssl][:, :], in_=bank(6)),
               reads=("ps6",), writes=(f"strip{ssl}",))
        bld.dma("sp", lambda e, h=h, qg=qg, ssl=ssl: e.dma_start(
            out=attnD[qg, :, h, :], in_=strip[ssl][:, :]),
            reads=(f"strip{ssl}",), writes=(f"attnD{ssl}",), sem=f"attnDs{ssl}")

    def load_qTh(h):
        bld.dma("sp", lambda e, h=h: e.dma_start(out=qTh[h % 2][:, :], in_=qTD[h]),
                reads=("qTDr",), writes=(f"qTh{h % 2}",), sem=f"qTh{h % 2}")
        bld.dma("sp", lambda e, h=h: e.dma_start(out=qThs[h % 2][:, :], in_=qTDs[h]),
                reads=("qTDsr",), writes=(f"qThs{h % 2}",), sem=f"qThs{h % 2}")

    load_qTh(0)
    load_qTh(1)
    work = []
    for hg in range(4):
        work.append(("v4", hg))
        for hh in range(4):
            h = hg * 4 + hh
            for qg in range(2):
                for qt in range(4):
                    work.append(("qt", h, qg, qt, hh == 0 and qg == 0 and qt == 0))
    prev = None
    g_i = 0

    def drain_prev():
        nonlocal prev
        if prev is not None:
            ph, pqg, pqt, ppar = prev
            emit_chain_b(ppar)
            emit_transposes(ph, pqg, pqt, ppar)
            if pqt == 3:
                emit_pv(ph, pqg)
            prev = None

    for w in work:
        if w[0] == "v4":
            drain_prev()
            emit_v4(w[1])
            continue
        _, h, qg, qt, _hd = w
        par = g_i % 2
        g_i += 1
        emit_scores_pe(h, qg * 4 + qt, par)
        if qg == 1 and qt == 3 and h + 2 < 16:
            load_qTh(h + 2)
        emit_chain(h, qg * 4 + qt, par)
        keep = prev
        prev = None
        if keep is not None:
            ph, pqg, pqt, ppar = keep
            emit_chain_b(ppar)
            emit_transposes(ph, pqg, pqt, ppar)
            if pqt == 3:
                emit_pv(ph, pqg)
        prev = (h, qg, qt, par)
    drain_prev()

    mark("P5")
    wo_sb = sb("wo", [128, 16 * 4 * 512], BF16, SEG_B1)
    bld.retire(["qTh0", "qTh1", "qThs0", "qThs1", "wvu", "v4"], "wo")
    for eo in range(4):
        bld.retire(["qTh0", "qTh1", "qThs0", "qThs1", "wvu", "v4"], f"wo{eo}")
    tT = sb("tT", [128, 16 * TOK], BF16, SEG_B2)
    bld.retire(["kT"], "tT")
    bo_b = sb("bob", [128, 2048], F32, SEG_D)
    normw_b = sb("normwb", [128, 2048], F32, SEG_D + 16 * KB)
    for rn in ("probsT", "probs0", "probs1", "bvub") + tuple(f"krs{i}" for i in range(6)):
        bld.retire([rn], "bob")
        bld.retire([rn], "normwb")
    x1b = [sb(f"x1b{i}", [128, E], F32, SEG_B2 + 32 * KB + i * 8 * KB) for i in range(2)]
    af32 = [sb(f"af32{i}", [128, E], F32, SEG_B2 + 48 * KB + i * 8 * KB) for i in range(2)]
    for i in range(2):
        bld.retire(["kT"], f"x1b{i}")
        bld.retire(["kT"], f"af32{i}")
    xn1b = sb("xn1b", [128, E], F32, SEG_D + 8 * KB)
    tw = sb("tw", [128, E], F32, SEG_D + 24 * KB)
    for rn in ("probsT", "probs0", "probs1", "bvub") + tuple(f"krs{i}" for i in range(6)):
        for nn in ("xn1b", "tw"):
            bld.retire([rn], nn)
    attw = sb("attw", [128, 16 * 512], BF16, SEG_C)
    bld.retire(["kv_dT"], "attw")
    t32T = sb("t32T", [128, 16 * 128], F32, SEG_E + 18 * KB)
    GT = sb("GT", [16, TOK], BF16, SEG_E)
    GTp = sb("GTp", [16, TOK], F32, SEG_E + 2 * KB)
    grow = sb("grow", [1, TOK], F32, SEG_E + 6 * KB)
    for rn in ("cosk", "sink") + tuple(f"rotb{i}" for i in range(6)):
        for nn in ("GT", "GTp", "grow"):
            bld.retire([rn], nn)
    for rn in ("wrkh0", "wrkh1", "strip0", "strip1", "rotb0", "krs0",
               "bkvs", "bqds", "bkus", "bqus", "brks", "brqs",
               "wkuh0", "wkuh1", "wquh0", "wquh1", "wrqh0", "wrqh1", "diag0", "diag1"):
        bld.retire([rn], "t32T")

    for eo in range(4):
        bld.dma("pool", lambda e, eo=eo: e.dma_start(
            out=wo_sb[:, :].rearrange("p (h e n) -> p h e n", h=16, e=4)[:, :, eo, :],
            in_=d["wo"][:, :, eo, :]),
            writes=(f"wo{eo}",), sem=f"wo{eo}")
    ldsp(bo_b[:, :], d["bo"][0:1, :].to_broadcast([128, 2048]), "bob")
    ldsp(normw_b[:, :], d["normw"][0:1, :].to_broadcast([128, 2048]), "normwb")

    def p5_load(i):
        if i % 4 == 0:
            bld.dma("sp", lambda e, i=i: e.dma_start(
                out=attw[:, :].rearrange("p (h q) -> p h q", h=16),
                in_=attnD[i // 4]),
                reads=("attnD0", "attnD1"), writes=("attw",), sem="attw")
        bld.dma("sp", lambda e, i=i: e.dma_start(out=x1b[i % 2][:, :],
                                                 in_=d["xk"][i * 128:(i + 1) * 128, :]),
                writes=(f"x1b{i % 2}",), sem=f"x1b{i % 2}")

    def p5_wo_pe(i):
        for eo in range(4):
            for h in range(16):
                bld.op("pe", lambda e, i=i, eo=eo, h=h: e.matmul(
                    bank(eo), attw[:, h * 512 + (i % 4) * 128: h * 512 + (i % 4) * 128 + 128],
                    wo_sb[:, (h * 4 + eo) * 512:(h * 4 + eo + 1) * 512],
                    start=(h == 0), stop=(h == 15)),
                    reads=("attw", f"wo{eo}"), writes=(f"ps{eo}",), inc=(h == 15))

    def p5_wo_evac(i):
        for eo in range(4):
            bld.op("dve", lambda e, i=i, eo=eo: e.scalar_tensor_tensor(
                out=af32[i % 2][:, eo * 512:(eo + 1) * 512], in0=bank(eo), scalar=1.0,
                in1=bo_b[:, eo * 512:(eo + 1) * 512], op0=ALU.mult, op1=ALU.add),
                reads=(f"ps{eo}", "bob"), writes=(f"af32{i % 2}",))

    def ln(src_, srcres, dst, dstres, sN):
        for bc in range(4):
            bld.op("dve", lambda e, src_=src_, bc=bc: e.bn_stats(
                out=stat[:, 2 + 6 * bc:8 + 6 * bc], in_=src_[:, bc * 512:(bc + 1) * 512]),
                reads=(srcres,), writes=("bn6",))
        bld.op("dve", lambda e: e.bn_aggr(out=stat[:, 0:2], in_=stat[:, 2:26]),
               reads=("bn6",), writes=("bn2",))
        bld.op("dve", lambda e, sN=sN: e.tensor_scalar_mul(stat[:, sN:sN + 1], stat[:, 0:1], -1.0),
               reads=("bn2",), writes=("nm0",))
        bld.op("act", lambda e, sN=sN: e.activation(out=stat[:, sN + 1:sN + 2], in_=stat[:, 1:2],
                                                    func=AF.Ln, bias=epsb[:, 0:1]),
               reads=("bn2", "epsb"), writes=("rs0",))
        bld.op("act", lambda e, sN=sN: e.activation(out=stat[:, sN + 1:sN + 2],
                                                    in_=stat[:, sN + 1:sN + 2], func=AF.Exp, scale=-0.5),
               reads=("rs0",), writes=("rs0",))
        bld.op("dve", lambda e, src_=src_, dst=dst, sN=sN: e.tensor_scalar(
            out=dst[:, :], in0=src_[:, :], scalar1=stat[:, sN:sN + 1],
            scalar2=stat[:, sN + 1:sN + 2], op0=ALU.add, op1=ALU.mult),
            reads=(srcres, "nm0", "rs0"), writes=(dstres,))

    def p5_chain(i):
        ip = i % 2
        ln(af32[ip], f"af32{ip}", tw, "tw", 26)
        bld.op("dve", lambda e, ip=ip: e.tensor_add(x1b[ip][:, :], x1b[ip][:, :], tw[:, :]),
               reads=(f"x1b{ip}", "tw"), writes=(f"x1b{ip}",))
        bld.dma("sp", lambda e, i=i, ip=ip: e.dma_start(out=x1D[i * 128:(i + 1) * 128, :], in_=x1b[ip][:, :]),
                reads=(f"x1b{ip}",), writes=("x1D",), sem="x1Ds")
        ln(x1b[ip], f"x1b{ip}", xn1b, "xn1b", 26)
        bld.dma("sp", lambda e, i=i: e.dma_start(out=xn1D[i * 128:(i + 1) * 128, :], in_=xn1b[:, :]),
                reads=("xn1b",), writes=("xn1D",), sem="xn1Ds")
        for bc in range(4):
            bld.op("dve", lambda e, bc=bc: e.bn_stats(
                out=stat[:, 2 + 6 * bc:8 + 6 * bc], in_=xn1b[:, bc * 512:(bc + 1) * 512]),
                reads=("xn1b",), writes=("bn6",))
        bld.op("dve", lambda e: e.bn_aggr(out=stat[:, 0:2], in_=stat[:, 2:26]),
               reads=("bn6",), writes=("bn2",))
        bld.op("dve", lambda e: e.tensor_mul(stat[:, 30:31], stat[:, 0:1], stat[:, 0:1]),
               reads=("bn2",), writes=("msq",))
        bld.op("dve", lambda e: e.tensor_add(stat[:, 30:31], stat[:, 30:31], stat[:, 1:2]),
               reads=("msq", "bn2"), writes=("msq",))
        bld.op("act", lambda e: e.activation(out=stat[:, 31:32], in_=stat[:, 30:31], func=AF.Ln, bias=epsb[:, 0:1]),
               reads=("msq", "epsb"), writes=("rms",))
        bld.op("act", lambda e: e.activation(out=stat[:, 31:32], in_=stat[:, 31:32], func=AF.Exp, scale=-0.5),
               reads=("rms",), writes=("rms",))
        bld.op("dve", lambda e: e.tensor_scalar_mul(tw[:, :], xn1b[:, :], stat[:, 31:32]),
               reads=("xn1b", "rms"), writes=("tw",))
        bld.op("dve", lambda e: e.tensor_mul(tw[:, :], tw[:, :], normw_b[:, :]),
               reads=("tw", "normwb"), writes=("tw",))
        for g in range(4):
            tb = 4 + g % 2
            for j in range(4):
                bld.op("pe", lambda e, g=g, j=j, tb=tb: e.transpose(
                    bank(tb)[:, j * 128:(j + 1) * 128], tw[:, (g * 4 + j) * 128:(g * 4 + j + 1) * 128],
                    ident32[:, :]),
                    reads=("tw", "ident32"), writes=(f"ps{tb}",), inc=(j == 3))
            bld.op("dve", lambda e, g=g, tb=tb: e.tensor_copy(
                out=t32T[:, :].rearrange("p (c t) -> p c t", c=16)[:, g * 4:g * 4 + 4, 0:128],
                in_=bank(tb)),
                reads=(f"ps{tb}",), writes=("t32T",))
        bld.op("act", lambda e, i=i: e.copy(
            out=tT[:, :].rearrange("p (c t) -> p c t", c=16)[:, :, i * 128:(i + 1) * 128],
            in_=t32T[:, :].rearrange("p (c t) -> p c t", c=16)[:, :, :]),
            reads=("t32T",), writes=("tT",))
        for kc in range(16):
            bld.op("pe", lambda e, kc=kc: e.matmul(
                bank(6)[:, 0:16], t32T[:, kc * 128:(kc + 1) * 128],
                gatew_sb[:, kc * 16:(kc + 1) * 16], start=(kc == 0), stop=(kc == 15)),
                reads=("t32T", "gatew"), writes=("ps6",), inc=(kc == 15))
        bld.op("dve", lambda e: e.tensor_add(stat[:, 64:80], bank(6)[:, 0:16], gateb_b[:, :]),
               reads=("ps6", "gatebb"), writes=("lg",))
        bld.op("dve", lambda e: e.max(out=stat[:, 80:88], in_=stat[:, 64:80]),
               reads=("lg",), writes=("v8",))
        bld.op("dve", lambda e: e.reduce_max(out=stat[:, 88:89], in_=stat[:, 64:80], axis=AX.X, negate=True),
               reads=("lg",), writes=("nmg",))
        bld.op("dve", lambda e: e.tensor_scalar(out=stat[:, 96:112], in0=stat[:, 64:80],
                                                scalar1=stat[:, 83:84], scalar2=None, op0=ALU.is_ge),
               reads=("lg", "v8"), writes=("mask",))
        bld.op("act", lambda e: e.activation(out=stat[:, 112:128], in_=stat[:, 64:80],
                                             func=AF.Exp, bias=stat[:, 88:89]),
               reads=("lg", "nmg"), writes=("eg",))
        bld.op("dve", lambda e: e.tensor_mul(stat[:, 112:128], stat[:, 112:128], stat[:, 96:112]),
               reads=("eg", "mask"), writes=("eg",))
        bld.op("dve", lambda e: e.reduce_sum(out=stat[:, 89:90], in_=stat[:, 112:128], axis=AX.X),
               reads=("eg",), writes=("zg",))
        bld.op("dve", lambda e: e.reciprocal(out=stat[:, 90:91], in_=stat[:, 89:90]),
               reads=("zg",), writes=("zgi",))
        bld.op("dve", lambda e: e.tensor_scalar_mul(stat[:, 128:144], stat[:, 112:128], stat[:, 90:91]),
               reads=("eg", "zgi"), writes=("G32",))
        bld.op("dve", lambda e: e.tensor_scalar_mul(stat[:, 144:160], stat[:, 128:144], 1.0 / ALPHA),
               reads=("G32",), writes=("Gp32",))
        bld.op("dve", lambda e: e.tensor_copy(out=stat[:, 160:176].bitcast(BF16)[:, 0:16],
                                              in_=stat[:, 128:144]),
               reads=("G32",), writes=("Gbf",))
        bld.op("pe", lambda e: e.transpose(bank_bf(7)[0:16, 0:128], stat[:, 160:176].bitcast(BF16)[:, 0:16],
                                           ident[:, :]),
               reads=("Gbf", "identb"), writes=("ps7",), inc=True)
        bld.op("act", lambda e, i=i: e.copy(out=GT[0:16, i * 128:(i + 1) * 128],
                                            in_=bank_bf(7)[0:16, 0:128]),
               reads=("ps7",), writes=("GTr",))
        bld.op("pe", lambda e: e.transpose(bank(7)[0:16, 128:256], stat[:, 144:160], ident32[:, :]),
               reads=("Gp32", "ident32"), writes=("ps7",), inc=True)
        bld.op("act", lambda e, i=i: e.copy(out=GTp[0:16, i * 128:(i + 1) * 128],
                                            in_=bank(7)[0:16, 128:256]),
               reads=("ps7",), writes=("GTpr",))

    for i in range(8):
        p5_load(i)
        p5_wo_pe(i)
        if i > 0:
            p5_chain(i - 1)
        p5_wo_evac(i)
    p5_chain(7)

    mark("P6")
    delta = sb("delta", [128, 16 * TOK], F32, SEG_B1)
    bld.retire(["wo"], "delta")
    act_e = sb("act_e", [128, 16 * TOK], BF16, SEG_B2 + 32 * KB)
    bld.retire(["kT"], "act_e")
    for i in range(2):
        bld.retire([f"x1b{i}"], "act_e")
        bld.retire([f"af32{i}"], "act_e")
    w2sl = [sb(f"w2sl{i}", [128, 16 * 256], BF16, SEG_C + i * 8 * KB) for i in range(2)]
    bld.retire(["bob", "normwb"], "w2sl0")
    bld.retire(["bob", "normwb"], "w2sl1")
    w1sl = [sb(f"w1sl{i}", [128, 16 * 256], BF16, SEG_D + i * 8 * KB) for i in range(4)]
    for nn in ("xn1b", "tw"):
        for i in range(4):
            bld.retire([nn], f"w1sl{i}")
    Gbp = [sb(f"Gbp{i}", [128, TOK], F32, SEG_E + 10 * KB + i * 4 * KB) for i in range(2)]
    bld.retire(["wrkh0", "wrkh1"], "Gbp0")
    bld.retire(["wrkh0", "wrkh1"], "Gbp1")
    b2sb = sb("b2sb", [16, 2048], BF16, SEG_E + 18 * KB)
    bld.retire(["t32T"], "b2sb")
    bld.retire(["rotb0"], "b2sb")
    bld.retire(["diag0"], "b2sb")
    bld.retire(["diag1"], "b2sb")
    afs = sb("afs", [128, 512], F32, SEG_E + 18 * KB)
    for oldr in ("rotb0", "t32T"):
        for nn in ("afs", "ufs", "sfs"):
            bld.retire([oldr], nn)
    for oldr in ("wrkh0", "wrkh1", "t32T"):
        bld.retire([oldr], "b1sb0")
        bld.retire([oldr], "b1sb1")
    ufs = sb("ufs", [128, 512], F32, SEG_E + 20 * KB)
    sfs = sb("sfs", [128, 512], F32, SEG_E + 22 * KB)
    b1sb = [sb(f"b1sb{i}", [128, 32], F32, SEG_E + 24 * KB + i * 128) for i in range(2)]

    ldpool(b2sb[:, :], d["b2"][:, :], "b2sb")
    for dc in range(16):
        for tc in range(2):
            bk = 4 + (dc * 2 + tc) % 2
            bld.op("pe", lambda e, dc=dc, tc=tc, bk=bk: e.matmul(
                bank(bk), b2sb[0:16, dc * 128:(dc + 1) * 128],
                GT[0:16, tc * 512:(tc + 1) * 512], start=True, stop=True),
                reads=("b2sb", "GTr"), writes=(f"ps{bk}",), inc=True)
            bld.op("dve", lambda e, dc=dc, tc=tc, bk=bk: e.tensor_copy(
                out=delta[:, dc * TOK + tc * 512: dc * TOK + (tc + 1) * 512], in_=bank(bk)),
                reads=(f"ps{bk}",), writes=("delta",))
    bld.retire(["b2sb"], "afs")
    bld.retire(["b2sb"], "ufs")

    def w1_load(e_, si):
        slab = (si // 2) if si % 2 == 0 else (si // 2 + 8)
        slot = si % 4
        ldpool(w1sl[slot][:, :], d["w1s"][e_, slab].rearrange("p a b -> p (a b)"), f"w1sl{slot}")

    def w2_load(e_, js):
        ldpool(w2sl[js % 2][:, :], d["w2s"][e_, js].rearrange("p a b -> p (a b)"), f"w2sl{js % 2}")

    for si in range(4):
        w1_load(0, si)
    for js in range(2):
        w2_load(0, js)
    ldsp(b1sb[0][:, :], d["b1t"][0], "b1sb0")
    bld.dma("sp", lambda e: e.dma_start(out=grow[:, :], in_=GTp[0:1, :]),
            reads=("GTpr",), writes=("grow",), sem="grow")

    for ex in range(16):
        exn = ex % 2
        for tc in range(2):
            bk = 6 + tc
            bld.op("pe", lambda e, tc=tc, bk=bk: e.matmul(
                bank(bk), ones32[0:1, :], grow[0:1, tc * 512:(tc + 1) * 512],
                start=True, stop=True),
                reads=("ones32", "grow"), writes=(f"ps{bk}",), inc=True)
            bld.op("dve", lambda e, tc=tc, bk=bk, exn=exn: e.tensor_copy(
                out=Gbp[exn][:, tc * 512:(tc + 1) * 512], in_=bank(bk)),
                reads=(f"ps{bk}",), writes=(f"Gbp{exn}",))
        if ex + 1 < 16:
            bld.dma("sp", lambda e, ex=ex: e.dma_start(out=grow[:, :], in_=GTp[ex + 1:ex + 2, :]),
                    reads=("GTpr",), writes=("grow",), sem="grow")
            ldsp(b1sb[1 - exn][:, :], d["b1t"][ex + 1], f"b1sb{1 - exn}")

        for j in range(8):
            slA = (2 * j) % 4
            slB = (2 * j + 1) % 4
            for part in range(2):
                for tc in range(2):
                    u = part * 2 + tc
                    bkA = u % 2
                    bkB = 2 + u % 2
                    ic = j * 2 + part
                    for kc in range(16):
                        bld.op("pe", lambda e, slA=slA, kc=kc, part=part, tc=tc, bkA=bkA: e.matmul(
                            bank(bkA), w1sl[slA][:, kc * 256 + part * 128: kc * 256 + part * 128 + 128],
                            tT[:, kc * TOK + tc * 512: kc * TOK + (tc + 1) * 512],
                            start=(kc == 0), stop=(kc == 15)),
                            reads=(f"w1sl{slA}", "tT"), writes=(f"ps{bkA}",), inc=(kc == 15))
                    for kc in range(16):
                        bld.op("pe", lambda e, slB=slB, kc=kc, part=part, tc=tc, bkB=bkB: e.matmul(
                            bank(bkB), w1sl[slB][:, kc * 256 + part * 128: kc * 256 + part * 128 + 128],
                            tT[:, kc * TOK + tc * 512: kc * TOK + (tc + 1) * 512],
                            start=(kc == 0), stop=(kc == 15)),
                            reads=(f"w1sl{slB}", "tT"), writes=(f"ps{bkB}",), inc=(kc == 15))
                    bld.op("dve", lambda e, bkA=bkA, exn=exn, j=j, part=part: e.tensor_scalar(
                        out=afs[:, :], in0=bank(bkA), scalar1=b1sb[exn][:, 2 * j + part: 2 * j + part + 1],
                        scalar2=LIMIT, op0=ALU.add, op1=ALU.min),
                        reads=(f"ps{bkA}", f"b1sb{exn}"), writes=("afs",))
                    bld.op("dve", lambda e, bkB=bkB, exn=exn, j=j, part=part: e.tensor_scalar(
                        out=ufs[:, :], in0=bank(bkB),
                        scalar1=b1sb[exn][:, 16 + 2 * j + part: 16 + 2 * j + part + 1],
                        scalar2=-LIMIT, op0=ALU.add, op1=ALU.max),
                        reads=(f"ps{bkB}", f"b1sb{exn}"), writes=("ufs",))
                    bld.op("dve", lambda e: e.tensor_scalar(
                        out=ufs[:, :], in0=ufs[:, :], scalar1=LIMIT, scalar2=1.0,
                        op0=ALU.min, op1=ALU.add),
                        reads=("ufs",), writes=("ufs",))
                    bld.op("act", lambda e: e.activation(out=sfs[:, :], in_=afs[:, :],
                                                         func=AF.Silu, scale=ALPHA),
                           reads=("afs",), writes=("sfs",))
                    bld.op("dve", lambda e: e.tensor_mul(sfs[:, :], sfs[:, :], ufs[:, :]),
                           reads=("sfs", "ufs"), writes=("sfs",))
                    bld.op("dve", lambda e, ic=ic, tc=tc, exn=exn: e.tensor_mul(
                        act_e[:, ic * TOK + tc * 512: ic * TOK + (tc + 1) * 512],
                        sfs[:, :], Gbp[exn][:, tc * 512:(tc + 1) * 512]),
                        reads=("sfs", f"Gbp{exn}"), writes=("act_e",))
            for s_ in (2 * j + 4, 2 * j + 5):
                if s_ < 16:
                    w1_load(ex, s_)
                elif ex + 1 < 16:
                    w1_load(ex + 1, s_ - 16)
        for js in range(8):
            for part in range(2):
                for tc in range(2):
                    u = part * 2 + tc
                    bk = 4 + u % 2
                    dc = js * 2 + part
                    for kc in range(16):
                        bld.op("pe", lambda e, js=js, kc=kc, part=part, tc=tc, bk=bk: e.matmul(
                            bank(bk), w2sl[js % 2][:, kc * 256 + part * 128: kc * 256 + part * 128 + 128],
                            act_e[:, kc * TOK + tc * 512: kc * TOK + (tc + 1) * 512],
                            start=(kc == 0), stop=(kc == 15)),
                            reads=(f"w2sl{js % 2}", "act_e"), writes=(f"ps{bk}",), inc=(kc == 15))
                    bld.op("dve", lambda e, dc=dc, tc=tc, bk=bk: e.tensor_tensor(
                        out=delta[:, dc * TOK + tc * 512: dc * TOK + (tc + 1) * 512],
                        in0=bank(bk),
                        in1=delta[:, dc * TOK + tc * 512: dc * TOK + (tc + 1) * 512],
                        op=ALU.add),
                        reads=(f"ps{bk}", "delta"), writes=("delta",))
            s_ = js + 2
            if s_ < 8:
                w2_load(ex, s_)
            elif ex + 1 < 16:
                w2_load(ex + 1, s_ - 8)

    mark("P7")
    x1r = [sb(f"x1r{i}", [128, E], F32, SEG_D + i * 8 * KB) for i in range(2)]
    xn1r = [sb(f"xn1r{i}", [128, E], F32, SEG_D + 16 * KB + i * 8 * KB) for i in range(2)]
    ffn = [sb(f"ffn{i}", [128, E], F32, SEG_B2 + i * 8 * KB) for i in range(2)]
    for i in range(2):
        for rn in ("bob", "normwb", "tw", "xn1b", "w1sl0", "w1sl1", "w1sl2", "w1sl3"):
            bld.retire([rn], f"x1r{i}")
            bld.retire([rn], f"xn1r{i}")
        bld.retire(["tT"], f"ffn{i}")

    def p7_load(i):
        ip = i % 2
        bld.dma("sp", lambda e, i=i, ip=ip: e.dma_start(out=x1r[ip][:, :], in_=x1D[i * 128:(i + 1) * 128, :]),
                reads=("x1D",), writes=(f"x1r{ip}",), sem=f"x1r{ip}")
        bld.dma("sp", lambda e, i=i, ip=ip: e.dma_start(out=xn1r[ip][:, :], in_=xn1D[i * 128:(i + 1) * 128, :]),
                reads=("xn1D",), writes=(f"xn1r{ip}",), sem=f"xn1r{ip}")

    def p7_trans(i):
        ip = i % 2
        for g in range(4):
            bk = 4 * ip + g
            for j in range(4):
                dc = g * 4 + j
                bld.op("pe", lambda e, dc=dc, i=i, bk=bk, j=j: e.transpose(
                    bank(bk)[:, j * 128:(j + 1) * 128],
                    delta[:, dc * TOK + i * 128: dc * TOK + (i + 1) * 128], ident32[:, :]),
                    reads=("delta", "ident32"), writes=(f"ps{bk}",), inc=(j == 3))

    def p7_chain(i):
        ip = i % 2
        for g in range(4):
            bk = 4 * ip + g
            bld.op("dve", lambda e, g=g, bk=bk, ip=ip: e.tensor_add(
                ffn[ip][:, g * 512:(g + 1) * 512], bank(bk), xn1r[ip][:, g * 512:(g + 1) * 512]),
                reads=(f"ps{bk}", f"xn1r{ip}"), writes=(f"ffn{ip}",))
        for bc in range(4):
            bld.op("dve", lambda e, bc=bc, ip=ip: e.bn_stats(
                out=stat[:, 2 + 6 * bc:8 + 6 * bc], in_=ffn[ip][:, bc * 512:(bc + 1) * 512]),
                reads=(f"ffn{ip}",), writes=("bn6",))
        bld.op("dve", lambda e: e.bn_aggr(out=stat[:, 0:2], in_=stat[:, 2:26]),
               reads=("bn6",), writes=("bn2",))
        bld.op("dve", lambda e: e.tensor_scalar_mul(stat[:, 26:27], stat[:, 0:1], -1.0),
               reads=("bn2",), writes=("nm0",))
        bld.op("act", lambda e: e.activation(out=stat[:, 27:28], in_=stat[:, 1:2], func=AF.Ln, bias=epsb[:, 0:1]),
               reads=("bn2", "epsb"), writes=("rs0",))
        bld.op("act", lambda e: e.activation(out=stat[:, 27:28], in_=stat[:, 27:28], func=AF.Exp, scale=-0.5),
               reads=("rs0",), writes=("rs0",))
        bld.op("dve", lambda e, ip=ip: e.tensor_scalar(
            out=ffn[ip][:, :], in0=ffn[ip][:, :], scalar1=stat[:, 26:27], scalar2=stat[:, 27:28],
            op0=ALU.add, op1=ALU.mult),
            reads=(f"ffn{ip}", "nm0", "rs0"), writes=(f"ffn{ip}",))
        bld.op("dve", lambda e, ip=ip: e.tensor_add(x1r[ip][:, :], x1r[ip][:, :], ffn[ip][:, :]),
               reads=(f"x1r{ip}", f"ffn{ip}"), writes=(f"x1r{ip}",))
        bld.dma("sp", lambda e, i=i, ip=ip: e.dma_start(out=out[i * 128:(i + 1) * 128, :], in_=x1r[ip][:, :]),
                reads=(f"x1r{ip}",), writes=("outD",), sem="outD")

    for i in range(8):
        p7_load(i)
        p7_trans(i)
        if i > 0:
            p7_chain(i - 1)
    p7_chain(7)
    bld.barrier("sp", ["outD"])

    mark("END")
    blk = nc.Block().__enter__()
    bld.finalize(blk)
    blk.__exit__(None, None, None)
    return nc


def kernel(**inputs):
    per_core = host_prep(inputs)
    nc = build_program()
    res = run_bass_kernel_spmd(nc, per_core, core_ids=list(range(8)))
    outp = np.empty((B, S, E), np.float32)
    for c in range(8):
        b, hf = c // 2, c % 2
        outp[b, hf * 1024:(hf + 1) * 1024] = res.results[c]["out"]
    return outp


# version 19
# speedup vs baseline: 1.0387x; 1.0387x over previous
import math

import numpy as np

import concourse.bass as bass
import concourse.mybir as mybir
from concourse.bass_utils import run_bass_kernel_spmd

F32 = mybir.dt.float32
BF16 = mybir.dt.bfloat16
AF = mybir.ActivationFunctionType
ALU = mybir.AluOpType
AX = mybir.AxisListType

B, S, E = 4, 2048, 2048
H, LAT, D, R = 16, 512, 128, 64
NEXP, II = 16, 2048
TOK = 1024
SF = 2048
EPS = 1e-5
ALPHA = 1.702
LIMIT = 7.0
SCALE = 1.0 / math.sqrt(D)

SBUF_BASE = 16512
SBUF_TOP = 229344
KB = 1024

SEG_A = SBUF_BASE
SEG_B1 = SEG_A + 4 * KB
SEG_B2 = SEG_B1 + 64 * KB
SEG_C = SEG_B2 + 64 * KB
SEG_D = SEG_C + 16 * KB
SEG_E = SEG_D + 32 * KB


class Builder:

    ENGS = ("pe", "dve", "act", "pool", "sp")

    def __init__(self, nc):
        self.nc = nc
        self.stream = {e: [] for e in self.ENGS}
        self.sems = {}
        self.counts = {}
        self._sem_guards = []
        for name in self.ENGS:
            self._new_sem(name)
        self.final_cells = {}
        self.res = {}
        self.waited = {e: {} for e in self.ENGS}
        self.pe_pending = []

    def _new_sem(self, name):
        cm = self.nc.semaphore(name + "_sem")
        self.sems[name] = cm.__enter__()
        self._sem_guards.append(cm)
        self.counts[name] = 0

    def _r(self, name):
        return self.res.setdefault(name, {"w": [], "r": []})

    def _emit_wait(self, eng, ev):
        sem, val = ev
        wm = self.waited[eng]
        s = self.sems[sem]
        if isinstance(val, list):
            if wm.get(sem) == "final":
                return
            wm[sem] = "final"
            self.stream[eng].append(lambda e, s=s, val=val: e.wait_ge(s, val[0]))
            return
        cur = wm.get(sem, 0)
        if cur == "final" or (cur != "final" and cur >= val):
            return
        wm[sem] = val
        self.stream[eng].append(lambda e, s=s, val=val: e.wait_ge(s, val))

    def _deps(self, eng, reads, writes):
        best = {}
        cells = []
        for rn in reads:
            r = self._r(rn)
            for ev in r["w"]:
                (cells.append(ev) if isinstance(ev[1], list) else
                 best.__setitem__(ev[0], max(best.get(ev[0], 0), ev[1])))
        for wn in writes:
            r = self._r(wn)
            for ev in r["w"] + r["r"]:
                (cells.append(ev) if isinstance(ev[1], list) else
                 best.__setitem__(ev[0], max(best.get(ev[0], 0), ev[1])))
        for ev in cells:
            self._emit_wait(eng, ev)
        for sem, val in best.items():
            self._emit_wait(eng, (sem, val))

    def _record(self, reads, writes, ev):
        for rn in reads:
            self._r(rn)["r"].append(ev)
        for wn in writes:
            r = self._r(wn)
            r["w"] = [ev]
            r["r"] = []

    def op(self, eng, emit, reads=(), writes=(), inc=True):
        self._deps(eng, reads, writes)
        if eng == "pe" and not inc:
            self.pe_pending.append((tuple(reads), tuple(writes)))
            self.stream[eng].append(emit)
            return
        self.counts[eng] += 1
        val = self.counts[eng]
        s = self.sems[eng]

        def wrapped(e, emit=emit, s=s):
            emit(e).then_inc(s, 1)
        self.stream[eng].append(wrapped)
        ev = (eng, val)
        self._record(reads, writes, ev)
        if eng == "pe":
            for rds, wrs in self.pe_pending:
                self._record(rds, wrs, ev)
            self.pe_pending = []

    def dma(self, eng, emit, reads=(), writes=(), sem=None, final=False):
        assert eng in ("sp", "pool") and sem is not None
        if sem not in self.sems:
            self._new_sem(sem)
            if final:
                self.final_cells[sem] = [None]
        self._deps(eng, reads, writes)
        self.counts[sem] += 16
        val = self.counts[sem]
        s = self.sems[sem]

        def wrapped(e, emit=emit, s=s):
            emit(e).then_inc(s, 16)
        self.stream[eng].append(wrapped)
        ev = (sem, self.final_cells[sem]) if final else (sem, val)
        if final:
            assert sem in self.final_cells
        self._record(reads, writes, ev)

    def record_extra_write(self, name, ev):
        self._r(name)["w"].append(ev)

    def barrier(self, eng, names):
        for rn in names:
            r = self._r(rn)
            for ev in r["w"]:
                self._emit_wait(eng, ev)
            for ev in r["r"]:
                self._emit_wait(eng, ev)

    def retire(self, old_names, new_name):
        rn = self._r(new_name)
        for old in old_names:
            ro = self._r(old)
            rn["r"].extend(ro["w"])
            rn["r"].extend(ro["r"])

    def finalize(self, blk):
        for sem, cell in self.final_cells.items():
            cell[0] = self.counts[sem]
        fns = {"pe": blk.tensor, "dve": blk.vector, "act": blk.scalar,
               "pool": blk.gpsimd, "sp": blk.sync}
        self.id_base = {}
        for eng in self.ENGS:
            ops = self.stream[eng]
            if not ops:
                continue
            self.id_base[eng] = self.nc.next_id()

            def body(e, ops=ops):
                for f in ops:
                    f(e)
            fns[eng](body)
        for sem, c in self.counts.items():
            assert c < 60000, (sem, c)



def _slab_pack(wT, nslab, slabw):
    Kin, Mout = wT.shape
    kc = Kin // 128
    assert Mout == nslab * slabw
    return np.ascontiguousarray(wT.reshape(kc, 128, nslab, slabw).transpose(2, 1, 0, 3))


def host_prep(inp):
    f32 = np.float32
    T = lambda a: np.ascontiguousarray(np.asarray(a, dtype=f32).T)
    out = {}
    out["wkv"] = _slab_pack(T(inp["L0_attn_Wkv_d"]), 1, 512)
    out["wqd"] = _slab_pack(T(inp["L0_attn_Wq_d"]), 1, 512)
    out["wku"] = _slab_pack(T(inp["L0_attn_Wk_u"]), 16, 64)
    out["wqu"] = _slab_pack(T(inp["L0_attn_Wq_u"]), 16, 64)
    out["wrk"] = _slab_pack(T(inp["L0_attn_Wrk"]), 16, 64)
    out["wrq"] = _slab_pack(T(inp["L0_attn_Wrq"]), 16, 64)
    out["wvu"] = _slab_pack(T(inp["L0_attn_Wv_u"]), 1, 2048)
    wo = T(inp["L0_attn_Wo"])
    out["wo"] = np.ascontiguousarray(wo.reshape(16, 128, 4, 512).transpose(1, 0, 2, 3))

    A = lambda k: np.asarray(inp[k], dtype=f32)
    out["bkv"] = np.ascontiguousarray(A("L0_attn_bkv_d").reshape(4, 128).T)
    out["bqd"] = np.ascontiguousarray(A("L0_attn_bq_d").reshape(4, 128).T)
    out["bku"] = np.ascontiguousarray(np.tile(A("L0_attn_bk_u").reshape(16, 64).T, (2, 1)))
    out["bqu"] = np.ascontiguousarray(A("L0_attn_bq_u").reshape(16, 64).T)
    out["brk"] = np.ascontiguousarray(np.tile(A("L0_attn_brk").reshape(16, 64).T, (2, 1)))
    brq = np.zeros((128, 16), f32)
    brq[64:] = A("L0_attn_brq").reshape(16, 64).T
    out["brq"] = brq
    out["bvu"] = A("L0_attn_bv_u").reshape(1, 2048)
    out["bo"] = A("L0_attn_bo").reshape(1, 2048)
    out["normw"] = A("L0_ffn_norm_w").reshape(1, 2048)
    out["gateb"] = A("L0_ffn_gate_b").reshape(1, 16)
    out["gatew"] = _slab_pack(T(inp["L0_ffn_gate_w"]), 1, 16)

    w1 = np.asarray(inp["L0_ffn_w1"], dtype=f32)
    perm = np.concatenate([np.arange(0, 2 * II, 2), np.arange(1, 2 * II, 2)])
    w1p = w1[:, perm, :]
    w1s = np.empty((16, 16, 128, 16, 256), f32)
    for e in range(16):
        w1s[e] = _slab_pack(np.ascontiguousarray(w1p[e].T), 16, 256)
    out["w1s"] = w1s
    out["b1t"] = np.ascontiguousarray(
        np.asarray(inp["L0_ffn_b1"], dtype=f32)[:, perm].reshape(16, 32, 128).transpose(0, 2, 1))
    w2 = np.asarray(inp["L0_ffn_w2"], dtype=f32)
    w2s = np.empty((16, 8, 128, 16, 256), f32)
    for e in range(16):
        w2s[e] = _slab_pack(np.ascontiguousarray(w2[e].T), 8, 256)
    out["w2s"] = w2s
    out["b2"] = np.asarray(inp["L0_ffn_b2"], dtype=f32)

    out["ident"] = np.eye(128, dtype=f32)
    out["ones"] = np.ones((1, 128), f32)

    inv_freq = 1.0 / (10000.0 ** (np.arange(0, R, 2, dtype=np.float64) / R))
    t64 = np.arange(S, dtype=np.float64)
    emb = np.concatenate([np.outer(t64, inv_freq)] * 2, axis=1)
    cos = np.cos(emb).astype(f32)
    sin_eff = np.sin(emb).astype(f32)
    sin_eff[:, :32] *= -1.0

    x = np.asarray(inp["L0_attn_x"], dtype=f32)
    per_core = []
    for c in range(8):
        b, hf = c // 2, c % 2
        own = slice(hf * 1024, (hf + 1) * 1024)
        oth = slice((1 - hf) * 1024, (2 - hf) * 1024)
        xk = np.concatenate([x[b, own], x[b, oth]], axis=0)
        order = np.concatenate([np.arange(hf * 1024, (hf + 1) * 1024),
                                np.arange((1 - hf) * 1024, (2 - hf) * 1024)])
        d = dict(out)
        d["xk"] = np.ascontiguousarray(xk)
        import ml_dtypes as _mld
        d["cosk"] = np.ascontiguousarray(np.tile(cos[order].T, (2, 1))).astype(_mld.bfloat16)
        d["sink"] = np.ascontiguousarray(np.tile(sin_eff[order].T, (2, 1))).astype(_mld.bfloat16)
        per_core.append(d)
    return per_core



def build_program():
    nc = bass.Bass("TRN2", target_bir_lowering=False, debug=False)
    gi = lambda name, shape: nc.dram_tensor(name, list(shape), F32, kind="ExternalInput").ap()
    d = {}
    d["xk"] = gi("xk", (SF, E))
    d["cosk"] = nc.dram_tensor("cosk", [128, SF], BF16, kind="ExternalInput").ap()
    d["sink"] = nc.dram_tensor("sink", [128, SF], BF16, kind="ExternalInput").ap()
    for n, shp in [
                   ("wkv", (1, 128, 16, 512)), ("wqd", (1, 128, 16, 512)),
                   ("wku", (16, 128, 4, 64)), ("wqu", (16, 128, 4, 64)),
                   ("wrk", (16, 128, 16, 64)), ("wrq", (16, 128, 4, 64)),
                   ("wvu", (1, 128, 4, 2048)), ("wo", (128, 16, 4, 512)),
                   ("bkv", (128, 4)), ("bqd", (128, 4)), ("bku", (128, 16)),
                   ("bqu", (64, 16)), ("brk", (128, 16)), ("brq", (128, 16)),
                   ("bvu", (1, 2048)), ("bo", (1, 2048)), ("normw", (1, 2048)),
                   ("gateb", (1, 16)), ("gatew", (1, 128, 16, 16)),
                   ("b1t", (16, 128, 32)), ("b2", (16, 2048)),
                   ("ident", (128, 128)), ("ones", (1, 128)),
                   ("w1s", (16, 16, 128, 16, 256)), ("w2s", (16, 8, 128, 16, 256))]:
        d[n] = gi(n, shp)

    out = nc.dram_tensor("out", [TOK, E], F32, kind="ExternalOutput").ap()
    x1D = nc.dram_tensor("x1_scr", [TOK, E], F32, kind="Internal").ap()
    xn1D = nc.dram_tensor("xn1_scr", [TOK, E], F32, kind="Internal").ap()
    attnD = nc.dram_tensor("attn_scr", [2, 128, 16, 512], BF16, kind="Internal").ap()
    qTD = nc.dram_tensor("qt_scr", [16, 128, TOK], BF16, kind="Internal").ap()
    qTDs = nc.dram_tensor("qts_scr", [16, 128, TOK], BF16, kind="Internal").ap()

    bld = Builder(nc)
    build_program.bld = bld
    build_program.phases = phases = []
    def mark(tag):
        phases.append((tag, {e: len(bld.stream[e]) for e in bld.ENGS}))
    psum = nc.alloc_psum_tensor([128, 4096], F32)
    bank = lambda i: psum[:, i * 512:(i + 1) * 512]
    bank_bf = lambda i: psum[:, i * 512:(i + 1) * 512].bitcast(BF16)

    nalloc = [0]

    def sb(name, shape, dtype, offset):
        nbytes = int(np.prod(shape[1:])) * mybir.dt.size(dtype)
        assert SBUF_BASE <= offset and offset + nbytes <= SBUF_TOP, (name, offset, nbytes)
        nalloc[0] += 1
        return nc.alloc_sbuf_tensor_at(f"{name}", shape=shape, dtype=dtype, offset=offset)

    ident = sb("identb", [128, 128], BF16, SEG_A)
    ident32 = sb("ident32", [128, 128], F32, SEG_A + 256)
    ones32 = sb("ones32", [1, 128], F32, SEG_A + 768)
    gatew_sb = sb("gatew", [128, 16 * 16], F32, SEG_A + 1280)
    gateb_b = sb("gatebb", [128, 16], F32, SEG_A + 2304)
    stat = sb("stat", [128, 384], F32, SEG_A + 2432)
    epsb = sb("epsb", [128, 1], F32, SEG_A + 4032)


    xnT = sb("xnT", [128, 16 * SF], BF16, SEG_B1)
    xw = [sb(f"xw{i}", [128, E], F32, SEG_B2 + i * 8 * KB) for i in range(2)]
    xnb = [sb(f"xnb{i}", [128, E], BF16, SEG_B2 + 16 * KB + i * 4 * KB) for i in range(2)]

    cosk = sb("coskb", [128, SF], BF16, SEG_E)
    sink = sb("sinkb", [128, SF], BF16, SEG_E + 4 * KB)
    krs = [sb(f"krs{i}", [128, 512], BF16, SEG_D + 24 * KB + i * KB) for i in range(6)]
    rotb = [sb(f"rotb{i}", [128, 512], BF16, SEG_E + 8 * KB + i * KB) for i in range(6)]
    bkv_sb = sb("bkvs", [128, 4], F32, SEG_E + 20 * KB)
    bqd_sb = sb("bqds", [128, 4], F32, SEG_E + 20 * KB + 32)
    bku_sb = sb("bkus", [128, 16], F32, SEG_E + 20 * KB + 64)
    bqu_sb = sb("bqus", [64, 16], F32, SEG_E + 20 * KB + 128)
    brk_sb = sb("brks", [128, 16], F32, SEG_E + 20 * KB + 192)
    brq_sb = sb("brqs", [128, 16], F32, SEG_E + 20 * KB + 256)
    wkuh = [sb(f"wkuh{i}", [128, 4 * 64], BF16, SEG_E + 20 * KB + 320 + i * 512) for i in range(2)]
    wquh = [sb(f"wquh{i}", [128, 4 * 64], BF16, SEG_E + 20 * KB + 1344 + i * 512) for i in range(2)]
    wrqh = [sb(f"wrqh{i}", [128, 4 * 64], BF16, SEG_E + 20 * KB + 2368 + i * 512) for i in range(2)]
    wrkh = [sb(f"wrkh{i}", [128, 16 * 64], BF16, SEG_E + 20 * KB + 3392 + i * 2 * KB) for i in range(2)]

    kv_dT = sb("kv_dT", [128, 4 * SF], BF16, SEG_C)
    wkvc = [sb(f"wkvc{i}", [128, 16 * 128], BF16, SEG_D + i * 4 * KB) for i in range(2)]
    wqdc = [sb(f"wqdc{i}", [128, 16 * 128], BF16, SEG_D + 8 * KB + i * 4 * KB) for i in range(2)]
    q_dT = sb("q_dT", [128, 4 * TOK], BF16, SEG_D + 16 * KB)

    def ldsp(dst, src, res, sem=None, final=False):
        bld.dma("sp", lambda e, dst=dst, src=src: e.dma_start(out=dst, in_=src),
                writes=(res,), sem=sem or res, final=final)

    def ldpool(dst, src, res, sem=None, final=False):
        bld.dma("pool", lambda e, dst=dst, src=src: e.dma_start(out=dst, in_=src),
                writes=(res,), sem=sem or res, final=final)

    ldpool(ident[:, :], d["ident"][:, :], "identb")
    ldsp(ident32[:, :], d["ident"][:, :], "ident32", sem="csp", final=True)
    ldsp(ones32[:, :], d["ones"][:, :], "ones32", sem="csp", final=True)
    ldsp(gatew_sb[:, :], d["gatew"][0].rearrange("p a b -> p (a b)"), "gatew", sem="csp", final=True)
    ldsp(gateb_b[:, :], d["gateb"][0:1, :].to_broadcast([128, 16]), "gatebb", sem="csp", final=True)
    ldsp(cosk[:, :], d["cosk"][:, :], "cosk", sem="csp", final=True)
    ldsp(sink[:, :], d["sink"][:, :], "sink", sem="csp", final=True)
    for nm, dst in [("bkv", bkv_sb), ("bqd", bqd_sb), ("bku", bku_sb),
                    ("bqu", bqu_sb), ("brk", brk_sb), ("brq", brq_sb)]:
        ldsp(dst[:, :], d[nm][:, :], nm + "s", sem="csp", final=True)
    ldpool(wkvc[0][:, :].rearrange("p (a b) -> p a b", a=16), d["wkv"][0, :, :, 0:128], "wkvc0")
    ldpool(wqdc[0][:, :].rearrange("p (a b) -> p a b", a=16), d["wqd"][0, :, :, 0:128], "wqdc0")

    bld.op("dve", lambda e: e.memset(epsb[:, :], EPS), writes=("epsb",))

    mark("P1")
    def p1_chain(kt):
        p = kt % 2
        xb = xw[p]
        xr = f"xw{p}"
        bld.dma("sp", lambda e, xb=xb, kt=kt: e.dma_start(out=xb[:, :], in_=d["xk"][kt * 128:(kt + 1) * 128, :]),
                writes=(xr,), sem=xr)
        sN = 26 + 2 * p
        for bc in range(4):
            bld.op("dve", lambda e, xb=xb, bc=bc: e.bn_stats(
                out=stat[:, 2 + 6 * bc:8 + 6 * bc], in_=xb[:, bc * 512:(bc + 1) * 512]),
                reads=(xr,), writes=("bn6",))
        bld.op("dve", lambda e: e.bn_aggr(out=stat[:, 0:2], in_=stat[:, 2:26]),
               reads=("bn6",), writes=("bn2",))
        bld.op("dve", lambda e, sN=sN: e.tensor_scalar_mul(stat[:, sN:sN + 1], stat[:, 0:1], -1.0),
               reads=("bn2",), writes=(f"nm{p}",))
        bld.op("act", lambda e, sN=sN: e.activation(out=stat[:, sN + 1:sN + 2], in_=stat[:, 1:2],
                                                    func=AF.Ln, bias=epsb[:, 0:1]),
               reads=("bn2", "epsb"), writes=(f"rs{p}",))
        bld.op("act", lambda e, sN=sN: e.activation(out=stat[:, sN + 1:sN + 2], in_=stat[:, sN + 1:sN + 2],
                                                    func=AF.Exp, scale=-0.5),
               reads=(f"rs{p}",), writes=(f"rs{p}",))
        bld.op("dve", lambda e, xb=xb, p=p, sN=sN: e.tensor_scalar(
            out=xnb[p][:, :], in0=xb[:, :], scalar1=stat[:, sN:sN + 1],
            scalar2=stat[:, sN + 1:sN + 2], op0=ALU.add, op1=ALU.mult),
            reads=(xr, f"nm{p}", f"rs{p}"), writes=(f"xnb{p}",))

    def p1_trans(kt):
        p = kt % 2
        for g in range(4):
            bk = 2 * p + g % 2
            for j in range(4):
                ec = g * 4 + j
                bld.op("pe", lambda e, p=p, ec=ec, bk=bk, j=j: e.transpose(
                    bank_bf(bk)[:, j * 128:(j + 1) * 128],
                    xnb[p][:, ec * 128:(ec + 1) * 128], ident[:, :]),
                    reads=(f"xnb{p}", "identb"), writes=(f"ps{bk}",), inc=(j == 3))
            bld.op("act", lambda e, g=g, bk=bk, kt=kt: e.copy(
                out=xnT[:, :].rearrange("p (c t) -> p c t", c=16)[:, g * 4:g * 4 + 4, kt * 128:(kt + 1) * 128],
                in_=bank_bf(bk)[:, 0:512]),
                reads=(f"ps{bk}",), writes=("xnT",))

    for kt in range(16):
        p1_chain(kt)
        if kt > 0:
            p1_trans(kt - 1)
    p1_trans(15)

    mark("P2")
    ldpool(wkvc[1][:, :].rearrange("p (a b) -> p a b", a=16), d["wkv"][0, :, :, 128:256], "wkvc1")
    ldpool(wqdc[1][:, :].rearrange("p (a b) -> p a b", a=16), d["wqd"][0, :, :, 128:256], "wqdc1")
    for mc in range(4):
        sl = mc % 2
        for tc in range(4):
            bk = tc % 2
            for kc in range(16):
                bld.op("pe", lambda e, sl=sl, kc=kc, tc=tc, bk=bk: e.matmul(
                    bank(bk), wkvc[sl][:, kc * 128:(kc + 1) * 128],
                    xnT[:, kc * SF + tc * 512: kc * SF + (tc + 1) * 512],
                    start=(kc == 0), stop=(kc == 15)),
                    reads=(f"wkvc{sl}", "xnT"), writes=(f"ps{bk}",), inc=(kc == 15))
            bld.op("dve", lambda e, mc=mc, tc=tc, bk=bk: e.tensor_scalar(
                out=kv_dT[:, mc * SF + tc * 512: mc * SF + (tc + 1) * 512],
                in0=bank(bk), scalar1=bkv_sb[:, mc:mc + 1], scalar2=None, op0=ALU.add),
                reads=(f"ps{bk}", "bkvs"), writes=("kv_dT",))
        for tc in range(2):
            bk = 2 + tc % 2
            for kc in range(16):
                bld.op("pe", lambda e, sl=sl, kc=kc, tc=tc, bk=bk: e.matmul(
                    bank(bk), wqdc[sl][:, kc * 128:(kc + 1) * 128],
                    xnT[:, kc * SF + tc * 512: kc * SF + (tc + 1) * 512],
                    start=(kc == 0), stop=(kc == 15)),
                    reads=(f"wqdc{sl}", "xnT"), writes=(f"ps{bk}",), inc=(kc == 15))
            bld.op("dve", lambda e, mc=mc, tc=tc, bk=bk: e.tensor_scalar(
                out=q_dT[:, mc * TOK + tc * 512: mc * TOK + (tc + 1) * 512],
                in0=bank(bk), scalar1=bqd_sb[:, mc:mc + 1], scalar2=None, op0=ALU.add),
                reads=(f"ps{bk}", "bqds"), writes=("q_dT",))
        if mc + 2 < 4:
            ldpool(wkvc[sl][:, :].rearrange("p (a b) -> p a b", a=16),
                   d["wkv"][0, :, :, (mc + 2) * 128:(mc + 3) * 128], f"wkvc{sl}")
            ldpool(wqdc[sl][:, :].rearrange("p (a b) -> p a b", a=16),
                   d["wqd"][0, :, :, (mc + 2) * 128:(mc + 3) * 128], f"wqdc{sl}")

    mark("P3a")
    kT = sb("kT", [128, 16 * SF], BF16, SEG_B2)
    bld.retire(["xw0", "xw1", "xnb0", "xnb1"], "kT")
    ldpool(wrkh[0][:, :], d["wrk"][0].rearrange("p a b -> p (a b)"), "wrkh0")
    ldpool(wkuh[0][:, :], d["wku"][0].rearrange("p a b -> p (a b)"), "wkuh0")
    rope_prev = None

    def rope_a(h, tc, bk, costab, brs, brsname, rp, ro=64):
        kb_, rb_ = krs[rp], rotb[rp]
        kr_, rr_ = f"krs{rp}", f"rotb{rp}"
        bld.op("dve", lambda e, bk=bk, h=h, ro=ro: e.tensor_scalar(
            out=kb_[ro:ro + 64, :], in0=bank(bk)[ro:ro + 64, :],
            scalar1=brs[ro:ro + 64, h:h + 1], scalar2=None, op0=ALU.add),
            reads=(f"ps{bk}", brsname), writes=(kr_,))
        bld.dma("sp", lambda e, kb_=kb_, rb_=rb_, ro=ro: e.dma_start(
            out=rb_[ro + 32:ro + 64, :], in_=kb_[ro:ro + 32, :]),
            reads=(kr_,), writes=(rr_,), sem="rot" + str(rp))
        bld.dma("sp", lambda e, kb_=kb_, rb_=rb_, ro=ro: e.dma_start(
            out=rb_[ro:ro + 32, :], in_=kb_[ro + 32:ro + 64, :]),
            reads=(kr_,), writes=(rr_,), sem="rot" + str(rp))
        bld.op("dve", lambda e, tc=tc, kb_=kb_, ro=ro: e.tensor_mul(
            kb_[ro:ro + 64, :], kb_[ro:ro + 64, :], costab[ro:ro + 64, tc * 512:(tc + 1) * 512]),
            reads=(kr_, "cosk"), writes=(kr_,))
        bld.op("pool", lambda e, tc=tc, rb_=rb_, ro=ro: e.tensor_mul(
            rb_[ro:ro + 64, :], rb_[ro:ro + 64, :], sink[ro:ro + 64, tc * 512:(tc + 1) * 512]),
            reads=(rr_, "sink"), writes=(rr_,))

    def rope_b(h, tc, tgt, tgtres, T, rp, ro=64):
        kb_, rb_ = krs[rp], rotb[rp]
        kr_, rr_ = f"krs{rp}", f"rotb{rp}"
        if tgt is None:
            bld.op("dve", lambda e, kb_=kb_, rb_=rb_: e.tensor_add(
                kb_[64:128, :], kb_[64:128, :], rb_[64:128, :]),
                reads=(kr_, rr_), writes=(kr_,))
            bld.dma("pool", lambda e, h=h, tc=tc, kb_=kb_: e.dma_start(
                out=qTD[h, :, tc * 512:(tc + 1) * 512], in_=kb_[:, :]),
                reads=(kr_,), writes=("qTDr",), sem="qtd" + str(rp))
            bld.dma("pool", lambda e, h=h, tc=tc, kb_=kb_: e.dma_start(
                out=qTDs[h, 0:64, tc * 512:(tc + 1) * 512], in_=kb_[64:128, :]),
                reads=(kr_,), writes=("qTDsr",), sem="qtds" + str(rp))
            bld.dma("pool", lambda e, h=h, tc=tc, kb_=kb_: e.dma_start(
                out=qTDs[h, 64:128, tc * 512:(tc + 1) * 512], in_=kb_[0:64, :]),
                reads=(kr_,), writes=("qTDsr",), sem="qtds" + str(rp))
            return
        bld.op("dve", lambda e, h=h, tc=tc, tgt=tgt, T=T, kb_=kb_, rb_=rb_, ro=ro: e.tensor_add(
            tgt[ro:ro + 64, h * T + tc * 512: h * T + (tc + 1) * 512],
            kb_[ro:ro + 64, :], rb_[ro:ro + 64, :]),
            reads=(kr_, rr_), writes=(tgtres,))

    ldpool(wquh[0][:, :], d["wqu"][0].rearrange("p a b -> p (a b)"), "wquh0")
    ldpool(wrqh[0][:, :], d["wrq"][0].rearrange("p a b -> p (a b)"), "wrqh0")
    gtile = [0]
    for h in range(16):
        sl = h % 2
        if h + 1 < 16:
            ldpool(wrkh[1 - sl][:, :], d["wrk"][h + 1].rearrange("p a b -> p (a b)"), f"wrkh{1 - sl}")
            ldpool(wkuh[1 - sl][:, :], d["wku"][h + 1].rearrange("p a b -> p (a b)"), f"wkuh{1 - sl}")
            ldpool(wquh[1 - sl][:, :], d["wqu"][h + 1].rearrange("p a b -> p (a b)"), f"wquh{1 - sl}")
            ldpool(wrqh[1 - sl][:, :], d["wrq"][h + 1].rearrange("p a b -> p (a b)"), f"wrqh{1 - sl}")
        for pair in range(2):
            tcA, tcB = 2 * pair, 2 * pair + 1
            bkA = gtile[0] % 4
            rpA = gtile[0] % 6
            gtile[0] += 1
            bkB = gtile[0] % 4
            rpB = gtile[0] % 6
            gtile[0] += 1
            for kc in range(4):
                bld.op("pe", lambda e, sl=sl, kc=kc, tcA=tcA, bkA=bkA: e.matmul(
                    bank(bkA)[0:64, :], wkuh[sl][:, kc * 64:(kc + 1) * 64],
                    kv_dT[:, kc * SF + tcA * 512: kc * SF + (tcA + 1) * 512],
                    start=(kc == 0), stop=(kc == 3), tile_position=(0, 0)),
                    reads=(f"wkuh{sl}", "kv_dT"), writes=(f"ps{bkA}",), inc=False)
                bld.op("pe", lambda e, sl=sl, kc=kc, tcB=tcB, bkB=bkB: e.matmul(
                    bank(bkB)[64:128, :], wkuh[sl][:, kc * 64:(kc + 1) * 64],
                    kv_dT[:, kc * SF + tcB * 512: kc * SF + (tcB + 1) * 512],
                    start=(kc == 0), stop=(kc == 3), tile_position=(0, 64)),
                    reads=(f"wkuh{sl}", "kv_dT"), writes=(f"ps{bkB}",), inc=False)
            for kc in range(16):
                bld.op("pe", lambda e, sl=sl, kc=kc, tcA=tcA, bkA=bkA: e.matmul(
                    bank(bkA)[64:128, :], wrkh[sl][:, kc * 64:(kc + 1) * 64],
                    xnT[:, kc * SF + tcA * 512: kc * SF + (tcA + 1) * 512],
                    start=(kc == 0), stop=(kc == 15), tile_position=(0, 64)),
                    reads=(f"wrkh{sl}", "xnT"), writes=(f"ps{bkA}",), inc=(kc == 15))
                bld.op("pe", lambda e, sl=sl, kc=kc, tcB=tcB, bkB=bkB: e.matmul(
                    bank(bkB)[0:64, :], wrkh[sl][:, kc * 64:(kc + 1) * 64],
                    xnT[:, kc * SF + tcB * 512: kc * SF + (tcB + 1) * 512],
                    start=(kc == 0), stop=(kc == 15), tile_position=(0, 0)),
                    reads=(f"wrkh{sl}", "xnT"), writes=(f"ps{bkB}",), inc=(kc == 15))
            bld.op("act", lambda e, h=h, tcA=tcA, bkA=bkA: e.activation(
                out=kT[0:64, h * SF + tcA * 512: h * SF + (tcA + 1) * 512],
                in_=bank(bkA)[0:64, :], func=AF.Identity, bias=bku_sb[0:64, h:h + 1]),
                reads=(f"ps{bkA}", "bkus"), writes=("kT",))
            rope_a(h, tcA, bkA, cosk, brk_sb, "brks", rpA, ro=64)
            if rope_prev is not None:
                rope_b(*rope_prev)
            rope_prev = (h, tcA, kT, "kT", SF, rpA, 64)
            bld.op("act", lambda e, h=h, tcB=tcB, bkB=bkB: e.activation(
                out=kT[64:128, h * SF + tcB * 512: h * SF + (tcB + 1) * 512],
                in_=bank(bkB)[64:128, :], func=AF.Identity, bias=bku_sb[64:128, h:h + 1]),
                reads=(f"ps{bkB}", "bkus"), writes=("kT",))
            rope_a(h, tcB, bkB, cosk, brk_sb, "brks", rpB, ro=0)
            if rope_prev is not None:
                rope_b(*rope_prev)
            rope_prev = (h, tcB, kT, "kT", SF, rpB, 0)
        for tc in range(2):
            bk = gtile[0] % 4
            rp = gtile[0] % 6
            gtile[0] += 1
            for kc in range(4):
                bld.op("pe", lambda e, sl=sl, kc=kc, tc=tc, bk=bk: e.matmul(
                    bank(bk)[0:64, :], wquh[sl][:, kc * 64:(kc + 1) * 64],
                    q_dT[:, kc * TOK + tc * 512: kc * TOK + (tc + 1) * 512],
                    start=(kc == 0), stop=(kc == 3), tile_position=(0, 0)),
                    reads=(f"wquh{sl}", "q_dT"), writes=(f"ps{bk}",), inc=False)
                bld.op("pe", lambda e, sl=sl, kc=kc, tc=tc, bk=bk: e.matmul(
                    bank(bk)[64:128, :], wrqh[sl][:, kc * 64:(kc + 1) * 64],
                    q_dT[:, kc * TOK + tc * 512: kc * TOK + (tc + 1) * 512],
                    start=(kc == 0), stop=(kc == 3), tile_position=(0, 64)),
                    reads=(f"wrqh{sl}", "q_dT"), writes=(f"ps{bk}",), inc=(kc == 3))
            bld.op("act", lambda e, h=h, tc=tc, bk=bk, kb_=krs[rp]: e.activation(
                out=kb_[0:64, :],
                in_=bank(bk)[0:64, :], func=AF.Identity, bias=bqu_sb[0:64, h:h + 1]),
                reads=(f"ps{bk}", "bqus"), writes=(f"krs{rp}",))
            rope_a(h, tc, bk, cosk, brq_sb, "brqs", rp, ro=64)
            if rope_prev is not None:
                rope_b(*rope_prev)
            rope_prev = (h, tc, None, None, TOK, rp, 64)
    mark("P3b")
    if rope_prev is not None:
        rope_b(*rope_prev)
        rope_prev = None

    mark("P4")
    qTh = [sb(f"qTh{i}", [128, TOK], BF16, SEG_B1 + i * 2 * KB) for i in range(2)]
    qThs = [sb(f"qThs{i}", [128, TOK], BF16, SEG_B1 + 4 * KB + i * 2 * KB) for i in range(2)]
    for nm in ("qTh0", "qTh1", "qThs0", "qThs1"):
        bld.retire(["xnT"], nm)
    wvu_sb = sb("wvu", [128, 4 * 2048], BF16, SEG_B1 + 32 * KB)
    v4 = sb("v4", [128, 16 * 512], BF16, SEG_B1 + 48 * KB)
    bld.retire(["xnT"], "wvu")
    bld.retire(["xnT"], "v4")
    probsT = sb("probsT", [128, 16 * 512], BF16, SEG_D)
    probs = [sb(f"probs{i}", [128, SF], BF16, SEG_D + 16 * KB + i * 4 * KB) for i in range(2)]
    bvu_b = sb("bvub", [128, 2048], F32, SEG_D + 24 * KB)
    for rn in ("wkvc0", "wkvc1", "wqdc0", "wqdc1", "q_dT"):
        bld.retire([rn], "probsT")
        bld.retire([rn], "probs0")
        bld.retire([rn], "probs1")
        bld.retire([rn], "bvub")
    strip = [sb(f"strip{i}", [128, 512], BF16, SEG_E + 16 * KB + i * KB) for i in range(2)]
    diag = [sb(f"diag{i}", [128, 128], BF16, SEG_E + 18 * KB + i * 256) for i in range(2)]
    bld.retire(["rotb0"], "diag0")
    bld.retire(["rotb0"], "diag1")
    for rn in ("krs0", "rotb0"):
        bld.retire([rn], "strip0")
        bld.retire([rn], "strip1")
    for rn in [f"krs{i}" for i in range(6)] + [f"rotb{i}" for i in range(6)]:
        for nn in ("probsT", "probs0", "probs1", "bvub"):
            bld.retire([rn], nn)

    ldpool(wvu_sb[:, :], d["wvu"][0].rearrange("p a b -> p (a b)"), "wvu")
    ldsp(bvu_b[:, :], d["bvu"][0:1, :].to_broadcast([128, 2048]), "bvub")

    def emit_v4(hg):
        for kt in range(16):
            for kc in range(4):
                bld.op("pe", lambda e, kc=kc, kt=kt, hg=hg: e.matmul(
                    bank(7), kv_dT[:, kc * SF + kt * 128: kc * SF + (kt + 1) * 128],
                    wvu_sb[:, kc * 2048 + hg * 512: kc * 2048 + (hg + 1) * 512],
                    start=(kc == 0), stop=(kc == 3)),
                    reads=("kv_dT", "wvu"), writes=("ps7",), inc=(kc == 3))
            bld.op("dve", lambda e, kt=kt, hg=hg: e.scalar_tensor_tensor(
                out=v4[:, kt * 512:(kt + 1) * 512], in0=bank(7), scalar=1.0,
                in1=bvu_b[:, hg * 512:(hg + 1) * 512], op0=ALU.mult, op1=ALU.add),
                reads=("ps7", "bvub"), writes=("v4",))

    def emit_scores_pe(h, qi, par):
        base = 4 * par
        for kb in (2, 3, 0, 1):
            qsrc = qTh if kb % 2 == 0 else qThs
            qres = f"qTh{h % 2}" if kb % 2 == 0 else f"qThs{h % 2}"
            bld.op("pe", lambda e, h=h, qi=qi, kb=kb, base=base, qsrc=qsrc, qres=qres: e.matmul(
                bank(base + kb), qsrc[h % 2][:, qi * 128:(qi + 1) * 128],
                kT[:, h * SF + kb * 512: h * SF + (kb + 1) * 512],
                start=True, stop=True),
                reads=(qres, "kT"), writes=(f"ps{base + kb}",), inc=True)

    def emit_chain(h, qi, par):
        sc = 32 + 16 * par
        base = 4 * par
        pbanks = tuple(f"ps{base + kb}" for kb in range(4))
        bld.op("dve", lambda e, sc=sc, base=base: e.reduce_max(
            out=stat[:, sc + 5:sc + 6],
            in_=psum[:, base * 512:base * 512 + 2048].rearrange("p (a b) -> p a b", b=8)[:, :, 0],
            axis=AX.X, negate=True),
            reads=pbanks, writes=(f"nmc_{par}",))
        bld.op("dve", lambda e, sc=sc: e.tensor_scalar_mul(
            stat[:, sc + 6:sc + 7], stat[:, sc + 5:sc + 6], SCALE),
            reads=(f"nmc_{par}",), writes=(f"nms_{par}",))
        bld.op("act", lambda e, sc=sc, par=par, base=base: e.activation(
            out=probs[par][:, :], in_=psum[:, base * 512:base * 512 + 2048],
            func=AF.Exp, bias=stat[:, sc + 6:sc + 7], scale=SCALE,
            accum_out=stat[:, sc + 12:sc + 13]),
            reads=pbanks + (f"nms_{par}",),
            writes=(f"probs{par}", f"z_{par}"))
    def emit_chain_b(par):
        sc = 32 + 16 * par
        bld.op("dve", lambda e, sc=sc: e.reciprocal(
            out=stat[:, sc + 13:sc + 14], in_=stat[:, sc + 12:sc + 13]),
            reads=(f"z_{par}",), writes=(f"zi_{par}",))
        bld.op("dve", lambda e, par=par, sc=sc: e.tensor_scalar_mul(
            diag[par][:, :], ident[:, :], stat[:, sc + 13:sc + 14]),
            reads=("identb", f"zi_{par}"), writes=(f"diag{par}",))

    def emit_transposes(h, qg, qt, par):
        base = 4 * par
        for g4 in range(4):
            tb = (6 + g4) if g4 < 2 else (base + g4 - 2)
            for j in range(4):
                bld.op("pe", lambda e, par=par, g4=g4, j=j, tb=tb: e.matmul(
                    bank(tb)[:, j * 128:(j + 1) * 128],
                    probs[par][:, (g4 * 4 + j) * 128:(g4 * 4 + j + 1) * 128],
                    diag[par][:, :], start=True, stop=True),
                    reads=(f"probs{par}", f"diag{par}"), writes=(f"ps{tb}",), inc=(j == 3))
            if g4 != 3:
                bld.op("dve", lambda e, g4=g4, qt=qt, tb=tb: e.tensor_copy(
                    out=probsT[:, :].rearrange("p (c q) -> p c q", c=16)
                        [:, g4 * 4:g4 * 4 + 4, qt * 128:(qt + 1) * 128],
                    in_=bank(tb)),
                    reads=(f"ps{tb}",), writes=("probsT",))
            else:
                bld.op("act", lambda e, g4=g4, qt=qt, tb=tb: e.copy(
                    out=probsT[:, :].rearrange("p (c q) -> p c q", c=16)
                        [:, g4 * 4:g4 * 4 + 4, qt * 128:(qt + 1) * 128],
                    in_=bank(tb)),
                    reads=(f"ps{tb}",), writes=("probsT",))

    def emit_pv(h, qg):
        hh = h % 4
        for kc in range(16):
            bld.op("pe", lambda e, kc=kc, hh=hh: e.matmul(
                bank(6), v4[:, kc * 512 + hh * 128: kc * 512 + (hh + 1) * 128],
                probsT[:, kc * 512:(kc + 1) * 512],
                start=(kc == 0), stop=(kc == 15)),
                reads=("v4", "probsT"), writes=("ps6",), inc=(kc == 15))
        ssl = qg % 2
        bld.op("act", lambda e, ssl=ssl: e.copy(out=strip[ssl][:, :], in_=bank(6)),
               reads=("ps6",), writes=(f"strip{ssl}",))
        bld.dma("sp", lambda e, h=h, qg=qg, ssl=ssl: e.dma_start(
            out=attnD[qg, :, h, :], in_=strip[ssl][:, :]),
            reads=(f"strip{ssl}",), writes=(f"attnD{ssl}",), sem=f"attnDs{ssl}")

    def load_qTh(h):
        bld.dma("sp", lambda e, h=h: e.dma_start(out=qTh[h % 2][:, :], in_=qTD[h]),
                reads=("qTDr",), writes=(f"qTh{h % 2}",), sem=f"qTh{h % 2}")
        bld.dma("sp", lambda e, h=h: e.dma_start(out=qThs[h % 2][:, :], in_=qTDs[h]),
                reads=("qTDsr",), writes=(f"qThs{h % 2}",), sem=f"qThs{h % 2}")

    load_qTh(0)
    load_qTh(1)
    work = []
    for hg in range(4):
        work.append(("v4", hg))
        for hh in range(4):
            h = hg * 4 + hh
            for qg in range(2):
                for qt in range(4):
                    work.append(("qt", h, qg, qt, hh == 0 and qg == 0 and qt == 0))
    prev = None
    g_i = 0

    def drain_prev():
        nonlocal prev
        if prev is not None:
            ph, pqg, pqt, ppar = prev
            emit_chain_b(ppar)
            emit_transposes(ph, pqg, pqt, ppar)
            if pqt == 3:
                emit_pv(ph, pqg)
            prev = None

    for w in work:
        if w[0] == "v4":
            drain_prev()
            emit_v4(w[1])
            continue
        _, h, qg, qt, _hd = w
        par = g_i % 2
        g_i += 1
        emit_scores_pe(h, qg * 4 + qt, par)
        if qg == 1 and qt == 3 and h + 2 < 16:
            load_qTh(h + 2)
        emit_chain(h, qg * 4 + qt, par)
        keep = prev
        prev = None
        if keep is not None:
            ph, pqg, pqt, ppar = keep
            emit_chain_b(ppar)
            emit_transposes(ph, pqg, pqt, ppar)
            if pqt == 3:
                emit_pv(ph, pqg)
        prev = (h, qg, qt, par)
    drain_prev()

    mark("P5")
    wo_sb = sb("wo", [128, 16 * 4 * 512], BF16, SEG_B1)
    bld.retire(["qTh0", "qTh1", "qThs0", "qThs1", "wvu", "v4"], "wo")
    for eo in range(4):
        bld.retire(["qTh0", "qTh1", "qThs0", "qThs1", "wvu", "v4"], f"wo{eo}")
    tT = sb("tT", [128, 16 * TOK], BF16, SEG_B2)
    bld.retire(["kT"], "tT")
    bo_b = sb("bob", [128, 2048], F32, SEG_D)
    normw_b = sb("normwb", [128, 2048], F32, SEG_D + 16 * KB)
    for rn in ("probsT", "probs0", "probs1", "bvub") + tuple(f"krs{i}" for i in range(6)):
        bld.retire([rn], "bob")
        bld.retire([rn], "normwb")
    x1b = [sb(f"x1b{i}", [128, E], F32, SEG_B2 + 32 * KB + i * 8 * KB) for i in range(2)]
    af32 = [sb(f"af32{i}", [128, E], F32, SEG_B2 + 48 * KB + i * 8 * KB) for i in range(2)]
    for i in range(2):
        bld.retire(["kT"], f"x1b{i}")
        bld.retire(["kT"], f"af32{i}")
    xn1b = sb("xn1b", [128, E], F32, SEG_D + 8 * KB)
    tw = sb("tw", [128, E], F32, SEG_D + 24 * KB)
    for rn in ("probsT", "probs0", "probs1", "bvub") + tuple(f"krs{i}" for i in range(6)):
        for nn in ("xn1b", "tw"):
            bld.retire([rn], nn)
    attw = sb("attw", [128, 16 * 512], BF16, SEG_C)
    bld.retire(["kv_dT"], "attw")
    t32T = sb("t32T", [128, 16 * 128], F32, SEG_E + 18 * KB)
    GT = sb("GT", [16, TOK], BF16, SEG_E)
    GTp = sb("GTp", [16, TOK], F32, SEG_E + 2 * KB)
    grow = sb("grow", [1, TOK], F32, SEG_E + 6 * KB)
    for rn in ("cosk", "sink") + tuple(f"rotb{i}" for i in range(6)):
        for nn in ("GT", "GTp", "grow"):
            bld.retire([rn], nn)
    for rn in ("wrkh0", "wrkh1", "strip0", "strip1", "rotb0", "krs0",
               "bkvs", "bqds", "bkus", "bqus", "brks", "brqs",
               "wkuh0", "wkuh1", "wquh0", "wquh1", "wrqh0", "wrqh1", "diag0", "diag1"):
        bld.retire([rn], "t32T")

    for eo in range(4):
        bld.dma("pool", lambda e, eo=eo: e.dma_start(
            out=wo_sb[:, :].rearrange("p (h e n) -> p h e n", h=16, e=4)[:, :, eo, :],
            in_=d["wo"][:, :, eo, :]),
            writes=(f"wo{eo}",), sem=f"wo{eo}")
    ldsp(bo_b[:, :], d["bo"][0:1, :].to_broadcast([128, 2048]), "bob")
    ldsp(normw_b[:, :], d["normw"][0:1, :].to_broadcast([128, 2048]), "normwb")

    def p5_load(i):
        if i % 4 == 0:
            bld.dma("sp", lambda e, i=i: e.dma_start(
                out=attw[:, :].rearrange("p (h q) -> p h q", h=16),
                in_=attnD[i // 4]),
                reads=("attnD0", "attnD1"), writes=("attw",), sem="attw")
        bld.dma("sp", lambda e, i=i: e.dma_start(out=x1b[i % 2][:, :],
                                                 in_=d["xk"][i * 128:(i + 1) * 128, :]),
                writes=(f"x1b{i % 2}",), sem=f"x1b{i % 2}")

    def p5_wo_pe(i):
        for eo in range(4):
            for h in range(16):
                bld.op("pe", lambda e, i=i, eo=eo, h=h: e.matmul(
                    bank(eo), attw[:, h * 512 + (i % 4) * 128: h * 512 + (i % 4) * 128 + 128],
                    wo_sb[:, (h * 4 + eo) * 512:(h * 4 + eo + 1) * 512],
                    start=(h == 0), stop=(h == 15)),
                    reads=("attw", f"wo{eo}"), writes=(f"ps{eo}",), inc=(h == 15))

    def p5_wo_evac(i):
        for eo in range(4):
            bld.op("dve", lambda e, i=i, eo=eo: e.scalar_tensor_tensor(
                out=af32[i % 2][:, eo * 512:(eo + 1) * 512], in0=bank(eo), scalar=1.0,
                in1=bo_b[:, eo * 512:(eo + 1) * 512], op0=ALU.mult, op1=ALU.add),
                reads=(f"ps{eo}", "bob"), writes=(f"af32{i % 2}",))

    def ln(src_, srcres, dst, dstres, sN):
        for bc in range(4):
            bld.op("dve", lambda e, src_=src_, bc=bc: e.bn_stats(
                out=stat[:, 2 + 6 * bc:8 + 6 * bc], in_=src_[:, bc * 512:(bc + 1) * 512]),
                reads=(srcres,), writes=("bn6",))
        bld.op("dve", lambda e: e.bn_aggr(out=stat[:, 0:2], in_=stat[:, 2:26]),
               reads=("bn6",), writes=("bn2",))
        bld.op("dve", lambda e, sN=sN: e.tensor_scalar_mul(stat[:, sN:sN + 1], stat[:, 0:1], -1.0),
               reads=("bn2",), writes=("nm0",))
        bld.op("act", lambda e, sN=sN: e.activation(out=stat[:, sN + 1:sN + 2], in_=stat[:, 1:2],
                                                    func=AF.Ln, bias=epsb[:, 0:1]),
               reads=("bn2", "epsb"), writes=("rs0",))
        bld.op("act", lambda e, sN=sN: e.activation(out=stat[:, sN + 1:sN + 2],
                                                    in_=stat[:, sN + 1:sN + 2], func=AF.Exp, scale=-0.5),
               reads=("rs0",), writes=("rs0",))
        bld.op("dve", lambda e, src_=src_, dst=dst, sN=sN: e.tensor_scalar(
            out=dst[:, :], in0=src_[:, :], scalar1=stat[:, sN:sN + 1],
            scalar2=stat[:, sN + 1:sN + 2], op0=ALU.add, op1=ALU.mult),
            reads=(srcres, "nm0", "rs0"), writes=(dstres,))

    def p5_chain(i):
        ip = i % 2
        ln(af32[ip], f"af32{ip}", tw, "tw", 26)
        bld.op("dve", lambda e, ip=ip: e.tensor_add(x1b[ip][:, :], x1b[ip][:, :], tw[:, :]),
               reads=(f"x1b{ip}", "tw"), writes=(f"x1b{ip}",))
        bld.dma("sp", lambda e, i=i, ip=ip: e.dma_start(out=x1D[i * 128:(i + 1) * 128, :], in_=x1b[ip][:, :]),
                reads=(f"x1b{ip}",), writes=("x1D",), sem="x1Ds")
        ln(x1b[ip], f"x1b{ip}", xn1b, "xn1b", 26)
        bld.dma("sp", lambda e, i=i: e.dma_start(out=xn1D[i * 128:(i + 1) * 128, :], in_=xn1b[:, :]),
                reads=("xn1b",), writes=("xn1D",), sem="xn1Ds")
        for bc in range(4):
            bld.op("dve", lambda e, bc=bc: e.bn_stats(
                out=stat[:, 2 + 6 * bc:8 + 6 * bc], in_=xn1b[:, bc * 512:(bc + 1) * 512]),
                reads=("xn1b",), writes=("bn6",))
        bld.op("dve", lambda e: e.bn_aggr(out=stat[:, 0:2], in_=stat[:, 2:26]),
               reads=("bn6",), writes=("bn2",))
        bld.op("dve", lambda e: e.tensor_mul(stat[:, 30:31], stat[:, 0:1], stat[:, 0:1]),
               reads=("bn2",), writes=("msq",))
        bld.op("dve", lambda e: e.tensor_add(stat[:, 30:31], stat[:, 30:31], stat[:, 1:2]),
               reads=("msq", "bn2"), writes=("msq",))
        bld.op("act", lambda e: e.activation(out=stat[:, 31:32], in_=stat[:, 30:31], func=AF.Ln, bias=epsb[:, 0:1]),
               reads=("msq", "epsb"), writes=("rms",))
        bld.op("act", lambda e: e.activation(out=stat[:, 31:32], in_=stat[:, 31:32], func=AF.Exp, scale=-0.5),
               reads=("rms",), writes=("rms",))
        bld.op("dve", lambda e: e.tensor_scalar_mul(tw[:, :], xn1b[:, :], stat[:, 31:32]),
               reads=("xn1b", "rms"), writes=("tw",))
        bld.op("dve", lambda e: e.tensor_mul(tw[:, :], tw[:, :], normw_b[:, :]),
               reads=("tw", "normwb"), writes=("tw",))
        for g in range(4):
            tb = 4 + g % 2
            for j in range(4):
                bld.op("pe", lambda e, g=g, j=j, tb=tb: e.transpose(
                    bank(tb)[:, j * 128:(j + 1) * 128], tw[:, (g * 4 + j) * 128:(g * 4 + j + 1) * 128],
                    ident32[:, :]),
                    reads=("tw", "ident32"), writes=(f"ps{tb}",), inc=(j == 3))
            bld.op("dve", lambda e, g=g, tb=tb: e.tensor_copy(
                out=t32T[:, :].rearrange("p (c t) -> p c t", c=16)[:, g * 4:g * 4 + 4, 0:128],
                in_=bank(tb)),
                reads=(f"ps{tb}",), writes=("t32T",))
        bld.op("act", lambda e, i=i: e.copy(
            out=tT[:, :].rearrange("p (c t) -> p c t", c=16)[:, :, i * 128:(i + 1) * 128],
            in_=t32T[:, :].rearrange("p (c t) -> p c t", c=16)[:, :, :]),
            reads=("t32T",), writes=("tT",))
        for kc in range(16):
            bld.op("pe", lambda e, kc=kc: e.matmul(
                bank(6)[:, 0:16], t32T[:, kc * 128:(kc + 1) * 128],
                gatew_sb[:, kc * 16:(kc + 1) * 16], start=(kc == 0), stop=(kc == 15)),
                reads=("t32T", "gatew"), writes=("ps6",), inc=(kc == 15))
        bld.op("dve", lambda e: e.tensor_add(stat[:, 64:80], bank(6)[:, 0:16], gateb_b[:, :]),
               reads=("ps6", "gatebb"), writes=("lg",))
        bld.op("dve", lambda e: e.max(out=stat[:, 80:88], in_=stat[:, 64:80]),
               reads=("lg",), writes=("v8",))
        bld.op("dve", lambda e: e.reduce_max(out=stat[:, 88:89], in_=stat[:, 64:80], axis=AX.X, negate=True),
               reads=("lg",), writes=("nmg",))
        bld.op("dve", lambda e: e.tensor_scalar(out=stat[:, 96:112], in0=stat[:, 64:80],
                                                scalar1=stat[:, 83:84], scalar2=None, op0=ALU.is_ge),
               reads=("lg", "v8"), writes=("mask",))
        bld.op("act", lambda e: e.activation(out=stat[:, 112:128], in_=stat[:, 64:80],
                                             func=AF.Exp, bias=stat[:, 88:89]),
               reads=("lg", "nmg"), writes=("eg",))
        bld.op("dve", lambda e: e.tensor_mul(stat[:, 112:128], stat[:, 112:128], stat[:, 96:112]),
               reads=("eg", "mask"), writes=("eg",))
        bld.op("dve", lambda e: e.reduce_sum(out=stat[:, 89:90], in_=stat[:, 112:128], axis=AX.X),
               reads=("eg",), writes=("zg",))
        bld.op("dve", lambda e: e.reciprocal(out=stat[:, 90:91], in_=stat[:, 89:90]),
               reads=("zg",), writes=("zgi",))
        bld.op("dve", lambda e: e.tensor_scalar_mul(stat[:, 128:144], stat[:, 112:128], stat[:, 90:91]),
               reads=("eg", "zgi"), writes=("G32",))
        bld.op("dve", lambda e: e.tensor_scalar_mul(stat[:, 144:160], stat[:, 128:144], 1.0 / ALPHA),
               reads=("G32",), writes=("Gp32",))
        bld.op("dve", lambda e: e.tensor_copy(out=stat[:, 160:176].bitcast(BF16)[:, 0:16],
                                              in_=stat[:, 128:144]),
               reads=("G32",), writes=("Gbf",))
        bld.op("pe", lambda e: e.transpose(bank_bf(7)[0:16, 0:128], stat[:, 160:176].bitcast(BF16)[:, 0:16],
                                           ident[:, :]),
               reads=("Gbf", "identb"), writes=("ps7",), inc=True)
        bld.op("act", lambda e, i=i: e.copy(out=GT[0:16, i * 128:(i + 1) * 128],
                                            in_=bank_bf(7)[0:16, 0:128]),
               reads=("ps7",), writes=("GTr",))
        bld.op("pe", lambda e: e.transpose(bank(7)[0:16, 128:256], stat[:, 144:160], ident32[:, :]),
               reads=("Gp32", "ident32"), writes=("ps7",), inc=True)
        bld.op("act", lambda e, i=i: e.copy(out=GTp[0:16, i * 128:(i + 1) * 128],
                                            in_=bank(7)[0:16, 128:256]),
               reads=("ps7",), writes=("GTpr",))

    for i in range(8):
        p5_load(i)
        p5_wo_pe(i)
        if i > 0:
            p5_chain(i - 1)
        p5_wo_evac(i)
    p5_chain(7)

    mark("P6")
    delta = sb("delta", [128, 16 * TOK], F32, SEG_B1)
    bld.retire(["wo"], "delta")
    act_e = sb("act_e", [128, 16 * TOK], BF16, SEG_B2 + 32 * KB)
    bld.retire(["kT"], "act_e")
    for i in range(2):
        bld.retire([f"x1b{i}"], "act_e")
        bld.retire([f"af32{i}"], "act_e")
    w2sl = [sb(f"w2sl{i}", [128, 16 * 256], BF16, SEG_C + i * 8 * KB) for i in range(2)]
    bld.retire(["bob", "normwb"], "w2sl0")
    bld.retire(["bob", "normwb"], "w2sl1")
    w1sl = [sb(f"w1sl{i}", [128, 16 * 256], BF16, SEG_D + i * 8 * KB) for i in range(4)]
    for nn in ("xn1b", "tw"):
        for i in range(4):
            bld.retire([nn], f"w1sl{i}")
    Gbp = [sb(f"Gbp{i}", [128, TOK], F32, SEG_E + 10 * KB + i * 4 * KB) for i in range(2)]
    bld.retire(["wrkh0", "wrkh1"], "Gbp0")
    bld.retire(["wrkh0", "wrkh1"], "Gbp1")
    b2sb = sb("b2sb", [16, 2048], BF16, SEG_E + 18 * KB)
    bld.retire(["t32T"], "b2sb")
    bld.retire(["rotb0"], "b2sb")
    bld.retire(["diag0"], "b2sb")
    bld.retire(["diag1"], "b2sb")
    afs = sb("afs", [128, 512], F32, SEG_E + 18 * KB)
    for oldr in ("rotb0", "t32T"):
        for nn in ("afs", "ufs", "sfs"):
            bld.retire([oldr], nn)
    for oldr in ("wrkh0", "wrkh1", "t32T"):
        bld.retire([oldr], "b1sb0")
        bld.retire([oldr], "b1sb1")
    ufs = sb("ufs", [128, 512], F32, SEG_E + 20 * KB)
    sfs = sb("sfs", [128, 512], F32, SEG_E + 22 * KB)
    b1sb = [sb(f"b1sb{i}", [128, 32], F32, SEG_E + 24 * KB + i * 128) for i in range(2)]

    ldpool(b2sb[:, :], d["b2"][:, :], "b2sb")
    for dc in range(16):
        for tc in range(2):
            bk = 4 + (dc * 2 + tc) % 2
            bld.op("pe", lambda e, dc=dc, tc=tc, bk=bk: e.matmul(
                bank(bk), b2sb[0:16, dc * 128:(dc + 1) * 128],
                GT[0:16, tc * 512:(tc + 1) * 512], start=True, stop=True),
                reads=("b2sb", "GTr"), writes=(f"ps{bk}",), inc=True)
            bld.op("dve", lambda e, dc=dc, tc=tc, bk=bk: e.tensor_copy(
                out=delta[:, dc * TOK + tc * 512: dc * TOK + (tc + 1) * 512], in_=bank(bk)),
                reads=(f"ps{bk}",), writes=("delta",))
    bld.retire(["b2sb"], "afs")
    bld.retire(["b2sb"], "ufs")

    def w1_load(e_, si):
        slab = (si // 2) if si % 2 == 0 else (si // 2 + 8)
        slot = si % 4
        ldpool(w1sl[slot][:, :], d["w1s"][e_, slab].rearrange("p a b -> p (a b)"), f"w1sl{slot}")

    def w2_load(e_, js):
        ldpool(w2sl[js % 2][:, :], d["w2s"][e_, js].rearrange("p a b -> p (a b)"), f"w2sl{js % 2}")

    for si in range(4):
        w1_load(0, si)
    for js in range(2):
        w2_load(0, js)
    ldsp(b1sb[0][:, :], d["b1t"][0], "b1sb0")
    bld.dma("sp", lambda e: e.dma_start(out=grow[:, :], in_=GTp[0:1, :]),
            reads=("GTpr",), writes=("grow",), sem="grow")

    for ex in range(16):
        exn = ex % 2
        for tc in range(2):
            bk = 6 + tc
            bld.op("pe", lambda e, tc=tc, bk=bk: e.matmul(
                bank(bk), ones32[0:1, :], grow[0:1, tc * 512:(tc + 1) * 512],
                start=True, stop=True),
                reads=("ones32", "grow"), writes=(f"ps{bk}",), inc=True)
            bld.op("dve", lambda e, tc=tc, bk=bk, exn=exn: e.tensor_copy(
                out=Gbp[exn][:, tc * 512:(tc + 1) * 512], in_=bank(bk)),
                reads=(f"ps{bk}",), writes=(f"Gbp{exn}",))
        if ex + 1 < 16:
            bld.dma("sp", lambda e, ex=ex: e.dma_start(out=grow[:, :], in_=GTp[ex + 1:ex + 2, :]),
                    reads=("GTpr",), writes=("grow",), sem="grow")
            ldsp(b1sb[1 - exn][:, :], d["b1t"][ex + 1], f"b1sb{1 - exn}")

        for j in range(8):
            slA = (2 * j) % 4
            slB = (2 * j + 1) % 4
            for part in range(2):
                for tc in range(2):
                    u = part * 2 + tc
                    bkA = u % 2
                    bkB = 2 + u % 2
                    ic = j * 2 + part
                    for kc in range(16):
                        bld.op("pe", lambda e, slA=slA, kc=kc, part=part, tc=tc, bkA=bkA: e.matmul(
                            bank(bkA), w1sl[slA][:, kc * 256 + part * 128: kc * 256 + part * 128 + 128],
                            tT[:, kc * TOK + tc * 512: kc * TOK + (tc + 1) * 512],
                            start=(kc == 0), stop=(kc == 15)),
                            reads=(f"w1sl{slA}", "tT"), writes=(f"ps{bkA}",), inc=(kc == 15))
                    for kc in range(16):
                        bld.op("pe", lambda e, slB=slB, kc=kc, part=part, tc=tc, bkB=bkB: e.matmul(
                            bank(bkB), w1sl[slB][:, kc * 256 + part * 128: kc * 256 + part * 128 + 128],
                            tT[:, kc * TOK + tc * 512: kc * TOK + (tc + 1) * 512],
                            start=(kc == 0), stop=(kc == 15)),
                            reads=(f"w1sl{slB}", "tT"), writes=(f"ps{bkB}",), inc=(kc == 15))
                    bld.op("dve", lambda e, bkA=bkA, exn=exn, j=j, part=part: e.tensor_scalar(
                        out=afs[:, :], in0=bank(bkA), scalar1=b1sb[exn][:, 2 * j + part: 2 * j + part + 1],
                        scalar2=LIMIT, op0=ALU.add, op1=ALU.min),
                        reads=(f"ps{bkA}", f"b1sb{exn}"), writes=("afs",))
                    bld.op("dve", lambda e, bkB=bkB, exn=exn, j=j, part=part: e.tensor_scalar(
                        out=ufs[:, :], in0=bank(bkB),
                        scalar1=b1sb[exn][:, 16 + 2 * j + part: 16 + 2 * j + part + 1],
                        scalar2=-LIMIT, op0=ALU.add, op1=ALU.max),
                        reads=(f"ps{bkB}", f"b1sb{exn}"), writes=("ufs",))
                    bld.op("dve", lambda e: e.tensor_scalar(
                        out=ufs[:, :], in0=ufs[:, :], scalar1=LIMIT, scalar2=1.0,
                        op0=ALU.min, op1=ALU.add),
                        reads=("ufs",), writes=("ufs",))
                    bld.op("act", lambda e: e.activation(out=sfs[:, :], in_=afs[:, :],
                                                         func=AF.Silu, scale=ALPHA),
                           reads=("afs",), writes=("sfs",))
                    bld.op("dve", lambda e: e.tensor_mul(sfs[:, :], sfs[:, :], ufs[:, :]),
                           reads=("sfs", "ufs"), writes=("sfs",))
                    bld.op("dve", lambda e, ic=ic, tc=tc, exn=exn: e.tensor_mul(
                        act_e[:, ic * TOK + tc * 512: ic * TOK + (tc + 1) * 512],
                        sfs[:, :], Gbp[exn][:, tc * 512:(tc + 1) * 512]),
                        reads=("sfs", f"Gbp{exn}"), writes=("act_e",))
            for s_ in (2 * j + 4, 2 * j + 5):
                if s_ < 16:
                    w1_load(ex, s_)
                elif ex + 1 < 16:
                    w1_load(ex + 1, s_ - 16)
        for js in range(8):
            for part in range(2):
                for tc in range(2):
                    u = part * 2 + tc
                    bk = 4 + u % 2
                    dc = js * 2 + part
                    for kc in range(16):
                        bld.op("pe", lambda e, js=js, kc=kc, part=part, tc=tc, bk=bk: e.matmul(
                            bank(bk), w2sl[js % 2][:, kc * 256 + part * 128: kc * 256 + part * 128 + 128],
                            act_e[:, kc * TOK + tc * 512: kc * TOK + (tc + 1) * 512],
                            start=(kc == 0), stop=(kc == 15)),
                            reads=(f"w2sl{js % 2}", "act_e"), writes=(f"ps{bk}",), inc=(kc == 15))
                    bld.op("dve", lambda e, dc=dc, tc=tc, bk=bk: e.tensor_tensor(
                        out=delta[:, dc * TOK + tc * 512: dc * TOK + (tc + 1) * 512],
                        in0=bank(bk),
                        in1=delta[:, dc * TOK + tc * 512: dc * TOK + (tc + 1) * 512],
                        op=ALU.add),
                        reads=(f"ps{bk}", "delta"), writes=("delta",))
            s_ = js + 2
            if s_ < 8:
                w2_load(ex, s_)
            elif ex + 1 < 16:
                w2_load(ex + 1, s_ - 8)

    mark("P7")
    x1r = [sb(f"x1r{i}", [128, E], F32, SEG_D + i * 8 * KB) for i in range(2)]
    xn1r = [sb(f"xn1r{i}", [128, E], F32, SEG_D + 16 * KB + i * 8 * KB) for i in range(2)]
    ffn = [sb(f"ffn{i}", [128, E], F32, SEG_B2 + i * 8 * KB) for i in range(2)]
    for i in range(2):
        for rn in ("bob", "normwb", "tw", "xn1b", "w1sl0", "w1sl1", "w1sl2", "w1sl3"):
            bld.retire([rn], f"x1r{i}")
            bld.retire([rn], f"xn1r{i}")
        bld.retire(["tT"], f"ffn{i}")

    def p7_load(i):
        ip = i % 2
        bld.dma("sp", lambda e, i=i, ip=ip: e.dma_start(out=x1r[ip][:, :], in_=x1D[i * 128:(i + 1) * 128, :]),
                reads=("x1D",), writes=(f"x1r{ip}",), sem=f"x1r{ip}")
        bld.dma("sp", lambda e, i=i, ip=ip: e.dma_start(out=xn1r[ip][:, :], in_=xn1D[i * 128:(i + 1) * 128, :]),
                reads=("xn1D",), writes=(f"xn1r{ip}",), sem=f"xn1r{ip}")

    def p7_trans(i):
        ip = i % 2
        for g in range(4):
            bk = 4 * ip + g
            for j in range(4):
                dc = g * 4 + j
                bld.op("pe", lambda e, dc=dc, i=i, bk=bk, j=j: e.transpose(
                    bank(bk)[:, j * 128:(j + 1) * 128],
                    delta[:, dc * TOK + i * 128: dc * TOK + (i + 1) * 128], ident32[:, :]),
                    reads=("delta", "ident32"), writes=(f"ps{bk}",), inc=(j == 3))

    def p7_chain(i):
        ip = i % 2
        for g in range(4):
            bk = 4 * ip + g
            bld.op("dve", lambda e, g=g, bk=bk, ip=ip: e.tensor_add(
                ffn[ip][:, g * 512:(g + 1) * 512], bank(bk), xn1r[ip][:, g * 512:(g + 1) * 512]),
                reads=(f"ps{bk}", f"xn1r{ip}"), writes=(f"ffn{ip}",))
        for bc in range(4):
            bld.op("dve", lambda e, bc=bc, ip=ip: e.bn_stats(
                out=stat[:, 2 + 6 * bc:8 + 6 * bc], in_=ffn[ip][:, bc * 512:(bc + 1) * 512]),
                reads=(f"ffn{ip}",), writes=("bn6",))
        bld.op("dve", lambda e: e.bn_aggr(out=stat[:, 0:2], in_=stat[:, 2:26]),
               reads=("bn6",), writes=("bn2",))
        bld.op("dve", lambda e: e.tensor_scalar_mul(stat[:, 26:27], stat[:, 0:1], -1.0),
               reads=("bn2",), writes=("nm0",))
        bld.op("act", lambda e: e.activation(out=stat[:, 27:28], in_=stat[:, 1:2], func=AF.Ln, bias=epsb[:, 0:1]),
               reads=("bn2", "epsb"), writes=("rs0",))
        bld.op("act", lambda e: e.activation(out=stat[:, 27:28], in_=stat[:, 27:28], func=AF.Exp, scale=-0.5),
               reads=("rs0",), writes=("rs0",))
        bld.op("dve", lambda e, ip=ip: e.tensor_scalar(
            out=ffn[ip][:, :], in0=ffn[ip][:, :], scalar1=stat[:, 26:27], scalar2=stat[:, 27:28],
            op0=ALU.add, op1=ALU.mult),
            reads=(f"ffn{ip}", "nm0", "rs0"), writes=(f"ffn{ip}",))
        bld.op("dve", lambda e, ip=ip: e.tensor_add(x1r[ip][:, :], x1r[ip][:, :], ffn[ip][:, :]),
               reads=(f"x1r{ip}", f"ffn{ip}"), writes=(f"x1r{ip}",))
        bld.dma("sp", lambda e, i=i, ip=ip: e.dma_start(out=out[i * 128:(i + 1) * 128, :], in_=x1r[ip][:, :]),
                reads=(f"x1r{ip}",), writes=("outD",), sem="outD")

    for i in range(8):
        p7_load(i)
        p7_trans(i)
        if i > 0:
            p7_chain(i - 1)
    p7_chain(7)
    bld.barrier("sp", ["outD"])

    mark("END")
    blk = nc.Block().__enter__()
    bld.finalize(blk)
    blk.__exit__(None, None, None)
    return nc


def kernel(**inputs):
    per_core = host_prep(inputs)
    nc = build_program()
    res = run_bass_kernel_spmd(nc, per_core, core_ids=list(range(8)))
    outp = np.empty((B, S, E), np.float32)
    for c in range(8):
        b, hf = c // 2, c % 2
        outp[b, hf * 1024:(hf + 1) * 1024] = res.results[c]["out"]
    return outp


# version 20
# speedup vs baseline: 1.1141x; 1.0725x over previous
import math

import numpy as np

import concourse.bass as bass
import concourse.mybir as mybir
from concourse.bass_utils import run_bass_kernel_spmd

F32 = mybir.dt.float32
BF16 = mybir.dt.bfloat16
AF = mybir.ActivationFunctionType
ALU = mybir.AluOpType
AX = mybir.AxisListType

B, S, E = 4, 2048, 2048
H, LAT, D, R = 16, 512, 128, 64
NEXP, II = 16, 2048
TOK = 1024
SF = 2048
EPS = 1e-5
ALPHA = 1.702
LIMIT = 7.0
SCALE = 1.0 / math.sqrt(D)

SBUF_BASE = 16512
SBUF_TOP = 229344
KB = 1024

SEG_A = SBUF_BASE
SEG_B1 = SEG_A + 4 * KB
SEG_B2 = SEG_B1 + 64 * KB
SEG_C = SEG_B2 + 64 * KB
SEG_D = SEG_C + 16 * KB
SEG_E = SEG_D + 32 * KB


class Builder:

    ENGS = ("pe", "dve", "act", "pool", "sp")

    def __init__(self, nc):
        self.nc = nc
        self.stream = {e: [] for e in self.ENGS}
        self.sems = {}
        self.counts = {}
        self._sem_guards = []
        for name in self.ENGS:
            self._new_sem(name)
        self.final_cells = {}
        self.res = {}
        self.waited = {e: {} for e in self.ENGS}
        self.pe_pending = []

    def _new_sem(self, name):
        cm = self.nc.semaphore(name + "_sem")
        self.sems[name] = cm.__enter__()
        self._sem_guards.append(cm)
        self.counts[name] = 0

    def _r(self, name):
        return self.res.setdefault(name, {"w": [], "r": []})

    def _emit_wait(self, eng, ev):
        sem, val = ev
        wm = self.waited[eng]
        s = self.sems[sem]
        if isinstance(val, list):
            if wm.get(sem) == "final":
                return
            wm[sem] = "final"
            self.stream[eng].append(lambda e, s=s, val=val: e.wait_ge(s, val[0]))
            return
        cur = wm.get(sem, 0)
        if cur == "final" or (cur != "final" and cur >= val):
            return
        wm[sem] = val
        self.stream[eng].append(lambda e, s=s, val=val: e.wait_ge(s, val))

    def _deps(self, eng, reads, writes):
        best = {}
        cells = []
        for rn in reads:
            r = self._r(rn)
            for ev in r["w"]:
                (cells.append(ev) if isinstance(ev[1], list) else
                 best.__setitem__(ev[0], max(best.get(ev[0], 0), ev[1])))
        for wn in writes:
            r = self._r(wn)
            for ev in r["w"] + r["r"]:
                (cells.append(ev) if isinstance(ev[1], list) else
                 best.__setitem__(ev[0], max(best.get(ev[0], 0), ev[1])))
        for ev in cells:
            self._emit_wait(eng, ev)
        for sem, val in best.items():
            self._emit_wait(eng, (sem, val))

    def _record(self, reads, writes, ev):
        for rn in reads:
            self._r(rn)["r"].append(ev)
        for wn in writes:
            r = self._r(wn)
            r["w"] = [ev]
            r["r"] = []

    def op(self, eng, emit, reads=(), writes=(), inc=True):
        self._deps(eng, reads, writes)
        if eng == "pe" and not inc:
            self.pe_pending.append((tuple(reads), tuple(writes)))
            self.stream[eng].append(emit)
            return
        self.counts[eng] += 1
        val = self.counts[eng]
        s = self.sems[eng]

        def wrapped(e, emit=emit, s=s):
            emit(e).then_inc(s, 1)
        self.stream[eng].append(wrapped)
        ev = (eng, val)
        self._record(reads, writes, ev)
        if eng == "pe":
            for rds, wrs in self.pe_pending:
                self._record(rds, wrs, ev)
            self.pe_pending = []

    def dma(self, eng, emit, reads=(), writes=(), sem=None, final=False):
        assert eng in ("sp", "pool") and sem is not None
        if sem not in self.sems:
            self._new_sem(sem)
            if final:
                self.final_cells[sem] = [None]
        self._deps(eng, reads, writes)
        self.counts[sem] += 16
        val = self.counts[sem]
        s = self.sems[sem]

        def wrapped(e, emit=emit, s=s):
            emit(e).then_inc(s, 16)
        self.stream[eng].append(wrapped)
        ev = (sem, self.final_cells[sem]) if final else (sem, val)
        if final:
            assert sem in self.final_cells
        self._record(reads, writes, ev)

    def record_extra_write(self, name, ev):
        self._r(name)["w"].append(ev)

    def barrier(self, eng, names):
        for rn in names:
            r = self._r(rn)
            for ev in r["w"]:
                self._emit_wait(eng, ev)
            for ev in r["r"]:
                self._emit_wait(eng, ev)

    def retire(self, old_names, new_name):
        rn = self._r(new_name)
        for old in old_names:
            ro = self._r(old)
            rn["r"].extend(ro["w"])
            rn["r"].extend(ro["r"])

    def finalize(self, blk):
        for sem, cell in self.final_cells.items():
            cell[0] = self.counts[sem]
        fns = {"pe": blk.tensor, "dve": blk.vector, "act": blk.scalar,
               "pool": blk.gpsimd, "sp": blk.sync}
        self.id_base = {}
        for eng in self.ENGS:
            ops = self.stream[eng]
            if not ops:
                continue
            self.id_base[eng] = self.nc.next_id()

            def body(e, ops=ops):
                for f in ops:
                    f(e)
            fns[eng](body)
        for sem, c in self.counts.items():
            assert c < 60000, (sem, c)



def _slab_pack(wT, nslab, slabw):
    Kin, Mout = wT.shape
    kc = Kin // 128
    assert Mout == nslab * slabw
    return np.ascontiguousarray(wT.reshape(kc, 128, nslab, slabw).transpose(2, 1, 0, 3))


def host_prep(inp):
    f32 = np.float32
    T = lambda a: np.ascontiguousarray(np.asarray(a, dtype=f32).T)
    out = {}
    out["wkv"] = _slab_pack(T(inp["L0_attn_Wkv_d"]), 1, 512)
    out["wqd"] = _slab_pack(T(inp["L0_attn_Wq_d"]), 1, 512)
    out["wku"] = _slab_pack(T(inp["L0_attn_Wk_u"]), 16, 64)
    out["wqu"] = _slab_pack(T(inp["L0_attn_Wq_u"]), 16, 64)
    out["wrk"] = _slab_pack(T(inp["L0_attn_Wrk"]), 16, 64)
    out["wrq"] = _slab_pack(T(inp["L0_attn_Wrq"]), 16, 64)
    out["wvu"] = _slab_pack(T(inp["L0_attn_Wv_u"]), 1, 2048)
    wo = T(inp["L0_attn_Wo"])
    out["wo"] = np.ascontiguousarray(wo.reshape(16, 128, 4, 512).transpose(1, 0, 2, 3))

    A = lambda k: np.asarray(inp[k], dtype=f32)
    out["bkv"] = np.ascontiguousarray(A("L0_attn_bkv_d").reshape(4, 128).T)
    out["bqd"] = np.ascontiguousarray(A("L0_attn_bq_d").reshape(4, 128).T)
    out["bku"] = np.ascontiguousarray(np.tile(A("L0_attn_bk_u").reshape(16, 64).T, (2, 1)))
    out["bqu"] = np.ascontiguousarray(A("L0_attn_bq_u").reshape(16, 64).T)
    out["brk"] = np.ascontiguousarray(np.tile(A("L0_attn_brk").reshape(16, 64).T, (2, 1)))
    brq = np.zeros((128, 16), f32)
    brq[64:] = A("L0_attn_brq").reshape(16, 64).T
    out["brq"] = brq
    out["bvu"] = A("L0_attn_bv_u").reshape(1, 2048)
    out["bo"] = A("L0_attn_bo").reshape(1, 2048)
    out["normw"] = A("L0_ffn_norm_w").reshape(1, 2048)
    out["gateb"] = A("L0_ffn_gate_b").reshape(1, 16)
    out["gatew"] = _slab_pack(T(inp["L0_ffn_gate_w"]), 1, 16)

    w1 = np.asarray(inp["L0_ffn_w1"], dtype=f32)
    perm = np.concatenate([np.arange(0, 2 * II, 2), np.arange(1, 2 * II, 2)])
    w1p = w1[:, perm, :]
    w1s = np.empty((16, 16, 128, 16, 256), f32)
    for e in range(16):
        w1s[e] = _slab_pack(np.ascontiguousarray(w1p[e].T), 16, 256)
    out["w1s"] = w1s
    out["b1t"] = np.ascontiguousarray(
        np.asarray(inp["L0_ffn_b1"], dtype=f32)[:, perm].reshape(16, 32, 128).transpose(0, 2, 1))
    w2 = np.asarray(inp["L0_ffn_w2"], dtype=f32)
    w2s = np.empty((16, 8, 128, 16, 256), f32)
    for e in range(16):
        w2s[e] = _slab_pack(np.ascontiguousarray(w2[e].T), 8, 256)
    out["w2s"] = w2s
    out["b2"] = np.asarray(inp["L0_ffn_b2"], dtype=f32)

    out["ident"] = np.eye(128, dtype=f32)
    out["ones"] = np.ones((1, 128), f32)

    inv_freq = 1.0 / (10000.0 ** (np.arange(0, R, 2, dtype=np.float64) / R))
    t64 = np.arange(S, dtype=np.float64)
    emb = np.concatenate([np.outer(t64, inv_freq)] * 2, axis=1)
    cos = np.cos(emb).astype(f32)
    sin_eff = np.sin(emb).astype(f32)
    sin_eff[:, :32] *= -1.0

    x = np.asarray(inp["L0_attn_x"], dtype=f32)
    per_core = []
    for c in range(8):
        b, hf = c // 2, c % 2
        own = slice(hf * 1024, (hf + 1) * 1024)
        oth = slice((1 - hf) * 1024, (2 - hf) * 1024)
        xk = np.concatenate([x[b, own], x[b, oth]], axis=0)
        order = np.concatenate([np.arange(hf * 1024, (hf + 1) * 1024),
                                np.arange((1 - hf) * 1024, (2 - hf) * 1024)])
        d = dict(out)
        d["xk"] = np.ascontiguousarray(xk)
        import ml_dtypes as _mld
        d["cosk"] = np.ascontiguousarray(np.tile(cos[order].T, (2, 1))).astype(_mld.bfloat16)
        d["sink"] = np.ascontiguousarray(np.tile(sin_eff[order].T, (2, 1))).astype(_mld.bfloat16)
        per_core.append(d)
    return per_core



def build_program():
    nc = bass.Bass("TRN2", target_bir_lowering=False, debug=False)
    gi = lambda name, shape: nc.dram_tensor(name, list(shape), F32, kind="ExternalInput").ap()
    d = {}
    d["xk"] = gi("xk", (SF, E))
    d["cosk"] = nc.dram_tensor("cosk", [128, SF], BF16, kind="ExternalInput").ap()
    d["sink"] = nc.dram_tensor("sink", [128, SF], BF16, kind="ExternalInput").ap()
    for n, shp in [
                   ("wkv", (1, 128, 16, 512)), ("wqd", (1, 128, 16, 512)),
                   ("wku", (16, 128, 4, 64)), ("wqu", (16, 128, 4, 64)),
                   ("wrk", (16, 128, 16, 64)), ("wrq", (16, 128, 4, 64)),
                   ("wvu", (1, 128, 4, 2048)), ("wo", (128, 16, 4, 512)),
                   ("bkv", (128, 4)), ("bqd", (128, 4)), ("bku", (128, 16)),
                   ("bqu", (64, 16)), ("brk", (128, 16)), ("brq", (128, 16)),
                   ("bvu", (1, 2048)), ("bo", (1, 2048)), ("normw", (1, 2048)),
                   ("gateb", (1, 16)), ("gatew", (1, 128, 16, 16)),
                   ("b1t", (16, 128, 32)), ("b2", (16, 2048)),
                   ("ident", (128, 128)), ("ones", (1, 128)),
                   ("w1s", (16, 16, 128, 16, 256)), ("w2s", (16, 8, 128, 16, 256))]:
        d[n] = gi(n, shp)

    out = nc.dram_tensor("out", [TOK, E], F32, kind="ExternalOutput").ap()
    x1D = nc.dram_tensor("x1_scr", [TOK, E], F32, kind="Internal").ap()
    xn1D = nc.dram_tensor("xn1_scr", [TOK, E], F32, kind="Internal").ap()
    attnD = nc.dram_tensor("attn_scr", [2, 128, 16, 512], BF16, kind="Internal").ap()
    qTD = nc.dram_tensor("qt_scr", [16, 128, TOK], BF16, kind="Internal").ap()
    qTDs = nc.dram_tensor("qts_scr", [16, 128, TOK], BF16, kind="Internal").ap()

    bld = Builder(nc)
    build_program.bld = bld
    build_program.phases = phases = []
    def mark(tag):
        phases.append((tag, {e: len(bld.stream[e]) for e in bld.ENGS}))
    psum = nc.alloc_psum_tensor([128, 4096], F32)
    bank = lambda i: psum[:, i * 512:(i + 1) * 512]
    bank_bf = lambda i: psum[:, i * 512:(i + 1) * 512].bitcast(BF16)

    nalloc = [0]

    def sb(name, shape, dtype, offset):
        nbytes = int(np.prod(shape[1:])) * mybir.dt.size(dtype)
        assert SBUF_BASE <= offset and offset + nbytes <= SBUF_TOP, (name, offset, nbytes)
        nalloc[0] += 1
        return nc.alloc_sbuf_tensor_at(f"{name}", shape=shape, dtype=dtype, offset=offset)

    ident = sb("identb", [128, 128], BF16, SEG_A)
    ident32 = sb("ident32", [128, 128], F32, SEG_A + 256)
    ones32 = sb("ones32", [1, 128], F32, SEG_A + 768)
    gatew_sb = sb("gatew", [128, 16 * 16], F32, SEG_A + 1280)
    gateb_b = sb("gatebb", [128, 16], F32, SEG_A + 2304)
    stat = sb("stat", [128, 384], F32, SEG_A + 2432)
    epsb = sb("epsb", [128, 1], F32, SEG_A + 4032)


    xnT = sb("xnT", [128, 16 * SF], BF16, SEG_B1)
    xw = [sb(f"xw{i}", [128, E], F32, SEG_B2 + i * 8 * KB) for i in range(2)]
    xnb = [sb(f"xnb{i}", [128, E], BF16, SEG_B2 + 16 * KB + i * 4 * KB) for i in range(2)]

    cosk = sb("coskb", [128, SF], BF16, SEG_E)
    sink = sb("sinkb", [128, SF], BF16, SEG_E + 4 * KB)
    krs = [sb(f"krs{i}", [128, 512], BF16, SEG_D + 24 * KB + i * KB) for i in range(6)]
    rotb = [sb(f"rotb{i}", [128, 512], BF16, SEG_E + 8 * KB + i * KB) for i in range(6)]
    bkv_sb = sb("bkvs", [128, 4], F32, SEG_E + 20 * KB)
    bqd_sb = sb("bqds", [128, 4], F32, SEG_E + 20 * KB + 32)
    bku_sb = sb("bkus", [128, 16], F32, SEG_E + 20 * KB + 64)
    bqu_sb = sb("bqus", [64, 16], F32, SEG_E + 20 * KB + 128)
    brk_sb = sb("brks", [128, 16], F32, SEG_E + 20 * KB + 192)
    brq_sb = sb("brqs", [128, 16], F32, SEG_E + 20 * KB + 256)
    wkuh = [sb(f"wkuh{i}", [128, 4 * 64], BF16, SEG_E + 20 * KB + 320 + i * 512) for i in range(2)]
    wquh = [sb(f"wquh{i}", [128, 4 * 64], BF16, SEG_E + 20 * KB + 1344 + i * 512) for i in range(2)]
    wrqh = [sb(f"wrqh{i}", [128, 4 * 64], BF16, SEG_E + 20 * KB + 2368 + i * 512) for i in range(2)]
    wrkh = [sb(f"wrkh{i}", [128, 16 * 64], BF16, SEG_E + 20 * KB + 3392 + i * 2 * KB) for i in range(2)]

    kv_dT = sb("kv_dT", [128, 4 * SF], BF16, SEG_C)
    wkvc = [sb(f"wkvc{i}", [128, 16 * 128], BF16, SEG_D + i * 4 * KB) for i in range(2)]
    wqdc = [sb(f"wqdc{i}", [128, 16 * 128], BF16, SEG_D + 8 * KB + i * 4 * KB) for i in range(2)]
    q_dT = sb("q_dT", [128, 4 * TOK], BF16, SEG_D + 16 * KB)

    def ldsp(dst, src, res, sem=None, final=False):
        bld.dma("sp", lambda e, dst=dst, src=src: e.dma_start(out=dst, in_=src),
                writes=(res,), sem=sem or res, final=final)

    def ldpool(dst, src, res, sem=None, final=False):
        bld.dma("pool", lambda e, dst=dst, src=src: e.dma_start(out=dst, in_=src),
                writes=(res,), sem=sem or res, final=final)

    ldpool(ident[:, :], d["ident"][:, :], "identb")
    ldsp(ident32[:, :], d["ident"][:, :], "ident32", sem="csp", final=True)
    ldsp(ones32[:, :], d["ones"][:, :], "ones32", sem="csp", final=True)
    ldsp(gatew_sb[:, :], d["gatew"][0].rearrange("p a b -> p (a b)"), "gatew", sem="csp", final=True)
    ldsp(gateb_b[:, :], d["gateb"][0:1, :].to_broadcast([128, 16]), "gatebb", sem="csp", final=True)
    ldsp(cosk[:, :], d["cosk"][:, :], "cosk", sem="csp", final=True)
    ldsp(sink[:, :], d["sink"][:, :], "sink", sem="csp", final=True)
    for nm, dst in [("bkv", bkv_sb), ("bqd", bqd_sb), ("bku", bku_sb),
                    ("bqu", bqu_sb), ("brk", brk_sb), ("brq", brq_sb)]:
        ldsp(dst[:, :], d[nm][:, :], nm + "s", sem="csp", final=True)
    ldpool(wkvc[0][:, :].rearrange("p (a b) -> p a b", a=16), d["wkv"][0, :, :, 0:128], "wkvc0")
    ldpool(wqdc[0][:, :].rearrange("p (a b) -> p a b", a=16), d["wqd"][0, :, :, 0:128], "wqdc0")

    bld.op("dve", lambda e: e.memset(epsb[:, :], EPS), writes=("epsb",))

    mark("P1")
    def p1_chain(kt):
        p = kt % 2
        xb = xw[p]
        xr = f"xw{p}"
        bld.dma("sp", lambda e, xb=xb, kt=kt: e.dma_start(out=xb[:, :], in_=d["xk"][kt * 128:(kt + 1) * 128, :]),
                writes=(xr,), sem=xr)
        sN = 26 + 2 * p
        for bc in range(4):
            bld.op("dve", lambda e, xb=xb, bc=bc: e.bn_stats(
                out=stat[:, 2 + 6 * bc:8 + 6 * bc], in_=xb[:, bc * 512:(bc + 1) * 512]),
                reads=(xr,), writes=("bn6",))
        bld.op("dve", lambda e: e.bn_aggr(out=stat[:, 0:2], in_=stat[:, 2:26]),
               reads=("bn6",), writes=("bn2",))
        bld.op("dve", lambda e, sN=sN: e.tensor_scalar_mul(stat[:, sN:sN + 1], stat[:, 0:1], -1.0),
               reads=("bn2",), writes=(f"nm{p}",))
        bld.op("act", lambda e, sN=sN: e.activation(out=stat[:, sN + 1:sN + 2], in_=stat[:, 1:2],
                                                    func=AF.Ln, bias=epsb[:, 0:1]),
               reads=("bn2", "epsb"), writes=(f"rs{p}",))
        bld.op("act", lambda e, sN=sN: e.activation(out=stat[:, sN + 1:sN + 2], in_=stat[:, sN + 1:sN + 2],
                                                    func=AF.Exp, scale=-0.5),
               reads=(f"rs{p}",), writes=(f"rs{p}",))
        bld.op("dve", lambda e, xb=xb, p=p, sN=sN: e.tensor_scalar(
            out=xnb[p][:, :], in0=xb[:, :], scalar1=stat[:, sN:sN + 1],
            scalar2=stat[:, sN + 1:sN + 2], op0=ALU.add, op1=ALU.mult),
            reads=(xr, f"nm{p}", f"rs{p}"), writes=(f"xnb{p}",))

    def p1_trans(kt):
        p = kt % 2
        for g in range(4):
            bk = 2 * p + g % 2
            for j in range(4):
                ec = g * 4 + j
                bld.op("pe", lambda e, p=p, ec=ec, bk=bk, j=j: e.transpose(
                    bank_bf(bk)[:, j * 128:(j + 1) * 128],
                    xnb[p][:, ec * 128:(ec + 1) * 128], ident[:, :]),
                    reads=(f"xnb{p}", "identb"), writes=(f"ps{bk}",), inc=(j == 3))
            bld.op("act", lambda e, g=g, bk=bk, kt=kt: e.copy(
                out=xnT[:, :].rearrange("p (c t) -> p c t", c=16)[:, g * 4:g * 4 + 4, kt * 128:(kt + 1) * 128],
                in_=bank_bf(bk)[:, 0:512]),
                reads=(f"ps{bk}",), writes=("xnT",))

    for kt in range(16):
        p1_chain(kt)
        if kt > 0:
            p1_trans(kt - 1)
    p1_trans(15)

    mark("P2")
    ldpool(wkvc[1][:, :].rearrange("p (a b) -> p a b", a=16), d["wkv"][0, :, :, 128:256], "wkvc1")
    ldpool(wqdc[1][:, :].rearrange("p (a b) -> p a b", a=16), d["wqd"][0, :, :, 128:256], "wqdc1")
    for mc in range(4):
        sl = mc % 2
        for tc in range(4):
            bk = tc % 2
            for kc in range(16):
                bld.op("pe", lambda e, sl=sl, kc=kc, tc=tc, bk=bk: e.matmul(
                    bank(bk), wkvc[sl][:, kc * 128:(kc + 1) * 128],
                    xnT[:, kc * SF + tc * 512: kc * SF + (tc + 1) * 512],
                    start=(kc == 0), stop=(kc == 15)),
                    reads=(f"wkvc{sl}", "xnT"), writes=(f"ps{bk}",), inc=(kc == 15))
            bld.op("dve", lambda e, mc=mc, tc=tc, bk=bk: e.tensor_scalar(
                out=kv_dT[:, mc * SF + tc * 512: mc * SF + (tc + 1) * 512],
                in0=bank(bk), scalar1=bkv_sb[:, mc:mc + 1], scalar2=None, op0=ALU.add),
                reads=(f"ps{bk}", "bkvs"), writes=("kv_dT",))
        for tc in range(2):
            bk = 2 + tc % 2
            for kc in range(16):
                bld.op("pe", lambda e, sl=sl, kc=kc, tc=tc, bk=bk: e.matmul(
                    bank(bk), wqdc[sl][:, kc * 128:(kc + 1) * 128],
                    xnT[:, kc * SF + tc * 512: kc * SF + (tc + 1) * 512],
                    start=(kc == 0), stop=(kc == 15)),
                    reads=(f"wqdc{sl}", "xnT"), writes=(f"ps{bk}",), inc=(kc == 15))
            bld.op("dve", lambda e, mc=mc, tc=tc, bk=bk: e.tensor_scalar(
                out=q_dT[:, mc * TOK + tc * 512: mc * TOK + (tc + 1) * 512],
                in0=bank(bk), scalar1=bqd_sb[:, mc:mc + 1], scalar2=None, op0=ALU.add),
                reads=(f"ps{bk}", "bqds"), writes=("q_dT",))
        if mc + 2 < 4:
            ldpool(wkvc[sl][:, :].rearrange("p (a b) -> p a b", a=16),
                   d["wkv"][0, :, :, (mc + 2) * 128:(mc + 3) * 128], f"wkvc{sl}")
            ldpool(wqdc[sl][:, :].rearrange("p (a b) -> p a b", a=16),
                   d["wqd"][0, :, :, (mc + 2) * 128:(mc + 3) * 128], f"wqdc{sl}")

    mark("P3a")
    kT = sb("kT", [128, 16 * SF], BF16, SEG_B2)
    bld.retire(["xw0", "xw1", "xnb0", "xnb1"], "kT")
    ldpool(wrkh[0][:, :], d["wrk"][0].rearrange("p a b -> p (a b)"), "wrkh0")
    ldpool(wkuh[0][:, :], d["wku"][0].rearrange("p a b -> p (a b)"), "wkuh0")
    rope_prev = None

    def rope_a(h, tc, bk, costab, brs, brsname, rp, ro=64):
        kb_, rb_ = krs[rp], rotb[rp]
        kr_, rr_ = f"krs{rp}", f"rotb{rp}"
        bld.op("dve", lambda e, bk=bk, h=h, ro=ro: e.tensor_scalar(
            out=kb_[ro:ro + 64, :], in0=bank(bk)[ro:ro + 64, :],
            scalar1=brs[ro:ro + 64, h:h + 1], scalar2=None, op0=ALU.add),
            reads=(f"ps{bk}", brsname), writes=(kr_,))
        bld.dma("sp", lambda e, kb_=kb_, rb_=rb_, ro=ro: e.dma_start(
            out=rb_[ro + 32:ro + 64, :], in_=kb_[ro:ro + 32, :]),
            reads=(kr_,), writes=(rr_,), sem="rot" + str(rp))
        bld.dma("sp", lambda e, kb_=kb_, rb_=rb_, ro=ro: e.dma_start(
            out=rb_[ro:ro + 32, :], in_=kb_[ro + 32:ro + 64, :]),
            reads=(kr_,), writes=(rr_,), sem="rot" + str(rp))
        bld.op("dve", lambda e, tc=tc, kb_=kb_, ro=ro: e.tensor_mul(
            kb_[ro:ro + 64, :], kb_[ro:ro + 64, :], costab[ro:ro + 64, tc * 512:(tc + 1) * 512]),
            reads=(kr_, "cosk"), writes=(kr_,))
        bld.op("pool", lambda e, tc=tc, rb_=rb_, ro=ro: e.tensor_mul(
            rb_[ro:ro + 64, :], rb_[ro:ro + 64, :], sink[ro:ro + 64, tc * 512:(tc + 1) * 512]),
            reads=(rr_, "sink"), writes=(rr_,))

    def rope_b(h, tc, tgt, tgtres, T, rp, ro=64):
        kb_, rb_ = krs[rp], rotb[rp]
        kr_, rr_ = f"krs{rp}", f"rotb{rp}"
        if tgt is None:
            bld.op("dve", lambda e, kb_=kb_, rb_=rb_: e.tensor_add(
                kb_[64:128, :], kb_[64:128, :], rb_[64:128, :]),
                reads=(kr_, rr_), writes=(kr_,))
            bld.dma("pool", lambda e, h=h, tc=tc, kb_=kb_: e.dma_start(
                out=qTD[h, :, tc * 512:(tc + 1) * 512], in_=kb_[:, :]),
                reads=(kr_,), writes=("qTDr",), sem="qtd" + str(rp))
            bld.dma("pool", lambda e, h=h, tc=tc, kb_=kb_: e.dma_start(
                out=qTDs[h, 0:64, tc * 512:(tc + 1) * 512], in_=kb_[64:128, :]),
                reads=(kr_,), writes=("qTDsr",), sem="qtds" + str(rp))
            bld.dma("pool", lambda e, h=h, tc=tc, kb_=kb_: e.dma_start(
                out=qTDs[h, 64:128, tc * 512:(tc + 1) * 512], in_=kb_[0:64, :]),
                reads=(kr_,), writes=("qTDsr",), sem="qtds" + str(rp))
            return
        bld.op("dve", lambda e, h=h, tc=tc, tgt=tgt, T=T, kb_=kb_, rb_=rb_, ro=ro: e.tensor_add(
            tgt[ro:ro + 64, h * T + tc * 512: h * T + (tc + 1) * 512],
            kb_[ro:ro + 64, :], rb_[ro:ro + 64, :]),
            reads=(kr_, rr_), writes=(tgtres,))

    ldpool(wquh[0][:, :], d["wqu"][0].rearrange("p a b -> p (a b)"), "wquh0")
    ldpool(wrqh[0][:, :], d["wrq"][0].rearrange("p a b -> p (a b)"), "wrqh0")
    gtile = [0]
    for h in range(16):
        sl = h % 2
        if h + 1 < 16:
            ldpool(wrkh[1 - sl][:, :], d["wrk"][h + 1].rearrange("p a b -> p (a b)"), f"wrkh{1 - sl}")
            ldpool(wkuh[1 - sl][:, :], d["wku"][h + 1].rearrange("p a b -> p (a b)"), f"wkuh{1 - sl}")
            ldpool(wquh[1 - sl][:, :], d["wqu"][h + 1].rearrange("p a b -> p (a b)"), f"wquh{1 - sl}")
            ldpool(wrqh[1 - sl][:, :], d["wrq"][h + 1].rearrange("p a b -> p (a b)"), f"wrqh{1 - sl}")
        for pair in range(2):
            tcA, tcB = 2 * pair, 2 * pair + 1
            bkA = gtile[0] % 4
            rpA = gtile[0] % 6
            gtile[0] += 1
            bkB = gtile[0] % 4
            rpB = gtile[0] % 6
            gtile[0] += 1
            for kc in range(4):
                bld.op("pe", lambda e, sl=sl, kc=kc, tcA=tcA, bkA=bkA: e.matmul(
                    bank(bkA)[0:64, :], wkuh[sl][:, kc * 64:(kc + 1) * 64],
                    kv_dT[:, kc * SF + tcA * 512: kc * SF + (tcA + 1) * 512],
                    start=(kc == 0), stop=(kc == 3), tile_position=(0, 0)),
                    reads=(f"wkuh{sl}", "kv_dT"), writes=(f"ps{bkA}",), inc=False)
                bld.op("pe", lambda e, sl=sl, kc=kc, tcB=tcB, bkB=bkB: e.matmul(
                    bank(bkB)[64:128, :], wkuh[sl][:, kc * 64:(kc + 1) * 64],
                    kv_dT[:, kc * SF + tcB * 512: kc * SF + (tcB + 1) * 512],
                    start=(kc == 0), stop=(kc == 3), tile_position=(0, 64)),
                    reads=(f"wkuh{sl}", "kv_dT"), writes=(f"ps{bkB}",), inc=False)
            for kc in range(16):
                bld.op("pe", lambda e, sl=sl, kc=kc, tcA=tcA, bkA=bkA: e.matmul(
                    bank(bkA)[64:128, :], wrkh[sl][:, kc * 64:(kc + 1) * 64],
                    xnT[:, kc * SF + tcA * 512: kc * SF + (tcA + 1) * 512],
                    start=(kc == 0), stop=(kc == 15), tile_position=(0, 64)),
                    reads=(f"wrkh{sl}", "xnT"), writes=(f"ps{bkA}",), inc=(kc == 15))
                bld.op("pe", lambda e, sl=sl, kc=kc, tcB=tcB, bkB=bkB: e.matmul(
                    bank(bkB)[0:64, :], wrkh[sl][:, kc * 64:(kc + 1) * 64],
                    xnT[:, kc * SF + tcB * 512: kc * SF + (tcB + 1) * 512],
                    start=(kc == 0), stop=(kc == 15), tile_position=(0, 0)),
                    reads=(f"wrkh{sl}", "xnT"), writes=(f"ps{bkB}",), inc=(kc == 15))
            bld.op("act", lambda e, h=h, tcA=tcA, bkA=bkA: e.activation(
                out=kT[0:64, h * SF + tcA * 512: h * SF + (tcA + 1) * 512],
                in_=bank(bkA)[0:64, :], func=AF.Identity, bias=bku_sb[0:64, h:h + 1]),
                reads=(f"ps{bkA}", "bkus"), writes=("kT",))
            rope_a(h, tcA, bkA, cosk, brk_sb, "brks", rpA, ro=64)
            if rope_prev is not None:
                rope_b(*rope_prev)
            rope_prev = (h, tcA, kT, "kT", SF, rpA, 64)
            bld.op("act", lambda e, h=h, tcB=tcB, bkB=bkB: e.activation(
                out=kT[64:128, h * SF + tcB * 512: h * SF + (tcB + 1) * 512],
                in_=bank(bkB)[64:128, :], func=AF.Identity, bias=bku_sb[64:128, h:h + 1]),
                reads=(f"ps{bkB}", "bkus"), writes=("kT",))
            rope_a(h, tcB, bkB, cosk, brk_sb, "brks", rpB, ro=0)
            if rope_prev is not None:
                rope_b(*rope_prev)
            rope_prev = (h, tcB, kT, "kT", SF, rpB, 0)
        for tc in range(2):
            bk = gtile[0] % 4
            rp = gtile[0] % 6
            gtile[0] += 1
            for kc in range(4):
                bld.op("pe", lambda e, sl=sl, kc=kc, tc=tc, bk=bk: e.matmul(
                    bank(bk)[0:64, :], wquh[sl][:, kc * 64:(kc + 1) * 64],
                    q_dT[:, kc * TOK + tc * 512: kc * TOK + (tc + 1) * 512],
                    start=(kc == 0), stop=(kc == 3), tile_position=(0, 0)),
                    reads=(f"wquh{sl}", "q_dT"), writes=(f"ps{bk}",), inc=False)
                bld.op("pe", lambda e, sl=sl, kc=kc, tc=tc, bk=bk: e.matmul(
                    bank(bk)[64:128, :], wrqh[sl][:, kc * 64:(kc + 1) * 64],
                    q_dT[:, kc * TOK + tc * 512: kc * TOK + (tc + 1) * 512],
                    start=(kc == 0), stop=(kc == 3), tile_position=(0, 64)),
                    reads=(f"wrqh{sl}", "q_dT"), writes=(f"ps{bk}",), inc=(kc == 3))
            bld.op("act", lambda e, h=h, tc=tc, bk=bk, kb_=krs[rp]: e.activation(
                out=kb_[0:64, :],
                in_=bank(bk)[0:64, :], func=AF.Identity, bias=bqu_sb[0:64, h:h + 1]),
                reads=(f"ps{bk}", "bqus"), writes=(f"krs{rp}",))
            rope_a(h, tc, bk, cosk, brq_sb, "brqs", rp, ro=64)
            if rope_prev is not None:
                rope_b(*rope_prev)
            rope_prev = (h, tc, None, None, TOK, rp, 64)
    mark("P3b")
    if rope_prev is not None:
        rope_b(*rope_prev)
        rope_prev = None

    mark("P4")
    qTh = [sb(f"qTh{i}", [128, TOK], BF16, SEG_B1 + i * 2 * KB) for i in range(2)]
    qThs = [sb(f"qThs{i}", [128, TOK], BF16, SEG_B1 + 4 * KB + i * 2 * KB) for i in range(2)]
    for nm in ("qTh0", "qTh1", "qThs0", "qThs1"):
        bld.retire(["xnT"], nm)
    wvu_sb = sb("wvu", [128, 4 * 2048], BF16, SEG_B1 + 32 * KB)
    v4 = sb("v4", [128, 16 * 512], BF16, SEG_B1 + 48 * KB)
    bld.retire(["xnT"], "wvu")
    bld.retire(["xnT"], "v4")
    probsT = sb("probsT", [128, 16 * 512], BF16, SEG_D)
    probs = [sb(f"probs{i}", [128, SF], BF16, SEG_D + 16 * KB + i * 4 * KB) for i in range(2)]
    bvu_b = sb("bvub", [128, 2048], F32, SEG_D + 24 * KB)
    for rn in ("wkvc0", "wkvc1", "wqdc0", "wqdc1", "q_dT"):
        bld.retire([rn], "probsT")
        bld.retire([rn], "probs0")
        bld.retire([rn], "probs1")
        bld.retire([rn], "bvub")
    strip = [sb(f"strip{i}", [128, 512], BF16, SEG_E + 16 * KB + i * KB) for i in range(2)]
    diag = [sb(f"diag{i}", [128, 128], BF16, SEG_E + 18 * KB + i * 256) for i in range(2)]
    bld.retire(["rotb0"], "diag0")
    bld.retire(["rotb0"], "diag1")
    for rn in ("krs0", "rotb0"):
        bld.retire([rn], "strip0")
        bld.retire([rn], "strip1")
    for rn in [f"krs{i}" for i in range(6)] + [f"rotb{i}" for i in range(6)]:
        for nn in ("probsT", "probs0", "probs1", "bvub"):
            bld.retire([rn], nn)

    ldpool(wvu_sb[:, :], d["wvu"][0].rearrange("p a b -> p (a b)"), "wvu")
    ldsp(bvu_b[:, :], d["bvu"][0:1, :].to_broadcast([128, 2048]), "bvub")

    def emit_v4(hg):
        for kt in range(16):
            for kc in range(4):
                bld.op("pe", lambda e, kc=kc, kt=kt, hg=hg: e.matmul(
                    bank(7), kv_dT[:, kc * SF + kt * 128: kc * SF + (kt + 1) * 128],
                    wvu_sb[:, kc * 2048 + hg * 512: kc * 2048 + (hg + 1) * 512],
                    start=(kc == 0), stop=(kc == 3)),
                    reads=("kv_dT", "wvu"), writes=("ps7",), inc=(kc == 3))
            bld.op("dve", lambda e, kt=kt, hg=hg: e.scalar_tensor_tensor(
                out=v4[:, kt * 512:(kt + 1) * 512], in0=bank(7), scalar=1.0,
                in1=bvu_b[:, hg * 512:(hg + 1) * 512], op0=ALU.mult, op1=ALU.add),
                reads=("ps7", "bvub"), writes=("v4",))

    def emit_scores_pe(h, qi, par):
        base = 4 * par
        for kb in range(4):
            qsrc = qTh if kb % 2 == 0 else qThs
            qres = f"qTh{h % 2}" if kb % 2 == 0 else f"qThs{h % 2}"
            bld.op("pe", lambda e, h=h, qi=qi, kb=kb, base=base, qsrc=qsrc, qres=qres: e.matmul(
                bank(base + kb), qsrc[h % 2][:, qi * 128:(qi + 1) * 128],
                kT[:, h * SF + kb * 512: h * SF + (kb + 1) * 512],
                start=True, stop=True),
                reads=(qres, "kT"), writes=(f"ps{base + kb}",), inc=True)

    def emit_chain(h, qi, par):
        sc = 32 + 16 * par
        base = 4 * par
        pbanks = tuple(f"ps{base + kb}" for kb in range(4))
        bld.op("dve", lambda e, sc=sc, base=base: e.reduce_max(
            out=stat[:, sc + 5:sc + 6],
            in_=psum[:, base * 512:base * 512 + 2048].rearrange("p (a b) -> p a b", b=8)[:, :, 0],
            axis=AX.X, negate=True),
            reads=pbanks, writes=(f"nmc_{par}",))
        bld.op("dve", lambda e, sc=sc: e.tensor_scalar_mul(
            stat[:, sc + 6:sc + 7], stat[:, sc + 5:sc + 6], SCALE),
            reads=(f"nmc_{par}",), writes=(f"nms_{par}",))
        bld.op("act", lambda e, sc=sc, par=par, base=base: e.activation(
            out=probs[par][:, :], in_=psum[:, base * 512:base * 512 + 2048],
            func=AF.Exp, bias=stat[:, sc + 6:sc + 7], scale=SCALE,
            accum_out=stat[:, sc + 12:sc + 13]),
            reads=pbanks + (f"nms_{par}",),
            writes=(f"probs{par}", f"z_{par}"))
    def emit_chain_b(par):
        sc = 32 + 16 * par
        bld.op("dve", lambda e, sc=sc: e.reciprocal(
            out=stat[:, sc + 13:sc + 14], in_=stat[:, sc + 12:sc + 13]),
            reads=(f"z_{par}",), writes=(f"zi_{par}",))
        bld.op("dve", lambda e, par=par, sc=sc: e.tensor_scalar_mul(
            diag[par][:, :], ident[:, :], stat[:, sc + 13:sc + 14]),
            reads=("identb", f"zi_{par}"), writes=(f"diag{par}",))

    def emit_transposes(h, qg, qt, par):
        base = 4 * par
        for g4 in range(4):
            tb = base + g4
            for j in range(4):
                bld.op("pe", lambda e, par=par, g4=g4, j=j, tb=tb: e.matmul(
                    bank(tb)[:, j * 128:(j + 1) * 128],
                    probs[par][:, (g4 * 4 + j) * 128:(g4 * 4 + j + 1) * 128],
                    diag[par][:, :], start=True, stop=True),
                    reads=(f"probs{par}", f"diag{par}"), writes=(f"ps{tb}",), inc=(j == 3))
            if g4 != 3:
                bld.op("dve", lambda e, g4=g4, qt=qt, tb=tb: e.tensor_copy(
                    out=probsT[:, :].rearrange("p (c q) -> p c q", c=16)
                        [:, g4 * 4:g4 * 4 + 4, qt * 128:(qt + 1) * 128],
                    in_=bank(tb)),
                    reads=(f"ps{tb}",), writes=("probsT",))
            else:
                bld.op("act", lambda e, g4=g4, qt=qt, tb=tb: e.copy(
                    out=probsT[:, :].rearrange("p (c q) -> p c q", c=16)
                        [:, g4 * 4:g4 * 4 + 4, qt * 128:(qt + 1) * 128],
                    in_=bank(tb)),
                    reads=(f"ps{tb}",), writes=("probsT",))

    def emit_pv(h, qg):
        hh = h % 4
        for kc in range(16):
            bld.op("pe", lambda e, kc=kc, hh=hh: e.matmul(
                bank(6), v4[:, kc * 512 + hh * 128: kc * 512 + (hh + 1) * 128],
                probsT[:, kc * 512:(kc + 1) * 512],
                start=(kc == 0), stop=(kc == 15)),
                reads=("v4", "probsT"), writes=("ps6",), inc=(kc == 15))
        ssl = qg % 2
        bld.op("act", lambda e, ssl=ssl: e.copy(out=strip[ssl][:, :], in_=bank(6)),
               reads=("ps6",), writes=(f"strip{ssl}",))
        bld.dma("sp", lambda e, h=h, qg=qg, ssl=ssl: e.dma_start(
            out=attnD[qg, :, h, :], in_=strip[ssl][:, :]),
            reads=(f"strip{ssl}",), writes=(f"attnD{ssl}",), sem=f"attnDs{ssl}")

    def load_qTh(h):
        bld.dma("sp", lambda e, h=h: e.dma_start(out=qTh[h % 2][:, :], in_=qTD[h]),
                reads=("qTDr",), writes=(f"qTh{h % 2}",), sem=f"qTh{h % 2}")
        bld.dma("sp", lambda e, h=h: e.dma_start(out=qThs[h % 2][:, :], in_=qTDs[h]),
                reads=("qTDsr",), writes=(f"qThs{h % 2}",), sem=f"qThs{h % 2}")

    load_qTh(0)
    load_qTh(1)
    work = []
    for hg in range(4):
        work.append(("v4", hg))
        for hh in range(4):
            h = hg * 4 + hh
            for qg in range(2):
                for qt in range(4):
                    work.append(("qt", h, qg, qt, hh == 0 and qg == 0 and qt == 0))
    prev = None
    g_i = 0

    def drain_prev():
        nonlocal prev
        if prev is not None:
            ph, pqg, pqt, ppar = prev
            emit_chain_b(ppar)
            emit_transposes(ph, pqg, pqt, ppar)
            if pqt == 3:
                emit_pv(ph, pqg)
            prev = None

    for w in work:
        if w[0] == "v4":
            drain_prev()
            emit_v4(w[1])
            continue
        _, h, qg, qt, _hd = w
        par = g_i % 2
        g_i += 1
        emit_scores_pe(h, qg * 4 + qt, par)
        if qg == 1 and qt == 3 and h + 2 < 16:
            load_qTh(h + 2)
        emit_chain(h, qg * 4 + qt, par)
        keep = prev
        prev = None
        if keep is not None:
            ph, pqg, pqt, ppar = keep
            emit_chain_b(ppar)
            emit_transposes(ph, pqg, pqt, ppar)
            if pqt == 3:
                emit_pv(ph, pqg)
        prev = (h, qg, qt, par)
    drain_prev()

    mark("P5")
    wo_sb = sb("wo", [128, 16 * 4 * 512], BF16, SEG_B1)
    bld.retire(["qTh0", "qTh1", "qThs0", "qThs1", "wvu", "v4"], "wo")
    for eo in range(4):
        bld.retire(["qTh0", "qTh1", "qThs0", "qThs1", "wvu", "v4"], f"wo{eo}")
    tT = sb("tT", [128, 16 * TOK], BF16, SEG_B2)
    bld.retire(["kT"], "tT")
    bo_b = sb("bob", [128, 2048], F32, SEG_D)
    normw_b = sb("normwb", [128, 2048], F32, SEG_D + 16 * KB)
    for rn in ("probsT", "probs0", "probs1", "bvub") + tuple(f"krs{i}" for i in range(6)):
        bld.retire([rn], "bob")
        bld.retire([rn], "normwb")
    x1b = [sb(f"x1b{i}", [128, E], F32, SEG_B2 + 32 * KB + i * 8 * KB) for i in range(2)]
    af32 = [sb(f"af32{i}", [128, E], F32, SEG_B2 + 48 * KB + i * 8 * KB) for i in range(2)]
    for i in range(2):
        bld.retire(["kT"], f"x1b{i}")
        bld.retire(["kT"], f"af32{i}")
    xn1b = sb("xn1b", [128, E], F32, SEG_D + 8 * KB)
    tw = sb("tw", [128, E], F32, SEG_D + 24 * KB)
    for rn in ("probsT", "probs0", "probs1", "bvub") + tuple(f"krs{i}" for i in range(6)):
        for nn in ("xn1b", "tw"):
            bld.retire([rn], nn)
    attw = sb("attw", [128, 16 * 512], BF16, SEG_C)
    bld.retire(["kv_dT"], "attw")
    t32T = sb("t32T", [128, 16 * 128], F32, SEG_E + 18 * KB)
    GT = sb("GT", [16, TOK], BF16, SEG_E)
    GTp = sb("GTp", [16, TOK], F32, SEG_E + 2 * KB)
    grow = sb("grow", [1, TOK], F32, SEG_E + 6 * KB)
    for rn in ("cosk", "sink") + tuple(f"rotb{i}" for i in range(6)):
        for nn in ("GT", "GTp", "grow"):
            bld.retire([rn], nn)
    for rn in ("wrkh0", "wrkh1", "strip0", "strip1", "rotb0", "krs0",
               "bkvs", "bqds", "bkus", "bqus", "brks", "brqs",
               "wkuh0", "wkuh1", "wquh0", "wquh1", "wrqh0", "wrqh1", "diag0", "diag1"):
        bld.retire([rn], "t32T")

    for eo in range(4):
        bld.dma("pool", lambda e, eo=eo: e.dma_start(
            out=wo_sb[:, :].rearrange("p (h e n) -> p h e n", h=16, e=4)[:, :, eo, :],
            in_=d["wo"][:, :, eo, :]),
            writes=(f"wo{eo}",), sem=f"wo{eo}")
    ldsp(bo_b[:, :], d["bo"][0:1, :].to_broadcast([128, 2048]), "bob")
    ldsp(normw_b[:, :], d["normw"][0:1, :].to_broadcast([128, 2048]), "normwb")

    def p5_load(i):
        if i % 4 == 0:
            bld.dma("sp", lambda e, i=i: e.dma_start(
                out=attw[:, :].rearrange("p (h q) -> p h q", h=16),
                in_=attnD[i // 4]),
                reads=("attnD0", "attnD1"), writes=("attw",), sem="attw")
        bld.dma("sp", lambda e, i=i: e.dma_start(out=x1b[i % 2][:, :],
                                                 in_=d["xk"][i * 128:(i + 1) * 128, :]),
                writes=(f"x1b{i % 2}",), sem=f"x1b{i % 2}")

    def p5_wo_pe(i):
        for eo in range(4):
            for h in range(16):
                bld.op("pe", lambda e, i=i, eo=eo, h=h: e.matmul(
                    bank(eo), attw[:, h * 512 + (i % 4) * 128: h * 512 + (i % 4) * 128 + 128],
                    wo_sb[:, (h * 4 + eo) * 512:(h * 4 + eo + 1) * 512],
                    start=(h == 0), stop=(h == 15)),
                    reads=("attw", f"wo{eo}"), writes=(f"ps{eo}",), inc=(h == 15))

    def p5_wo_evac(i):
        for eo in range(4):
            bld.op("dve", lambda e, i=i, eo=eo: e.scalar_tensor_tensor(
                out=af32[i % 2][:, eo * 512:(eo + 1) * 512], in0=bank(eo), scalar=1.0,
                in1=bo_b[:, eo * 512:(eo + 1) * 512], op0=ALU.mult, op1=ALU.add),
                reads=(f"ps{eo}", "bob"), writes=(f"af32{i % 2}",))

    def ln(src_, srcres, dst, dstres, sN):
        for bc in range(4):
            bld.op("dve", lambda e, src_=src_, bc=bc: e.bn_stats(
                out=stat[:, 2 + 6 * bc:8 + 6 * bc], in_=src_[:, bc * 512:(bc + 1) * 512]),
                reads=(srcres,), writes=("bn6",))
        bld.op("dve", lambda e: e.bn_aggr(out=stat[:, 0:2], in_=stat[:, 2:26]),
               reads=("bn6",), writes=("bn2",))
        bld.op("dve", lambda e, sN=sN: e.tensor_scalar_mul(stat[:, sN:sN + 1], stat[:, 0:1], -1.0),
               reads=("bn2",), writes=("nm0",))
        bld.op("act", lambda e, sN=sN: e.activation(out=stat[:, sN + 1:sN + 2], in_=stat[:, 1:2],
                                                    func=AF.Ln, bias=epsb[:, 0:1]),
               reads=("bn2", "epsb"), writes=("rs0",))
        bld.op("act", lambda e, sN=sN: e.activation(out=stat[:, sN + 1:sN + 2],
                                                    in_=stat[:, sN + 1:sN + 2], func=AF.Exp, scale=-0.5),
               reads=("rs0",), writes=("rs0",))
        bld.op("dve", lambda e, src_=src_, dst=dst, sN=sN: e.tensor_scalar(
            out=dst[:, :], in0=src_[:, :], scalar1=stat[:, sN:sN + 1],
            scalar2=stat[:, sN + 1:sN + 2], op0=ALU.add, op1=ALU.mult),
            reads=(srcres, "nm0", "rs0"), writes=(dstres,))

    def p5_chain(i):
        ip = i % 2
        ln(af32[ip], f"af32{ip}", tw, "tw", 26)
        bld.op("dve", lambda e, ip=ip: e.tensor_add(x1b[ip][:, :], x1b[ip][:, :], tw[:, :]),
               reads=(f"x1b{ip}", "tw"), writes=(f"x1b{ip}",))
        bld.dma("sp", lambda e, i=i, ip=ip: e.dma_start(out=x1D[i * 128:(i + 1) * 128, :], in_=x1b[ip][:, :]),
                reads=(f"x1b{ip}",), writes=("x1D",), sem="x1Ds")
        ln(x1b[ip], f"x1b{ip}", xn1b, "xn1b", 26)
        bld.dma("sp", lambda e, i=i: e.dma_start(out=xn1D[i * 128:(i + 1) * 128, :], in_=xn1b[:, :]),
                reads=("xn1b",), writes=("xn1D",), sem="xn1Ds")
        for bc in range(4):
            bld.op("dve", lambda e, bc=bc: e.bn_stats(
                out=stat[:, 2 + 6 * bc:8 + 6 * bc], in_=xn1b[:, bc * 512:(bc + 1) * 512]),
                reads=("xn1b",), writes=("bn6",))
        bld.op("dve", lambda e: e.bn_aggr(out=stat[:, 0:2], in_=stat[:, 2:26]),
               reads=("bn6",), writes=("bn2",))
        bld.op("dve", lambda e: e.tensor_mul(stat[:, 30:31], stat[:, 0:1], stat[:, 0:1]),
               reads=("bn2",), writes=("msq",))
        bld.op("dve", lambda e: e.tensor_add(stat[:, 30:31], stat[:, 30:31], stat[:, 1:2]),
               reads=("msq", "bn2"), writes=("msq",))
        bld.op("act", lambda e: e.activation(out=stat[:, 31:32], in_=stat[:, 30:31], func=AF.Ln, bias=epsb[:, 0:1]),
               reads=("msq", "epsb"), writes=("rms",))
        bld.op("act", lambda e: e.activation(out=stat[:, 31:32], in_=stat[:, 31:32], func=AF.Exp, scale=-0.5),
               reads=("rms",), writes=("rms",))
        bld.op("dve", lambda e: e.tensor_scalar_mul(tw[:, :], xn1b[:, :], stat[:, 31:32]),
               reads=("xn1b", "rms"), writes=("tw",))
        bld.op("dve", lambda e: e.tensor_mul(tw[:, :], tw[:, :], normw_b[:, :]),
               reads=("tw", "normwb"), writes=("tw",))
        for g in range(4):
            tb = 4 + g % 2
            for j in range(4):
                bld.op("pe", lambda e, g=g, j=j, tb=tb: e.transpose(
                    bank(tb)[:, j * 128:(j + 1) * 128], tw[:, (g * 4 + j) * 128:(g * 4 + j + 1) * 128],
                    ident32[:, :]),
                    reads=("tw", "ident32"), writes=(f"ps{tb}",), inc=(j == 3))
            bld.op("dve", lambda e, g=g, tb=tb: e.tensor_copy(
                out=t32T[:, :].rearrange("p (c t) -> p c t", c=16)[:, g * 4:g * 4 + 4, 0:128],
                in_=bank(tb)),
                reads=(f"ps{tb}",), writes=("t32T",))
        bld.op("act", lambda e, i=i: e.copy(
            out=tT[:, :].rearrange("p (c t) -> p c t", c=16)[:, :, i * 128:(i + 1) * 128],
            in_=t32T[:, :].rearrange("p (c t) -> p c t", c=16)[:, :, :]),
            reads=("t32T",), writes=("tT",))
        for kc in range(16):
            bld.op("pe", lambda e, kc=kc: e.matmul(
                bank(6)[:, 0:16], t32T[:, kc * 128:(kc + 1) * 128],
                gatew_sb[:, kc * 16:(kc + 1) * 16], start=(kc == 0), stop=(kc == 15)),
                reads=("t32T", "gatew"), writes=("ps6",), inc=(kc == 15))
        bld.op("dve", lambda e: e.tensor_add(stat[:, 64:80], bank(6)[:, 0:16], gateb_b[:, :]),
               reads=("ps6", "gatebb"), writes=("lg",))
        bld.op("dve", lambda e: e.max(out=stat[:, 80:88], in_=stat[:, 64:80]),
               reads=("lg",), writes=("v8",))
        bld.op("dve", lambda e: e.reduce_max(out=stat[:, 88:89], in_=stat[:, 64:80], axis=AX.X, negate=True),
               reads=("lg",), writes=("nmg",))
        bld.op("dve", lambda e: e.tensor_scalar(out=stat[:, 96:112], in0=stat[:, 64:80],
                                                scalar1=stat[:, 83:84], scalar2=None, op0=ALU.is_ge),
               reads=("lg", "v8"), writes=("mask",))
        bld.op("act", lambda e: e.activation(out=stat[:, 112:128], in_=stat[:, 64:80],
                                             func=AF.Exp, bias=stat[:, 88:89]),
               reads=("lg", "nmg"), writes=("eg",))
        bld.op("dve", lambda e: e.tensor_mul(stat[:, 112:128], stat[:, 112:128], stat[:, 96:112]),
               reads=("eg", "mask"), writes=("eg",))
        bld.op("dve", lambda e: e.reduce_sum(out=stat[:, 89:90], in_=stat[:, 112:128], axis=AX.X),
               reads=("eg",), writes=("zg",))
        bld.op("dve", lambda e: e.reciprocal(out=stat[:, 90:91], in_=stat[:, 89:90]),
               reads=("zg",), writes=("zgi",))
        bld.op("dve", lambda e: e.tensor_scalar_mul(stat[:, 128:144], stat[:, 112:128], stat[:, 90:91]),
               reads=("eg", "zgi"), writes=("G32",))
        bld.op("dve", lambda e: e.tensor_scalar_mul(stat[:, 144:160], stat[:, 128:144], 1.0 / ALPHA),
               reads=("G32",), writes=("Gp32",))
        bld.op("dve", lambda e: e.tensor_copy(out=stat[:, 160:176].bitcast(BF16)[:, 0:16],
                                              in_=stat[:, 128:144]),
               reads=("G32",), writes=("Gbf",))
        bld.op("pe", lambda e: e.transpose(bank_bf(7)[0:16, 0:128], stat[:, 160:176].bitcast(BF16)[:, 0:16],
                                           ident[:, :]),
               reads=("Gbf", "identb"), writes=("ps7",), inc=True)
        bld.op("act", lambda e, i=i: e.copy(out=GT[0:16, i * 128:(i + 1) * 128],
                                            in_=bank_bf(7)[0:16, 0:128]),
               reads=("ps7",), writes=("GTr",))
        bld.op("pe", lambda e: e.transpose(bank(7)[0:16, 128:256], stat[:, 144:160], ident32[:, :]),
               reads=("Gp32", "ident32"), writes=("ps7",), inc=True)
        bld.op("act", lambda e, i=i: e.copy(out=GTp[0:16, i * 128:(i + 1) * 128],
                                            in_=bank(7)[0:16, 128:256]),
               reads=("ps7",), writes=("GTpr",))

    for i in range(8):
        p5_load(i)
        p5_wo_pe(i)
        if i > 0:
            p5_chain(i - 1)
        p5_wo_evac(i)
    p5_chain(7)

    mark("P6")
    delta = sb("delta", [128, 16 * TOK], F32, SEG_B1)
    bld.retire(["wo"], "delta")
    act_e = sb("act_e", [128, 16 * TOK], BF16, SEG_B2 + 32 * KB)
    bld.retire(["kT"], "act_e")
    for i in range(2):
        bld.retire([f"x1b{i}"], "act_e")
        bld.retire([f"af32{i}"], "act_e")
    w2sl = [sb(f"w2sl{i}", [128, 16 * 256], BF16, SEG_C + i * 8 * KB) for i in range(2)]
    bld.retire(["bob", "normwb"], "w2sl0")
    bld.retire(["bob", "normwb"], "w2sl1")
    w1sl = [sb(f"w1sl{i}", [128, 16 * 256], BF16, SEG_D + i * 8 * KB) for i in range(4)]
    for nn in ("xn1b", "tw"):
        for i in range(4):
            bld.retire([nn], f"w1sl{i}")
    Gbp = [sb(f"Gbp{i}", [128, TOK], F32, SEG_E + 10 * KB + i * 4 * KB) for i in range(2)]
    bld.retire(["wrkh0", "wrkh1"], "Gbp0")
    bld.retire(["wrkh0", "wrkh1"], "Gbp1")
    b2sb = sb("b2sb", [16, 2048], BF16, SEG_E + 18 * KB)
    bld.retire(["t32T"], "b2sb")
    bld.retire(["rotb0"], "b2sb")
    bld.retire(["diag0"], "b2sb")
    bld.retire(["diag1"], "b2sb")
    afs = sb("afs", [128, 512], F32, SEG_E + 18 * KB)
    for oldr in ("rotb0", "t32T"):
        for nn in ("afs", "ufs", "sfs"):
            bld.retire([oldr], nn)
    for oldr in ("wrkh0", "wrkh1", "t32T"):
        bld.retire([oldr], "b1sb0")
        bld.retire([oldr], "b1sb1")
    ufs = sb("ufs", [128, 512], F32, SEG_E + 20 * KB)
    sfs = sb("sfs", [128, 512], F32, SEG_E + 22 * KB)
    b1sb = [sb(f"b1sb{i}", [128, 32], F32, SEG_E + 24 * KB + i * 128) for i in range(2)]

    ldpool(b2sb[:, :], d["b2"][:, :], "b2sb")
    for dc in range(16):
        for tc in range(2):
            bk = 4 + (dc * 2 + tc) % 2
            bld.op("pe", lambda e, dc=dc, tc=tc, bk=bk: e.matmul(
                bank(bk), b2sb[0:16, dc * 128:(dc + 1) * 128],
                GT[0:16, tc * 512:(tc + 1) * 512], start=True, stop=True),
                reads=("b2sb", "GTr"), writes=(f"ps{bk}",), inc=True)
            bld.op("dve", lambda e, dc=dc, tc=tc, bk=bk: e.tensor_copy(
                out=delta[:, dc * TOK + tc * 512: dc * TOK + (tc + 1) * 512], in_=bank(bk)),
                reads=(f"ps{bk}",), writes=("delta",))
    bld.retire(["b2sb"], "afs")
    bld.retire(["b2sb"], "ufs")

    def w1_load(e_, si):
        slab = (si // 2) if si % 2 == 0 else (si // 2 + 8)
        slot = si % 4
        ldpool(w1sl[slot][:, :], d["w1s"][e_, slab].rearrange("p a b -> p (a b)"), f"w1sl{slot}")

    def w2_load(e_, js):
        ldpool(w2sl[js % 2][:, :], d["w2s"][e_, js].rearrange("p a b -> p (a b)"), f"w2sl{js % 2}")

    for si in range(4):
        w1_load(0, si)
    for js in range(2):
        w2_load(0, js)
    ldsp(b1sb[0][:, :], d["b1t"][0], "b1sb0")
    bld.dma("sp", lambda e: e.dma_start(out=grow[:, :], in_=GTp[0:1, :]),
            reads=("GTpr",), writes=("grow",), sem="grow")

    for ex in range(16):
        exn = ex % 2
        for tc in range(2):
            bk = 6 + tc
            bld.op("pe", lambda e, tc=tc, bk=bk: e.matmul(
                bank(bk), ones32[0:1, :], grow[0:1, tc * 512:(tc + 1) * 512],
                start=True, stop=True),
                reads=("ones32", "grow"), writes=(f"ps{bk}",), inc=True)
            bld.op("dve", lambda e, tc=tc, bk=bk, exn=exn: e.tensor_copy(
                out=Gbp[exn][:, tc * 512:(tc + 1) * 512], in_=bank(bk)),
                reads=(f"ps{bk}",), writes=(f"Gbp{exn}",))
        if ex + 1 < 16:
            bld.dma("sp", lambda e, ex=ex: e.dma_start(out=grow[:, :], in_=GTp[ex + 1:ex + 2, :]),
                    reads=("GTpr",), writes=("grow",), sem="grow")
            ldsp(b1sb[1 - exn][:, :], d["b1t"][ex + 1], f"b1sb{1 - exn}")

        for j in range(8):
            slA = (2 * j) % 4
            slB = (2 * j + 1) % 4
            for part in range(2):
                for tc in range(2):
                    u = part * 2 + tc
                    bkA = u % 2
                    bkB = 2 + u % 2
                    ic = j * 2 + part
                    for kc in range(16):
                        bld.op("pe", lambda e, slA=slA, kc=kc, part=part, tc=tc, bkA=bkA: e.matmul(
                            bank(bkA), w1sl[slA][:, kc * 256 + part * 128: kc * 256 + part * 128 + 128],
                            tT[:, kc * TOK + tc * 512: kc * TOK + (tc + 1) * 512],
                            start=(kc == 0), stop=(kc == 15)),
                            reads=(f"w1sl{slA}", "tT"), writes=(f"ps{bkA}",), inc=(kc == 15))
                    for kc in range(16):
                        bld.op("pe", lambda e, slB=slB, kc=kc, part=part, tc=tc, bkB=bkB: e.matmul(
                            bank(bkB), w1sl[slB][:, kc * 256 + part * 128: kc * 256 + part * 128 + 128],
                            tT[:, kc * TOK + tc * 512: kc * TOK + (tc + 1) * 512],
                            start=(kc == 0), stop=(kc == 15)),
                            reads=(f"w1sl{slB}", "tT"), writes=(f"ps{bkB}",), inc=(kc == 15))
                    bld.op("dve", lambda e, bkA=bkA, exn=exn, j=j, part=part: e.tensor_scalar(
                        out=afs[:, :], in0=bank(bkA), scalar1=b1sb[exn][:, 2 * j + part: 2 * j + part + 1],
                        scalar2=LIMIT, op0=ALU.add, op1=ALU.min),
                        reads=(f"ps{bkA}", f"b1sb{exn}"), writes=("afs",))
                    bld.op("dve", lambda e, bkB=bkB, exn=exn, j=j, part=part: e.tensor_scalar(
                        out=ufs[:, :], in0=bank(bkB),
                        scalar1=b1sb[exn][:, 16 + 2 * j + part: 16 + 2 * j + part + 1],
                        scalar2=-LIMIT, op0=ALU.add, op1=ALU.max),
                        reads=(f"ps{bkB}", f"b1sb{exn}"), writes=("ufs",))
                    bld.op("dve", lambda e: e.tensor_scalar(
                        out=ufs[:, :], in0=ufs[:, :], scalar1=LIMIT, scalar2=1.0,
                        op0=ALU.min, op1=ALU.add),
                        reads=("ufs",), writes=("ufs",))
                    bld.op("act", lambda e: e.activation(out=sfs[:, :], in_=afs[:, :],
                                                         func=AF.Silu, scale=ALPHA),
                           reads=("afs",), writes=("sfs",))
                    bld.op("dve", lambda e: e.tensor_mul(sfs[:, :], sfs[:, :], ufs[:, :]),
                           reads=("sfs", "ufs"), writes=("sfs",))
                    bld.op("dve", lambda e, ic=ic, tc=tc, exn=exn: e.tensor_mul(
                        act_e[:, ic * TOK + tc * 512: ic * TOK + (tc + 1) * 512],
                        sfs[:, :], Gbp[exn][:, tc * 512:(tc + 1) * 512]),
                        reads=("sfs", f"Gbp{exn}"), writes=("act_e",))
            for s_ in (2 * j + 4, 2 * j + 5):
                if s_ < 16:
                    w1_load(ex, s_)
                elif ex + 1 < 16:
                    w1_load(ex + 1, s_ - 16)
        for js in range(8):
            for part in range(2):
                for tc in range(2):
                    u = part * 2 + tc
                    bk = 4 + u % 2
                    dc = js * 2 + part
                    for kc in range(16):
                        bld.op("pe", lambda e, js=js, kc=kc, part=part, tc=tc, bk=bk: e.matmul(
                            bank(bk), w2sl[js % 2][:, kc * 256 + part * 128: kc * 256 + part * 128 + 128],
                            act_e[:, kc * TOK + tc * 512: kc * TOK + (tc + 1) * 512],
                            start=(kc == 0), stop=(kc == 15)),
                            reads=(f"w2sl{js % 2}", "act_e"), writes=(f"ps{bk}",), inc=(kc == 15))
                    bld.op("dve", lambda e, dc=dc, tc=tc, bk=bk: e.tensor_tensor(
                        out=delta[:, dc * TOK + tc * 512: dc * TOK + (tc + 1) * 512],
                        in0=bank(bk),
                        in1=delta[:, dc * TOK + tc * 512: dc * TOK + (tc + 1) * 512],
                        op=ALU.add),
                        reads=(f"ps{bk}", "delta"), writes=("delta",))
            s_ = js + 2
            if s_ < 8:
                w2_load(ex, s_)
            elif ex + 1 < 16:
                w2_load(ex + 1, s_ - 8)

    mark("P7")
    x1r = [sb(f"x1r{i}", [128, E], F32, SEG_D + i * 8 * KB) for i in range(2)]
    xn1r = [sb(f"xn1r{i}", [128, E], F32, SEG_D + 16 * KB + i * 8 * KB) for i in range(2)]
    ffn = [sb(f"ffn{i}", [128, E], F32, SEG_B2 + i * 8 * KB) for i in range(2)]
    for i in range(2):
        for rn in ("bob", "normwb", "tw", "xn1b", "w1sl0", "w1sl1", "w1sl2", "w1sl3"):
            bld.retire([rn], f"x1r{i}")
            bld.retire([rn], f"xn1r{i}")
        bld.retire(["tT"], f"ffn{i}")

    def p7_load(i):
        ip = i % 2
        bld.dma("sp", lambda e, i=i, ip=ip: e.dma_start(out=x1r[ip][:, :], in_=x1D[i * 128:(i + 1) * 128, :]),
                reads=("x1D",), writes=(f"x1r{ip}",), sem=f"x1r{ip}")
        bld.dma("sp", lambda e, i=i, ip=ip: e.dma_start(out=xn1r[ip][:, :], in_=xn1D[i * 128:(i + 1) * 128, :]),
                reads=("xn1D",), writes=(f"xn1r{ip}",), sem=f"xn1r{ip}")

    def p7_trans(i):
        ip = i % 2
        for g in range(4):
            bk = 4 * ip + g
            for j in range(4):
                dc = g * 4 + j
                bld.op("pe", lambda e, dc=dc, i=i, bk=bk, j=j: e.transpose(
                    bank(bk)[:, j * 128:(j + 1) * 128],
                    delta[:, dc * TOK + i * 128: dc * TOK + (i + 1) * 128], ident32[:, :]),
                    reads=("delta", "ident32"), writes=(f"ps{bk}",), inc=(j == 3))

    def p7_chain(i):
        ip = i % 2
        for g in range(4):
            bk = 4 * ip + g
            bld.op("dve", lambda e, g=g, bk=bk, ip=ip: e.tensor_add(
                ffn[ip][:, g * 512:(g + 1) * 512], bank(bk), xn1r[ip][:, g * 512:(g + 1) * 512]),
                reads=(f"ps{bk}", f"xn1r{ip}"), writes=(f"ffn{ip}",))
        for bc in range(4):
            bld.op("dve", lambda e, bc=bc, ip=ip: e.bn_stats(
                out=stat[:, 2 + 6 * bc:8 + 6 * bc], in_=ffn[ip][:, bc * 512:(bc + 1) * 512]),
                reads=(f"ffn{ip}",), writes=("bn6",))
        bld.op("dve", lambda e: e.bn_aggr(out=stat[:, 0:2], in_=stat[:, 2:26]),
               reads=("bn6",), writes=("bn2",))
        bld.op("dve", lambda e: e.tensor_scalar_mul(stat[:, 26:27], stat[:, 0:1], -1.0),
               reads=("bn2",), writes=("nm0",))
        bld.op("act", lambda e: e.activation(out=stat[:, 27:28], in_=stat[:, 1:2], func=AF.Ln, bias=epsb[:, 0:1]),
               reads=("bn2", "epsb"), writes=("rs0",))
        bld.op("act", lambda e: e.activation(out=stat[:, 27:28], in_=stat[:, 27:28], func=AF.Exp, scale=-0.5),
               reads=("rs0",), writes=("rs0",))
        bld.op("dve", lambda e, ip=ip: e.tensor_scalar(
            out=ffn[ip][:, :], in0=ffn[ip][:, :], scalar1=stat[:, 26:27], scalar2=stat[:, 27:28],
            op0=ALU.add, op1=ALU.mult),
            reads=(f"ffn{ip}", "nm0", "rs0"), writes=(f"ffn{ip}",))
        bld.op("dve", lambda e, ip=ip: e.tensor_add(x1r[ip][:, :], x1r[ip][:, :], ffn[ip][:, :]),
               reads=(f"x1r{ip}", f"ffn{ip}"), writes=(f"x1r{ip}",))
        bld.dma("sp", lambda e, i=i, ip=ip: e.dma_start(out=out[i * 128:(i + 1) * 128, :], in_=x1r[ip][:, :]),
                reads=(f"x1r{ip}",), writes=("outD",), sem="outD")

    for i in range(8):
        p7_load(i)
        p7_trans(i)
        if i > 0:
            p7_chain(i - 1)
    p7_chain(7)
    bld.barrier("sp", ["outD"])

    mark("END")
    blk = nc.Block().__enter__()
    bld.finalize(blk)
    blk.__exit__(None, None, None)
    return nc


def kernel(**inputs):
    per_core = host_prep(inputs)
    nc = build_program()
    res = run_bass_kernel_spmd(nc, per_core, core_ids=list(range(8)))
    outp = np.empty((B, S, E), np.float32)
    for c in range(8):
        b, hf = c // 2, c % 2
        outp[b, hf * 1024:(hf + 1) * 1024] = res.results[c]["out"]
    return outp
